# version 12
# baseline (speedup 1.0000x reference)
import numpy as np
import ml_dtypes
import concourse.bass as bass
import concourse.mybir as mybir
from concourse.bass_utils import run_bass_kernel_spmd
from contextlib import ExitStack

F32 = mybir.dt.float32
BF16 = mybir.dt.bfloat16
I32 = mybir.dt.int32
AF = mybir.ActivationFunctionType
ALU = mybir.AluOpType
BF = ml_dtypes.bfloat16

NCORES = 8
H = 2048
HB = 16
T = 1027
TT = 1024
OC = 67
PROJ = 8576
NH = 64
PD = 64
IC = 43
INTER = 5504
DC = 16

DEBUG = []
PHASE_LIMIT = None
DEV_SYNC = False
SKIP_CC = False
SKIP_BUILD = False


class Sem:
    def __init__(self, handle, dma=False):
        self.h = handle
        self.n = 0
        self.dma = dma

    def bump(self, k):
        self.n += k
        return self.n


class _MMShim:

    def __init__(self, flow):
        self.flow = flow

    def then_inc(self, sem, k):
        self.flow.e.wait_ge(self.flow.s, self.flow.n)
        return self.flow.e.nop().then_inc(sem, k)


class PEFlow:

    LAG = 1

    def __init__(self, e, sem_handle):
        self.e = e
        self.s = sem_handle
        self.n = 0

    def _flow(self, ins):
        ins.then_inc(self.s, 1)
        self.n += 1
        if self.n > self.LAG:
            self.e.wait_ge(self.s, self.n - self.LAG)
        return _MMShim(self)

    def matmul(self, *a, **k):
        return self._flow(self.e.matmul(*a, **k))

    def transpose(self, *a, **k):
        return self._flow(self.e.transpose(*a, **k))

    def __getattr__(self, name):
        return getattr(self.e, name)


class Prog:
    ENGS = ("sync", "tensor", "vector", "scalar", "gpsimd")

    def __init__(self, nc):
        self.nc = nc
        self.ev = {e: [] for e in self.ENGS}
        self.used = set()

    def do(self, eng, fn):
        self.used.add(eng)
        self.ev[eng].append(fn)

    def wait(self, eng, sem, val):
        if val and val > 0:
            if sem.dma:
                val = sem.n
            self.do(eng, lambda e, s=sem.h, v=val: e.wait_ge(s, v))

    def dma(self, eng, out, in_, sem, **kw):
        tgt = sem.bump(16)
        self.do(eng, lambda e, o=out, i=in_, s=sem.h, kw=kw: e.dma_start(out=o, in_=i, **kw).then_inc(s, 16))
        return tgt

    def op(self, eng, fn, sem=None):
        if sem is None:
            self.do(eng, fn)
            return None
        tgt = sem.bump(1)
        self.do(eng, lambda e, f=fn, s=sem.h: f(e).then_inc(s, 1))
        return tgt


def build_program():
    nc = bass.Bass("TRN2", target_bir_lowering=False, debug=False, detect_race_conditions=False)
    p = Prog(nc)
    dram = {}

    def din(name, shape, dt):
        dram[name] = nc.dram_tensor(name, list(shape), dt, kind="ExternalInput")
        return dram[name]

    def dint(name, shape, dt, **kw):
        dram[name] = nc.dram_tensor(name, list(shape), dt, kind="Internal", **kw)
        return dram[name]

    x_in = din("x_core", (H, T), F32)
    w_inT = din("w_inT", (H, PROJ), F32)
    w_outT = din("w_outT", (4096, H), F32)
    convw = din("convw", (128, 34 * 4), F32)
    convb = din("convb", (128, 34), F32)
    dtb = din("dt_bias", (NH, 1), F32)
    alog = din("A_log", (NH, 1), F32)
    dvec = din("D", (NH, 1), F32)
    nw = din("norm_w", (128, 32), F32)
    lut = din("lut", (16,), F32)
    gidx = din("gidxT", (256, INTER), I32)
    uidx = din("uidxT", (256, INTER), I32)
    didx = din("didxT", (688, H), I32)
    sel = din("sel", (1, 8), F32)

    out_t = nc.dram_tensor("out_core", [H, TT], F32, kind="ExternalOutput")

    w_in_bf = dint("w_in_bf", (H, PROJ), BF16)
    w_out_bf = dint("w_out_bf", (4096, H), BF16)
    z_dram = dint("z_dram", (32, 128, TT), BF16)
    x4_dram = dint("x4_dram", (32, 128, TT), BF16)
    yf_dram = dint("yf_dram", (32, 128, TT), BF16)
    acs_dram = dint("acs_dram", (NH, TT), F32)
    expa_dram = dint("expa_dram", (NH, TT), BF16)
    aend_dram = dint("aend_dram", (1, 4 * NH), F32)
    cc_in_g = dint("cc_in_g", (256, INTER), BF16)
    cc_in_u = dint("cc_in_u", (256, INTER), BF16)
    cc_in_d = dint("cc_in_d", (688, H), BF16)
    wg_dram = dint("wg_dram", (H, INTER), BF16)
    wu_dram = dint("wu_dram", (H, INTER), BF16)
    wd_dram = dint("wd_dram", (INTER, H), BF16)
    cc_in_s = dint("cc_in_s", (128, NH * PD), BF16)
    cc_out_s = dint("cc_out_s", (128 * NCORES, NH * PD), BF16)

    ident_bf = nc.inline_tensor(np.eye(128, dtype=BF), name="ident_bf")
    ident_f = nc.inline_tensor(np.eye(128, dtype=np.float32), name="ident_f")
    ones_sq_bf = nc.inline_tensor(np.ones((128, 128), BF), name="ones_sq_bf")
    _m0 = (np.arange(256)[None, :] >= np.arange(128)[:, None]).astype(BF)
    mask0 = nc.inline_tensor(_m0, name="mask0")
    eps_c = nc.inline_tensor(np.full((128, 1), 1e-5, np.float32), name="eps_c")

    dbg = {}
    if "wg_dram" in DEBUG:
        dbg["wd_dram"] = nc.dram_tensor("dbg_wd_dram", [INTER, H], BF16, kind="ExternalOutput")

    def dbg_out(name, shape, dt):
        if name in DEBUG:
            dbg[name] = nc.dram_tensor("dbg_" + name, list(shape), dt, kind="ExternalOutput")
            return dbg[name]
        return None

    ctx = ExitStack()
    with ctx:
        def sem(name, dma=False):
            return Sem(ctx.enter_context(nc.semaphore(name)), dma=dma)

        LD = sem("LD", dma=True)
        LDS = sem("LDS", dma=True)
        STS = sem("STS", dma=True)
        PEQ = sem("PEQ")
        PE = sem("PE")
        VE = sem("VE")
        AC = sem("AC")
        GP = sem("GP", dma=True)
        CC = sem("CC", dma=True)
        ST = sem("ST", dma=True)
        STB = sem("STB", dma=True)

        def sbuf_in(stack, name, shape, dt, align=32):
            return stack.enter_context(nc.sbuf_tensor(name, list(shape), dt, align_bytes=align))

        def sbuf(name, shape, dt):
            return sbuf_in(ctx, name, shape, dt)

        snaps = []

        def barrier():
            if DEV_SYNC:
                p.wait("gpsimd", CC, CC.n)
                p.wait("gpsimd", GP, GP.n)
            mpe = p.op("tensor", lambda e: e.nop(), PE)
            mve = p.op("vector", lambda e: e.nop(), VE)
            mac = p.op("scalar", lambda e: e.nop(), AC)
            for eng in ("sync", "tensor", "vector", "scalar", "gpsimd"):
                p.wait(eng, PE, mpe)
                p.wait(eng, VE, mve)
                p.wait(eng, AC, mac)
                p.wait(eng, LD, LD.n)
                p.wait(eng, ST, ST.n)
                p.wait(eng, LDS, LDS.n)
                p.wait(eng, STS, STS.n)
                p.wait(eng, STB, STB.n)
                if DEV_SYNC:
                    p.wait(eng, CC, CC.n)
            snaps.append({e: len(p.ev[e]) for e in Prog.ENGS})

        xn = sbuf("xn", [128, HB * T], BF16)
        b_f = sbuf("b_f", [128, TT], BF16)
        c_f = sbuf("c_f", [128, TT], BF16)
        dt_t = sbuf("dt_t", [128, 8 * NH], F32)
        acs_t = sbuf("acs_t", [128, 8 * NH], F32)
        decay_t = sbuf("decay_t", [128, 8 * NH], F32)
        wbc = sbuf("wbc", [128, 4 * NH], F32)
        dbc_f = sbuf("dbc_f", [128, 32], F32)
        selbc = sbuf("selbc", [128, 8], F32)
        nw_sb = sbuf("nw_sb", [128, 32], F32)
        lut_sb = sbuf("lut_sb", [128, 16], F32)
        convw_sb = sbuf("convw_sb", [128, 34 * 4], F32)
        convb_sb = sbuf("convb_sb", [128, 34], F32)
        dtb_sb = sbuf("dtb_sb", [NH, 1], F32)
        alog_sb = sbuf("alog_sb", [NH, 1], F32)
        nexpal_sb = sbuf("nexpal_sb", [NH, 1], F32)
        d_sb = sbuf("d_sb", [NH, 1], F32)
        idb_sb = sbuf("idb_sb", [128, 128], BF16)
        idf_sb = sbuf("idf_sb", [128, 128], F32)
        ones_sb = sbuf("ones_sb", [128, 128], BF16)
        mask0_sb = sbuf("mask0_sb", [128, 256], BF16)
        eps_sb = sbuf("eps_sb", [128, 1], F32)
        ss_sb = sbuf("ss_sb", [128, TT], F32)
        flsh_b = sbuf("flsh_b", [128, 256], BF16)
        flsh_f = sbuf("flsh_f", [128, 64], F32)

        for dst, src in [
            (idb_sb, ident_bf), (idf_sb, ident_f), (ones_sb, ones_sq_bf), (mask0_sb, mask0),
        ]:
            p.dma("sync", dst[:], src.ap(), LD)
        p.dma("sync", convw_sb[:], convw.ap(), LD)
        p.dma("sync", convb_sb[:], convb.ap(), LD)
        p.dma("sync", dtb_sb[:], dtb.ap(), LD)
        p.dma("sync", alog_sb[:], alog.ap(), LD)
        p.dma("sync", d_sb[:], dvec.ap(), LD)
        p.dma("sync", nw_sb[:], nw.ap(), LD)
        p.dma("sync", lut_sb[:], lut.ap()[None, :].broadcast_to([128, 16]), LD)
        p.dma("sync", selbc[:], sel.ap()[0:1, :].broadcast_to([128, 8]), LD)
        for ph in range(2):
            p.dma("sync", dbc_f[ph * 64:(ph + 1) * 64, :],
                  dvec.ap()[ph::2, 0][None, :].broadcast_to([64, 32]), LD, allow_slow_non_contiguous=True)
        ld_small = p.dma("sync", eps_sb[:], eps_c.ap(), LD)


        barrier()
        a_ln = ExitStack()
        x_f = sbuf_in(a_ln, "x_f", [128, HB * T], F32)
        x_bf = sbuf_in(a_ln, "x_bf", [128, 2 * T], BF16)
        x2b = sbuf_in(a_ln, "x2b", [128, 2 * T], BF16)
        rstd_ln = sbuf_in(a_ln, "rstd_ln", [128, T], F32)
        mr_ln = sbuf_in(a_ln, "mr_ln", [128, T], F32)
        vtmp = sbuf_in(a_ln, "vtmp", [128, T], F32)
        xtmp = sbuf_in(a_ln, "xtmp", [128, T], F32)
        ldx = []
        for kk in range(HB):
            ldx.append(p.dma("sync", x_f[:, kk * T:(kk + 1) * T], x_in.ap()[kk * 128:(kk + 1) * 128, :], LD))
        p.wait("gpsimd", LD, ldx[-1])
        gp_win_marks = []
        for j in range(8):
            c0, c1 = j * 9 * 128, min((j + 1) * 9 * 128, PROJ)
            p.dma("gpsimd", w_in_bf.ap()[:, c0:c1], w_inT.ap()[:, c0:c1], GP)
            gp_win_marks.append(GP.n)
        slices = [(0, 512), (512, 1024), (1024, 1027)]
        ve_xn = None
        with nc.psum_tensor("ps_sum", [128, 1536], F32) as ps_sum, \
             nc.psum_tensor("ps_sq", [128, 1536], F32) as ps_sq:
            pe_kk = {}
            ac_kk = {}
            ve_kk = {}
            for kk in range(HB):
                par = kk % 2
                p.wait("vector", LD, ldx[kk])
                if kk >= 2:
                    p.wait("vector", PE, pe_kk[kk - 2])
                ve_kk[kk] = p.op("vector", lambda e, kk=kk, par=par: e.tensor_copy(
                    x_bf[:, par * T:(par + 1) * T], x_f[:, kk * T:(kk + 1) * T]), VE)
                p.wait("scalar", VE, ve_kk[kk])
                ac_kk[kk] = p.op("scalar", lambda e, kk=kk, par=par: e.activation(
                    x2b[:, par * T:(par + 1) * T], x_f[:, kk * T:(kk + 1) * T], AF.Square), AC)
                p.wait("tensor", AC, ac_kk[kk])
                if kk == 0:
                    p.wait("tensor", LD, ld_small)
                for si, (a, b) in enumerate(slices):
                    p.op("tensor", lambda e, kk=kk, par=par, a=a, b=b, si=si: e.matmul(
                        ps_sum[:, si * 512: si * 512 + (b - a)], ones_sb[:],
                        x_bf[:, par * T + a: par * T + b], start=(kk == 0), stop=(kk == HB - 1)))
                    pe_kk[kk] = p.op("tensor", lambda e, kk=kk, par=par, a=a, b=b, si=si: e.matmul(
                        ps_sq[:, si * 512: si * 512 + (b - a)], ones_sb[:],
                        x2b[:, par * T + a: par * T + b], start=(kk == 0), stop=(kk == HB - 1)),
                        PE)
            p.wait("vector", PE, pe_kk[HB - 1])

            def _stats(e):
                for si, (a, b) in enumerate(slices):
                    e.tensor_scalar(mr_ln[:, a:b], ps_sum[:, si * 512: si * 512 + (b - a)], 1.0 / H, None, op0=ALU.mult)
                    last = e.tensor_scalar(vtmp[:, a:b], ps_sq[:, si * 512: si * 512 + (b - a)], 1.0 / H, None, op0=ALU.mult)
                return last
            ve_m = p.op("vector", _stats, VE)
            p.wait("scalar", VE, ve_m)
            am2 = p.op("scalar", lambda e: e.activation(xtmp[:, :], mr_ln[:, :], AF.Square), AC)
            p.wait("vector", AC, am2)
            ve_var = p.op("vector", lambda e: e.tensor_sub(rstd_ln[:, :], vtmp[:, :], xtmp[:, :]), VE)
            p.wait("scalar", VE, ve_var)
            p.op("scalar", lambda e: e.activation(vtmp[:, :], rstd_ln[:, :], AF.Ln, bias=eps_sb[:, 0:1]))
            a_r = p.op("scalar", lambda e: e.activation(rstd_ln[:, :], vtmp[:, :], AF.Exp, scale=-0.5), AC)
            p.wait("vector", AC, a_r)
            ve_mr = p.op("vector", lambda e: e.tensor_mul(xtmp[:, :], mr_ln[:, :], rstd_ln[:, :]), VE)
            for kk in range(HB):
                def _xn(e, kk=kk):
                    e.tensor_mul(vtmp[:, :], x_f[:, kk * T:(kk + 1) * T], rstd_ln[:, :])
                    return e.tensor_sub(xn[:, kk * T:(kk + 1) * T], vtmp[:, :], xtmp[:, :])
                ve_xn = p.op("vector", _xn, VE)
        if dbg_out("xn", (128, HB * T), BF16) is not None:
            p.wait("sync", VE, ve_xn)
            p.dma("sync", dbg["xn"].ap(), xn[:], ST)
        if dbg_out("ln_rstd", (128, T), F32) is not None:
            p.wait("sync", VE, ve_xn)
            p.dma("sync", dbg["ln_rstd"].ap(), rstd_ln[:], ST)
        if dbg_out("ln_mr", (128, T), F32) is not None:
            p.wait("sync", VE, ve_xn)
            p.dma("sync", dbg["ln_mr"].ap(), mr_ln[:], ST)
        barrier()
        a_ln.close()

        a_p23 = ExitStack()
        xbc_pre = sbuf_in(a_p23, "xbc_pre", [128, 34 * T], BF16)
        dt_raw = sbuf_in(a_p23, "dt_raw", [NH, TT], F32)
        a_p2 = ExitStack()
        z_stage = sbuf_in(a_p2, "z_stage", [128, 2 * TT], BF16)
        wog = sbuf_in(a_p2, "wog", [128, 2 * HB * 128], BF16)
        idx_buf = sbuf_in(a_p2, "idx_buf", [128, INTER], I32)
        idxbf = sbuf_in(a_p2, "idxbf", [128, INTER], BF16)
        bacc = sbuf_in(a_p2, "bacc", [128, INTER], BF16)
        btmp = sbuf_in(a_p2, "btmp", [128, INTER], BF16)

        build_list = [
            (gidx, 0, 128, INTER, cc_in_g, 0), (gidx, 128, 128, INTER, cc_in_g, 128),
            (uidx, 0, 128, INTER, cc_in_u, 0), (uidx, 128, 128, INTER, cc_in_u, 128),
            (didx, 0, 128, H, cc_in_d, 0), (didx, 128, 128, H, cc_in_d, 128),
            (didx, 256, 128, H, cc_in_d, 256), (didx, 384, 128, H, cc_in_d, 384),
            (didx, 512, 128, H, cc_in_d, 512), (didx, 640, 48, H, cc_in_d, 640),
        ]
        build_st = []
        _bstate = {"prev_ve": None}

        def emit_build_tile(bi):
            src_, r0, nr, ncol, ccdst, dr0 = build_list[bi]
            if _bstate["prev_ve"] is not None:
                p.wait("sync", VE, _bstate["prev_ve"])
            ldt = p.dma("sync", idx_buf[0:nr, 0:ncol], src_.ap()[r0:r0 + nr, :], LD)
            p.wait("vector", LD, ldt)

            def _build(e, ncol=ncol):
                e.tensor_copy(idxbf[:, 0:ncol], idx_buf[:, 0:ncol])
                e.tensor_scalar(bacc[:, 0:ncol], idxbf[:, 0:ncol], 0.0, lut_sb[:, 0:1], op0=ALU.is_equal, op1=ALU.mult)
                for v in range(1, 16):
                    e.tensor_scalar(btmp[:, 0:ncol], idxbf[:, 0:ncol], float(v), lut_sb[:, v:v + 1], op0=ALU.is_equal, op1=ALU.mult)
                    last = e.tensor_add(bacc[:, 0:ncol], bacc[:, 0:ncol], btmp[:, 0:ncol])
                return last
            p.op("vector", _build, VE)
            prev_ve = p.op("vector", lambda e, ncol=ncol: e.tensor_copy(
                flsh_b[:, 0:ncol // 32], bacc[:, 0:ncol:32]), VE)
            _bstate["prev_ve"] = prev_ve
            p.wait("sync", VE, prev_ve)
            build_st.append(p.dma("sync", ccdst.ap()[dr0:dr0 + nr, :], bacc[0:nr, 0:ncol], STB))

        with nc.psum_tensor("ps_p0", [128, T], F32) as ps_p0, \
             nc.psum_tensor("ps_p1", [128, T], F32) as ps_p1:
            psets = (ps_p0, ps_p1)
            evac_m = [0, 0]
            pe_m = [0, 0]
            zst_m = [0, 0]
            wlds = [0] * OC
            gp_waited = [0]

            def emit_wload(oc):
                s = oc % 2
                tgt = gp_win_marks[min(oc // 9, 7)]
                if tgt > gp_waited[0]:
                    p.wait("scalar", GP, tgt)
                    gp_waited[0] = tgt
                if pe_m[s]:
                    p.wait("scalar", PE, pe_m[s])
                src = w_in_bf.ap()[:, oc * 128:(oc + 1) * 128].rearrange("(k q) c -> q k c", k=HB)
                for hh in range(2):
                    wlds[oc] = p.dma("scalar", wog[:, s * 2048 + hh * 1024: s * 2048 + (hh + 1) * 1024]
                                     .rearrange("q (k c) -> q k c", k=8), src[:, hh * 8:(hh + 1) * 8, :], LDS)
            emit_wload(0)
            emit_wload(1)
            for oc in range(OC):
                if not SKIP_BUILD and oc % 6 == 0 and oc // 6 < len(build_list):
                    emit_build_tile(oc // 6)
                s = oc % 2
                ps = psets[s]
                p.wait("tensor", LDS, wlds[oc])
                if oc == 0:
                    p.wait("tensor", VE, ve_xn)
                if evac_m[s]:
                    p.wait("tensor", AC, evac_m[s])
                osl = [(0, 512, 0), (512, 1024, 512), (1024, 1027, 1024)] if 32 <= oc < 66 \
                    else [(3, 515, 0), (515, 1027, 512)]
                for kk in range(HB):
                    for (a, b, oa) in osl:
                        pe_m[s] = p.op("tensor", lambda e, s=s, kk=kk, a=a, b=b, oa=oa, ps=ps: e.matmul(
                            ps[:, oa:oa + (b - a)],
                            wog[:, s * 2048 + kk * 128: s * 2048 + (kk + 1) * 128],
                            xn[:, kk * T + a: kk * T + b],
                            start=(kk == 0), stop=(kk == HB - 1)),
                            PE if (kk == HB - 1 and b == 1027) else None)
                if oc + 2 < OC:
                    emit_wload(oc + 2)
                p.wait("scalar", PE, pe_m[s])
                if oc < 32:
                    if zst_m[s]:
                        p.wait("scalar", ST, zst_m[s])
                    p.op("scalar", lambda e, ps=ps, s=s: e.activation(
                        z_stage[:, s * TT:(s + 1) * TT], ps[:, 0:TT], AF.Silu))
                    evac_m[s] = p.op("scalar", lambda e, s=s: e.activation(
                        flsh_b[:, 0:64], z_stage[:, s * TT:(s + 1) * TT:16], AF.Identity), AC)
                    p.wait("sync", AC, evac_m[s])
                    zst_m[s] = p.dma("sync", z_dram.ap()[oc, :, :], z_stage[:, s * TT:(s + 1) * TT], ST)
                elif oc < 66:
                    bj = oc - 32
                    evac_m[s] = p.op("scalar", lambda e, ps=ps, bj=bj: e.activation(
                        xbc_pre[:, bj * T: bj * T + T], ps[:, 0:T], AF.Identity), AC)
                else:
                    evac_m[s] = p.op("scalar", lambda e, ps=ps: e.activation(
                        dt_raw[0:64, :], ps[0:64, 0:TT], AF.Identity), AC)
        cc_marks = {"g": 0, "u": 0, "d": 0}
        for key, st_i, cin, cout in (("g", 1, cc_in_g, wg_dram), ("u", 3, cc_in_u, wu_dram), ("d", 9, cc_in_d, wd_dram)) \
                if not SKIP_BUILD else ():
            p.wait("gpsimd", STB, build_st[st_i])
            cc_marks[key] = CC.bump(1)
            if SKIP_CC:
                p.do("gpsimd", lambda e: e.nop().then_inc(CC.h, 1))
            else:
                p.do("gpsimd", lambda e, cin=cin, cout=cout: e.collective_compute(
                    "AllGather", ALU.bypass, replica_groups=[list(range(NCORES))],
                    ins=[cin.ap().opt()], outs=[cout.ap().opt()]).then_inc(CC.h, 1))
        for i in range(2):
            p.dma("gpsimd", w_out_bf.ap()[i * 2048:(i + 1) * 2048, :], w_outT.ap()[i * 2048:(i + 1) * 2048, :], GP)
        gp_wout = GP.n
        if dbg_out("wg_dram", (H, INTER), BF16) is not None:
            p.wait("sync", CC, cc_marks["d"])
            p.dma("sync", dbg["wg_dram"].ap(), wg_dram.ap(), ST)
            p.dma("sync", dbg["wd_dram"].ap(), wd_dram.ap(), ST)
        if dbg_out("z_dram", (32, 128, TT), BF16) is not None:
            p.wait("sync", STS, STS.n)
            p.dma("sync", dbg["z_dram"].ap(), z_dram.ap(), ST)
        if dbg_out("xbc_pre", (128, 34 * T), BF16) is not None:
            p.wait("sync", AC, AC.n)
            p.dma("sync", dbg["xbc_pre"].ap(), xbc_pre[:], ST)
        if dbg_out("dt_raw", (NH, TT), F32) is not None:
            p.wait("sync", AC, AC.n)
            p.dma("sync", dbg["dt_raw"].ap(), dt_raw[:], ST)
        barrier()
        a_p2.close()

        a_p3 = ExitStack()
        x4_stage = sbuf_in(a_p3, "x4_stage", [128, 2 * TT], BF16)
        cacc = sbuf_in(a_p3, "cacc", [128, 2 * TT], F32)
        trow = sbuf_in(a_p3, "trow", [NH, 3 * TT], F32)
        dt_f = sbuf_in(a_p3, "dt_f", [NH, TT], F32)
        acs_f = sbuf_in(a_p3, "acs_f", [NH, TT], F32)
        decay_f = sbuf_in(a_p3, "decay_f", [NH, TT], F32)
        aend_f = sbuf_in(a_p3, "aend_f", [NH, 4], F32)
        aend_e = sbuf_in(a_p3, "aend_e", [NH, 4], F32)
        expa_sb = sbuf_in(a_p3, "expa_sb", [NH, TT], BF16)
        x4st_m = [0, 0]
        ac_conv = None
        for bj in range(34):
            s = bj % 2
            if ac_conv:
                p.wait("vector", AC, ac_conv)

            def _conv(e, bj=bj):
                base = bj * T
                e.tensor_scalar(cacc[:, 0:TT], xbc_pre[:, base: base + TT],
                                convw_sb[:, bj * 4: bj * 4 + 1], None, op0=ALU.mult)
                e.scalar_tensor_tensor(cacc[:, TT:2 * TT], xbc_pre[:, base + 1: base + 1 + TT],
                                       convw_sb[:, bj * 4 + 1: bj * 4 + 2],
                                       cacc[:, 0:TT], op0=ALU.mult, op1=ALU.add)
                e.scalar_tensor_tensor(cacc[:, 0:TT], xbc_pre[:, base + 2: base + 2 + TT],
                                       convw_sb[:, bj * 4 + 2: bj * 4 + 3],
                                       cacc[:, TT:2 * TT], op0=ALU.mult, op1=ALU.add)
                return e.scalar_tensor_tensor(cacc[:, TT:2 * TT], xbc_pre[:, base + 3: base + 3 + TT],
                                              convw_sb[:, bj * 4 + 3: bj * 4 + 4],
                                              cacc[:, 0:TT], op0=ALU.mult, op1=ALU.add)
            ve_conv = p.op("vector", _conv, VE)
            p.wait("scalar", VE, ve_conv)
            if bj < 32:
                if x4st_m[s]:
                    p.wait("scalar", ST, x4st_m[s])
                p.op("scalar", lambda e, bj=bj, s=s: e.activation(
                    x4_stage[:, s * TT:(s + 1) * TT], cacc[:, TT:2 * TT], AF.Silu,
                    bias=convb_sb[:, bj:bj + 1]))
                ac_conv = p.op("scalar", lambda e, s=s: e.activation(
                    flsh_b[:, 0:64], x4_stage[:, s * TT:(s + 1) * TT:16], AF.Identity), AC)
                p.wait("sync", AC, ac_conv)
                x4st_m[s] = p.dma("sync", x4_dram.ap()[bj, :, :], x4_stage[:, s * TT:(s + 1) * TT], ST)
            elif bj == 32:
                ac_conv = p.op("scalar", lambda e: e.activation(
                    b_f[:, :], cacc[:, TT:2 * TT], AF.Silu, bias=convb_sb[:, 32:33]), AC)
            else:
                ac_conv = p.op("scalar", lambda e: e.activation(
                    c_f[:, :], cacc[:, TT:2 * TT], AF.Silu, bias=convb_sb[:, 33:34]), AC)

        ac_a = p.op("scalar", lambda e: e.activation(nexpal_sb[:, :], alog_sb[:, :], AF.Exp), AC)
        p.op("scalar", lambda e: e.activation(trow[:, 0:TT], dt_raw[:, :], AF.Abs, bias=dtb_sb[:, 0:1]))
        p.op("scalar", lambda e: e.activation(trow[:, TT:2 * TT], trow[:, 0:TT], AF.Exp, scale=-1.0))
        p.op("scalar", lambda e: e.activation(trow[:, 0:TT], trow[:, TT:2 * TT], AF.Ln, bias=1.0))
        ac_sp = p.op("scalar", lambda e: e.activation(trow[:, TT:2 * TT], dt_raw[:, :], AF.Relu, bias=dtb_sb[:, 0:1]), AC)
        p.wait("vector", AC, ac_sp)

        def _dt1(e):
            e.tensor_add(trow[:, 2 * TT:3 * TT], trow[:, 0:TT], trow[:, TT:2 * TT])
            e.tensor_scalar_min(dt_f[:, :], trow[:, 2 * TT:3 * TT], 1e10)
            e.tensor_scalar(trow[:, 0:TT], dt_f[:, :], nexpal_sb[:, 0:1], -1.0, op0=ALU.mult, op1=ALU.mult)
            for c in range(4):
                last = e.tensor_tensor_scan(acs_f[:, c * 256:(c + 1) * 256], trow[:, c * 256:(c + 1) * 256],
                                            trow[:, c * 256:(c + 1) * 256], 0.0, op0=ALU.add, op1=ALU.bypass)
            return last
        ve_scan = p.op("vector", _dt1, VE)
        p.wait("vector", VE, ve_scan)

        def _dt2(e):
            for c in range(4):
                e.tensor_copy(aend_f[:, c:c + 1], acs_f[:, c * 256 + 255: c * 256 + 256])
            for c in range(4):
                last = e.tensor_scalar(trow[:, TT + c * 256: TT + (c + 1) * 256], acs_f[:, c * 256:(c + 1) * 256],
                                       aend_f[:, c:c + 1], None, op0=ALU.subtract)
            return last
        ve_dt = p.op("vector", _dt2, VE)
        p.wait("scalar", VE, ve_dt)
        ac_dec = p.op("scalar", lambda e: e.activation(decay_f[:, :], trow[:, TT:2 * TT], AF.Exp, scale=-1.0), AC)
        ac_expa = p.op("scalar", lambda e: e.activation(expa_sb[:, :], acs_f[:, :], AF.Exp), AC)
        ve_fl = p.op("vector", lambda e: e.tensor_copy(flsh_f[0:64, 0:64], acs_f[:, 0:TT - 3:16]), VE)
        p.wait("sync", VE, ve_fl)
        p.dma("sync", acs_dram.ap(), acs_f[:, :], ST)
        ac_fl = p.op("scalar", lambda e: e.activation(flsh_b[0:64, 0:64], expa_sb[:, 0:TT - 3:16], AF.Identity), AC)
        p.wait("sync", AC, ac_fl)
        p.dma("sync", expa_dram.ap(), expa_sb[:, :], ST)
        p.op("scalar", lambda e: e.activation(aend_e[:, :], aend_f[:, :], AF.Exp))
        ac_aee = p.op("scalar", lambda e: e.activation(flsh_f[0:64, 0:4], aend_e[:, :], AF.Identity), AC)
        p.wait("sync", AC, ac_aee)
        st_aend = p.dma("sync", aend_dram.ap().rearrange("o (c h) -> (o h) c", h=NH), aend_e[:, :], ST, allow_slow_non_contiguous=True)
        p.wait("sync", ST, st_aend)
        ld_wbc = p.dma("sync", wbc[:], aend_dram.ap()[0:1, :].broadcast_to([128, 4 * NH]), LD)

        with nc.psum_tensor("ps_tr", [128, 512], F32) as ps_tr:
            p.wait("tensor", AC, ac_dec)
            for src_t, dst_t in ((dt_f, dt_t), (acs_f, acs_t), (decay_f, decay_t)):
                for g in range(2):
                    for q in range(4):
                        sub = g * 4 + q
                        pm = p.op("tensor", lambda e, src_t=src_t, sub=sub, q=q: e.transpose(
                            ps_tr[:, q * 128: q * 128 + 64], src_t[0:64, sub * 128:(sub + 1) * 128], idf_sb[0:64, 0:64]),
                            PE if q == 3 else None)
                    p.wait("scalar", PE, pm)

                    def _etr(e, dst_t=dst_t, g=g):
                        for q in range(4):
                            last = e.activation(dst_t[:, (g * 4 + q) * 64:(g * 4 + q + 1) * 64],
                                                ps_tr[:, q * 128: q * 128 + 64], AF.Identity)
                        return last
                    am = p.op("scalar", _etr, AC)
                    p.wait("tensor", AC, am)

        for name, t_, shp, dt_ in (("dt_f", dt_f, (NH, TT), F32), ("acs_f", acs_f, (NH, TT), F32),
                                   ("decay_f", decay_f, (NH, TT), F32), ("dt_t", dt_t, (128, 8 * NH), F32),
                                   ("acs_t", acs_t, (128, 8 * NH), F32), ("decay_t", decay_t, (128, 8 * NH), F32),
                                   ("wbc", wbc, (128, 4 * NH), F32)):
            if dbg_out(name, shp, dt_) is not None:
                p.wait("sync", VE, VE.n)
                p.wait("sync", AC, AC.n)
                p.wait("sync", LD, LD.n)
                p.dma("sync", dbg[name].ap(), t_[:], ST)
        if dbg_out("bc_f", (2, 128, TT), BF16) is not None:
            p.wait("sync", AC, AC.n)
            p.dma("sync", dbg["bc_f"].ap()[0], b_f[:], ST)
            p.dma("sync", dbg["bc_f"].ap()[1], c_f[:], ST)
        barrier()
        a_p3.close()
        a_p23.close()

        a_p4o = ExitStack()
        x4c = sbuf_in(a_p4o, "x4c", [128, 32 * 256], BF16)
        xs_t0 = sbuf_in(a_p4o, "xs_t0", [128, 4096], BF16)
        xs_t1 = sbuf_in(a_p4o, "xs_t1", [128, 4096], BF16)
        xs_t = (xs_t0, xs_t1)
        b_t0 = sbuf_in(a_p4o, "b_t0", [128, 128], BF16)
        b_t1 = sbuf_in(a_p4o, "b_t1", [128, 128], BF16)
        b_t = (b_t0, b_t1)
        s0_all = sbuf_in(a_p4o, "s0_all", [128, 4 * NH * PD], BF16)

        a_p4 = ExitStack()
        xs2 = sbuf_in(a_p4, "xs2", [128, 4 * PD], BF16)
        states_all = sbuf_in(a_p4, "states_all", [128, 4 * NH * PD], BF16)
        s_loc = sbuf_in(a_p4, "s_loc", [128, 2 * NH * PD], BF16)
        gbuf = sbuf_in(a_p4, "gbuf", [128, 2 * NH * PD], BF16)

        def emit_chunk_transposes(c, tag, ps_ext=None, prev_pe=0, prev_ve=0):
            p.wait("sync", PE, prev_pe)
            p.wait("sync", VE, prev_ve)
            ldm = 0
            for g in range(4):
                ldm = p.dma("sync", x4c[:, g * 8 * 256:(g + 1) * 8 * 256].rearrange("q (j c) -> q j c", j=8),
                            x4_dram.ap()[g * 8:(g + 1) * 8, :, c * 256:(c + 1) * 256].rearrange("j q c -> q j c"), LD)
            from contextlib import nullcontext
            cm = nullcontext(ps_ext) if ps_ext is not None else nc.psum_tensor("ps_xT_%s" % tag, [128, 1024], BF16)
            with cm as ps_xT:
                p.wait("tensor", LD, ldm)
                for g in range(8):
                    for q in range(4):
                        j = g * 4 + q
                        for lt in range(2):
                            pm = p.op("tensor", lambda e, j=j, lt=lt, q=q: e.transpose(
                                ps_xT[:, (q * 2 + lt) * 128:(q * 2 + lt + 1) * 128],
                                x4c[:, j * 256 + lt * 128: j * 256 + (lt + 1) * 128], idb_sb[:, :]),
                                PE if (q == 3 and lt == 1) else None)
                    p.wait("vector", PE, pm)

                    def _evac(e, g=g, c=c):
                        for q in range(4):
                            j = g * 4 + q
                            for lt in range(2):
                                sub = 2 * c + lt
                                last = e.tensor_tensor(
                                    xs_t[lt][:, j * 128:(j + 1) * 128],
                                    ps_xT[:, (q * 2 + lt) * 128:(q * 2 + lt + 1) * 128],
                                    dt_t[:, sub * 64 + 2 * j: sub * 64 + 2 * j + 2][:, :, None].broadcast_to([128, 2, 64]),
                                    op=ALU.mult)
                        return last
                    ve_l = p.op("vector", _evac, VE)
                    p.wait("tensor", VE, ve_l)
                for lt in range(2):
                    pm = p.op("tensor", lambda e, lt=lt, c=c: e.transpose(
                        ps_xT[:, lt * 128:(lt + 1) * 128], b_f[:, c * 256 + lt * 128: c * 256 + (lt + 1) * 128],
                        idb_sb[:, :]), PE if lt == 1 else None)
                p.wait("scalar", PE, pm)

                def _ebt(e):
                    for lt in range(2):
                        last = e.activation(b_t[lt][:, :], ps_xT[:, lt * 128:(lt + 1) * 128], AF.Identity)
                    return last
                am = p.op("scalar", _ebt, AC)
                p.wait("tensor", AC, am)
                pm_tr = p.op("tensor", lambda e: e.nop(), PE)
            return ve_l, am, pm_tr

        with nc.psum_tensor("ps_st0", [128, 256], F32) as ps_st0, \
             nc.psum_tensor("ps_st1", [128, 256], F32) as ps_st1, \
             nc.psum_tensor("ps_st2", [128, 256], F32) as ps_st2, \
             nc.psum_tensor("ps_st3", [128, 256], F32) as ps_st3:
            st_banks = [(ps_st0, 0), (ps_st1, 0), (ps_st2, 0), (ps_st3, 0)]
            prev_pe_tr = 0
            for c in range(4):
                ve_l, am, prev_pe_tr = emit_chunk_transposes(c, "p1c%d" % c, prev_pe=prev_pe_tr)
                p.wait("vector", AC, am)
                last_evac = [0, 0, 0, 0]
                pe_h = {}
                for h in range(NH):
                    slot = h % 4
                    ps, half = st_banks[slot]
                    xpar = h % 2
                    if h >= 2:
                        p.wait("vector", PE, pe_h[h - 2])

                    def _xs2(e, c=c, h=h, xpar=xpar):
                        for lt in range(2):
                            sub = 2 * c + lt
                            last = e.tensor_scalar(xs2[:, (xpar * 2 + lt) * PD:(xpar * 2 + lt + 1) * PD],
                                                   xs_t[lt][:, h * PD:(h + 1) * PD],
                                                   decay_t[:, sub * NH + h: sub * NH + h + 1], None, op0=ALU.mult)
                        return last
                    vm = p.op("vector", _xs2, VE)
                    p.wait("tensor", VE, vm)
                    if last_evac[slot]:
                        p.wait("tensor", AC, last_evac[slot])
                    for lt in range(2):
                        pe_h[h] = p.op("tensor", lambda e, lt=lt, ps=ps, half=half, xpar=xpar: e.matmul(
                            ps[:, half * 256: half * 256 + PD], b_t[lt][:, :],
                            xs2[:, (xpar * 2 + lt) * PD:(xpar * 2 + lt + 1) * PD],
                            start=(lt == 0), stop=(lt == 1)), PE if lt == 1 else None)
                    p.wait("scalar", PE, pe_h[h])
                    last_evac[slot] = p.op("scalar", lambda e, c=c, h=h, ps=ps, half=half: e.activation(
                        states_all[:, (c * NH + h) * PD:(c * NH + h + 1) * PD], ps[:, half * 256: half * 256 + PD],
                        AF.Identity), AC)
                p.wait("vector", AC, last_evac[(NH - 1) % 4])

        p.wait("vector", AC, AC.n)

        def _rec_local(e):
            e.memset(s_loc[:, 0:NH * PD], 0.0)
            for c in range(4):
                src_h = (c % 2) * NH * PD
                dst_h = ((c + 1) % 2) * NH * PD
                for h in range(NH):
                    last = e.scalar_tensor_tensor(
                        s_loc[:, dst_h + h * PD: dst_h + (h + 1) * PD],
                        s_loc[:, src_h + h * PD: src_h + (h + 1) * PD],
                        wbc[:, c * NH + h: c * NH + h + 1],
                        states_all[:, (c * NH + h) * PD:(c * NH + h + 1) * PD],
                        op0=ALU.mult, op1=ALU.add)
            return last
        ve_rec = p.op("vector", _rec_local, VE)
        ve_rec = p.op("vector", lambda e: e.tensor_copy(flsh_b[:, 0:256], s_loc[:, 0:NH * PD:16]), VE)
        p.wait("sync", VE, ve_rec)
        st_spub = p.dma("sync", cc_in_s.ap(), s_loc[:, 0:NH * PD], ST)
        p.wait("gpsimd", ST, st_spub)
        cc_s = CC.bump(1)
        if SKIP_CC:
            p.do("gpsimd", lambda e: e.nop().then_inc(CC.h, 1))
        else:
            p.do("gpsimd", lambda e: e.collective_compute(
                "AllGather", ALU.bypass, replica_groups=[list(range(NCORES))],
                ins=[cc_in_s.ap().opt()], outs=[cc_out_s.ap().opt()]).then_inc(CC.h, 1))
        p.wait("sync", CC, cc_s)
        for r in range(NCORES):
            gld = p.dma("sync", gbuf[:, (r % 2) * NH * PD:((r % 2) + 1) * NH * PD],
                        cc_out_s.ap()[r * 128:(r + 1) * 128, :], LD)
            p.wait("vector", LD, gld)
            if r == 0:
                vm = p.op("vector", lambda e: e.tensor_scalar(
                    s0_all[:, 0:NH * PD], gbuf[:, 0:NH * PD], selbc[:, 0:1], None, op0=ALU.mult), VE)
            else:
                def _acc(e, r=r):
                    e.tensor_scalar(s_loc[:, NH * PD:2 * NH * PD], gbuf[:, (r % 2) * NH * PD:((r % 2) + 1) * NH * PD],
                                    selbc[:, r:r + 1], None, op0=ALU.mult)
                    return e.tensor_add(s0_all[:, 0:NH * PD], s0_all[:, 0:NH * PD], s_loc[:, NH * PD:2 * NH * PD])
                vm = p.op("vector", _acc, VE)
            p.wait("sync", VE, vm)

        def _rec_corr(e):
            for c in range(3):
                for h in range(NH):
                    last = e.scalar_tensor_tensor(
                        s0_all[:, ((c + 1) * NH + h) * PD:((c + 1) * NH + h + 1) * PD],
                        s0_all[:, (c * NH + h) * PD:(c * NH + h + 1) * PD],
                        wbc[:, c * NH + h: c * NH + h + 1],
                        states_all[:, (c * NH + h) * PD:(c * NH + h + 1) * PD],
                        op0=ALU.mult, op1=ALU.add)
            return last
        ve_s0 = p.op("vector", _rec_corr, VE)
        if dbg_out("s0_all", (128, 4 * NH * PD), BF16) is not None:
            p.wait("sync", VE, ve_s0)
            p.dma("sync", dbg["s0_all"].ap(), s0_all[:], ST)
        if dbg_out("states_all", (128, 4 * NH * PD), BF16) is not None:
            p.wait("sync", VE, ve_s0)
            p.dma("sync", dbg["states_all"].ap(), states_all[:], ST)
        barrier()
        a_p4.close()

        a_p5 = ExitStack()
        y_t0 = sbuf_in(a_p5, "y_t0", [128, 4096], BF16)
        y_t1 = sbuf_in(a_p5, "y_t1", [128, 4096], BF16)
        y_t = (y_t0, y_t1)
        cbm = sbuf_in(a_p5, "cbm", [128, 384], BF16)
        m_t = sbuf_in(a_p5, "m_t", [128, 384], BF16)
        dcl = sbuf_in(a_p5, "dcl", [128, 384], F32)
        acsbc = sbuf_in(a_p5, "acsbc", [128, 4096], F32)
        eabc = sbuf_in(a_p5, "eabc", [128, 4096], BF16)
        ce_sb = sbuf_in(a_p5, "ce_sb", [128, 2 * 4096], BF16)
        e_t = sbuf_in(a_p5, "e_t", [128, 384], BF16)
        z_c = sbuf_in(a_p5, "z_c", [128, 32 * 256], BF16)
        yfst = sbuf_in(a_p5, "yfst", [128, 32 * 256], BF16)
        yft = sbuf_in(a_p5, "yft", [128, 256], BF16)

        with nc.psum_tensor("ps_xT2", [128, 1024], BF16) as ps_xT2, \
             nc.psum_tensor("ps_cb", [128, 512], F32) as ps_cb, \
             nc.psum_tensor("ps_yA", [128, 384], F32) as ps_yA, \
             nc.psum_tensor("ps_yB", [128, 384], F32) as ps_yB:
            st_yf = 0
            am_sq_prev = 0
            prev_pe_tr = 0
            prev_ve_yf = 0
            for c in range(4):
                ve_l, am_bt, prev_pe_tr = emit_chunk_transposes(c, "p2c%d" % c, ps_ext=ps_xT2,
                                                               prev_pe=prev_pe_tr, prev_ve=prev_ve_yf)
                if am_sq_prev:
                    p.wait("sync", AC, am_sq_prev)
                ldz = 0
                for g in range(4):
                    ldz = p.dma("sync", z_c[:, g * 8 * 256:(g + 1) * 8 * 256].rearrange("q (j c) -> q j c", j=8),
                                z_dram.ap()[g * 8:(g + 1) * 8, :, c * 256:(c + 1) * 256].rearrange("j q c -> q j c"), LD)
                p.wait("tensor", AC, am_bt)
                p.op("tensor", lambda e, c=c: e.matmul(
                    ps_cb[:, 0:256], b_f[:, c * 256: c * 256 + 128], c_f[:, c * 256:(c + 1) * 256],
                    start=True, stop=True))
                pm_cb = p.op("tensor", lambda e, c=c: e.matmul(
                    ps_cb[:, 256:384], b_f[:, c * 256 + 128: c * 256 + 256], c_f[:, c * 256 + 128:(c + 1) * 256],
                    start=True, stop=True), PE)
                p.wait("vector", PE, pm_cb)

                def _cbm(e):
                    e.tensor_mul(cbm[:, 0:256], ps_cb[:, 0:256], mask0_sb[:, :])
                    return e.tensor_mul(cbm[:, 256:384], ps_cb[:, 256:384], mask0_sb[:, 0:128])
                ve_cbm = p.op("vector", _cbm, VE)
                p.wait("tensor", VE, ve_cbm)

                y_evac = [0, 0]
                pm_y_group = {}
                vm_d_last = 0
                ve_ce_last = 0
                for g in range(4):
                    par = g % 2
                    if g >= 2:
                        p.wait("sync", PE, pm_y_group[g - 2])
                    if vm_d_last:
                        p.wait("sync", VE, max(vm_d_last, ve_ce_last))
                    lde = 0
                    for hf in range(2):
                        lde = p.dma("sync",
                                    eabc[:, hf * 2048:(hf + 1) * 2048].rearrange("q (h l) -> q h l", h=8),
                                    expa_dram.ap()[g * 16 + hf * 8: g * 16 + (hf + 1) * 8,
                                                   c * 256:(c + 1) * 256][None].broadcast_to([128, 8, 256]), LD)
                        lde = p.dma("sync",
                                    acsbc[:, hf * 2048:(hf + 1) * 2048].rearrange("q (h l) -> q h l", h=8),
                                    acs_dram.ap()[g * 16 + hf * 8: g * 16 + (hf + 1) * 8,
                                                  c * 256:(c + 1) * 256][None].broadcast_to([128, 8, 256]), LD)
                    p.wait("vector", LD, lde)
                    if g >= 2:
                        p.wait("vector", PE, pm_y_group[g - 2])

                    def _ce(e, c=c, par=par):
                        for hl in range(16):
                            last = e.tensor_mul(ce_sb[:, par * 4096 + hl * 256: par * 4096 + (hl + 1) * 256],
                                                c_f[:, c * 256:(c + 1) * 256],
                                                eabc[:, hl * 256:(hl + 1) * 256])
                        return last
                    ve_ce = p.op("vector", _ce, VE)
                    ve_ce_last = ve_ce
                    pm_y = 0
                    for hl in range(16):
                        h = g * 16 + hl
                        slot = h % 2
                        psY = (ps_yA, ps_yB)[slot]

                        def _dclamp(e, c=c, h=h, hl=hl):
                            s0c = 2 * c
                            e.tensor_scalar(dcl[:, 0:256], acsbc[:, hl * 256:(hl + 1) * 256],
                                            acs_t[:, s0c * NH + h: s0c * NH + h + 1],
                                            0.0, op0=ALU.subtract, op1=ALU.min)
                            return e.tensor_scalar(dcl[:, 256:384], acsbc[:, hl * 256 + 128:(hl + 1) * 256],
                                                   acs_t[:, (s0c + 1) * NH + h: (s0c + 1) * NH + h + 1],
                                                   0.0, op0=ALU.subtract, op1=ALU.min)
                        vm_d = p.op("vector", _dclamp, VE)
                        vm_d_last = vm_d
                        p.wait("scalar", VE, vm_d)
                        am_e = p.op("scalar", lambda e: e.activation(e_t[:, 0:384], dcl[:, 0:384], AF.Exp), AC)
                        p.wait("vector", AC, am_e)
                        vm_m = p.op("vector", lambda e: e.tensor_mul(m_t[:, 0:384], e_t[:, 0:384], cbm[:, 0:384]), VE)
                        p.wait("tensor", VE, vm_m)
                        if y_evac[slot]:
                            p.wait("tensor", AC, y_evac[slot])
                        if hl == 0:
                            p.wait("tensor", VE, ve_ce)

                        def _ymm(e, c=c, h=h, hl=hl, par=par, psY=psY):
                            e.matmul(psY[:, 0:64], m_t[:, 0:128], xs_t[0][:, h * PD:(h + 1) * PD], start=True, stop=False)
                            e.matmul(psY[:, 0:64], ce_sb[:, par * 4096 + hl * 256: par * 4096 + hl * 256 + 128],
                                     s0_all[:, (c * NH + h) * PD:(c * NH + h + 1) * PD], start=False, stop=True)
                            e.matmul(psY[:, 64:128], m_t[:, 128:256], xs_t[0][:, h * PD:(h + 1) * PD], start=True, stop=False)
                            e.matmul(psY[:, 64:128], m_t[:, 256:384], xs_t[1][:, h * PD:(h + 1) * PD], start=False, stop=False)
                            return e.matmul(psY[:, 64:128],
                                            ce_sb[:, par * 4096 + hl * 256 + 128: par * 4096 + (hl + 1) * 256],
                                            s0_all[:, (c * NH + h) * PD:(c * NH + h + 1) * PD], start=False, stop=True)
                        pm_y = p.op("tensor", _ymm, PE)
                        p.wait("scalar", PE, pm_y)

                        def _yev(e, h=h, psY=psY):
                            e.activation(y_t[0][:, h * PD:(h + 1) * PD], psY[:, 0:64], AF.Identity)
                            return e.activation(y_t[1][:, h * PD:(h + 1) * PD], psY[:, 64:128], AF.Identity)
                        y_evac[slot] = p.op("scalar", _yev, AC)
                    pm_y_group[g] = pm_y

                p.wait("tensor", AC, y_evac[0])
                p.wait("tensor", AC, y_evac[1])
                p.wait("tensor", LD, ldz)
                if st_yf:
                    p.wait("tensor", ST, st_yf)
                ve_yf = None
                for g in range(8):
                    for q in range(4):
                        j = g * 4 + q
                        for lt in range(2):
                            pm = p.op("tensor", lambda e, j=j, lt=lt, q=q: e.transpose(
                                ps_xT2[:, (q * 2 + lt) * 128:(q * 2 + lt + 1) * 128],
                                y_t[lt][:, j * 128:(j + 1) * 128], idb_sb[:, :]),
                                PE if (q == 3 and lt == 1) else None)
                    p.wait("vector", PE, pm)

                    def _yfev(e, g=g, c=c):
                        for q in range(4):
                            j = g * 4 + q
                            for lt in range(2):
                                e.scalar_tensor_tensor(
                                    yft[:, lt * 128:(lt + 1) * 128],
                                    x4c[:, j * 256 + lt * 128: j * 256 + (lt + 1) * 128],
                                    dbc_f[:, j:j + 1],
                                    ps_xT2[:, (q * 2 + lt) * 128:(q * 2 + lt + 1) * 128],
                                    op0=ALU.mult, op1=ALU.add)
                            last = e.tensor_mul(yfst[:, j * 256:(j + 1) * 256], yft[:, 0:256], z_c[:, j * 256:(j + 1) * 256])
                        return last
                    ve_yf = p.op("vector", _yfev, VE)
                    p.wait("tensor", VE, ve_yf)
                p.wait("scalar", VE, ve_yf)
                am_sq = None
                for q in range(8):
                    am_sq = p.op("scalar", lambda e, q=q: e.activation(
                        z_c[:, q * 1024:(q + 1) * 1024], yfst[:, q * 1024:(q + 1) * 1024], AF.Square), AC)
                am_sq_prev = am_sq
                p.wait("tensor", AC, am_sq)
                for j in range(32):
                    pm_ss = p.op("tensor", lambda e, j=j: e.matmul(
                        ps_yA[:, 128:384], ones_sb[:], z_c[:, j * 256:(j + 1) * 256],
                        start=(j == 0), stop=(j == 31)), PE if j == 31 else None)
                p.wait("vector", PE, pm_ss)
                vm_ss = p.op("vector", lambda e, c=c: e.tensor_copy(
                    ss_sb[:, c * 256:(c + 1) * 256], ps_yA[:, 128:384]), VE)
                p.wait("tensor", VE, vm_ss)
                prev_ve_yf = ve_yf
                ve_yf = p.op("vector", lambda e: e.tensor_copy(flsh_b[:, 0:256], yfst[:, 0:32 * 256:32]), VE)
                p.wait("sync", VE, ve_yf)
                for g in range(4):
                    st_yf = p.dma("sync", yf_dram.ap()[g * 8:(g + 1) * 8, :, c * 256:(c + 1) * 256]
                                  .rearrange("j q c -> q j c"),
                                  yfst[:, g * 8 * 256:(g + 1) * 8 * 256].rearrange("q (j c) -> q j c", j=8), ST)
                if dbg_out("y_t_c%d" % c, (2, 128, 4096), BF16) is not None:
                    p.wait("sync", AC, AC.n)
                    p.dma("sync", dbg["y_t_c%d" % c].ap()[0], y_t[0][:], ST)
                    p.dma("sync", dbg["y_t_c%d" % c].ap()[1], y_t[1][:], ST)
            ve_ss_done = vm_ss
        if dbg_out("yf", (32, 128, TT), BF16) is not None:
            p.wait("sync", ST, st_yf)
            p.dma("sync", dbg["yf"].ap(), yf_dram.ap(), ST)
        barrier()
        a_p5.close()
        a_p4o.close()

        a_out = ExitStack()
        acc_bf = sbuf_in(a_out, "acc_bf", [128, DC * TT], BF16)
        a_p6 = ExitStack()
        rstd_bc = sbuf_in(a_p6, "rstd_bc", [128, TT], F32)
        yf_td = sbuf_in(a_p6, "yf_td", [128, 32 * 512], BF16)
        yfn_b = sbuf_in(a_p6, "yfn_b", [128, 2 * 512], BF16, align=2048)
        wo_buf = sbuf_in(a_p6, "wo_buf", [128, 2 * 1024], BF16, align=2048)

        ve_v = p.op("vector", lambda e: e.tensor_scalar(rstd_bc[:, :], ss_sb[:, :], 1.0 / 4096.0, None, op0=ALU.mult), VE)
        p.wait("scalar", VE, ve_v)
        p.op("scalar", lambda e: e.activation(ss_sb[:, :], rstd_bc[:, :], AF.Ln, bias=eps_sb[:, 0:1]))
        am_r = p.op("scalar", lambda e: e.activation(rstd_bc[:, :], ss_sb[:, :], AF.Exp, scale=-0.5), AC)

        with nc.psum_tensor("ps_out", [128, 8 * 512], F32) as ps_out:
            ac_out_evac = 0
            pe_wo = [0, 0]
            pe_last = 0
            p.wait("sync", GP, gp_wout)
            for td in range(2):
                if pe_last:
                    p.wait("sync", PE, pe_last)
                ldy = 0
                for g in range(8):
                    ldy = p.dma("sync", yf_td[:, g * 4 * 512:(g + 1) * 4 * 512].rearrange("q (j c) -> q j c", j=4),
                                yf_dram.ap()[g * 4:(g + 1) * 4, :, td * 512:(td + 1) * 512]
                                .rearrange("j q c -> q j c"), LD)
                for dh in range(2):
                    for I in range(32):
                        par = I % 2
                        if pe_wo[par]:
                            p.wait("sync", PE, pe_wo[par])
                            p.wait("vector", PE, pe_wo[par])
                        wld2 = p.dma("sync", wo_buf[:, par * 1024:(par + 1) * 1024],
                                     w_out_bf.ap()[I * 128:(I + 1) * 128, dh * 1024:(dh + 1) * 1024], LD)
                        p.wait("vector", LD, ldy if I == 0 else 0)
                        if td == 0 and dh == 0 and I == 0:
                            p.wait("vector", AC, am_r)

                        vm_n = p.op("vector", lambda e, I=I, td=td, par=par: e.scalar_tensor_tensor(
                            yfn_b[:, par * 512:(par + 1) * 512], yf_td[:, I * 512:(I + 1) * 512], nw_sb[:, I:I + 1],
                            rstd_bc[:, td * 512:(td + 1) * 512], op0=ALU.mult, op1=ALU.mult), VE)
                        p.wait("tensor", VE, vm_n)
                        p.wait("tensor", LD, wld2)
                        if I == 0 and ac_out_evac:
                            p.wait("tensor", AC, ac_out_evac)
                        for d in range(8):
                            pm = p.op("tensor", lambda e, d=d, par=par, I=I: e.matmul(
                                ps_out[:, d * 512:(d + 1) * 512],
                                wo_buf[:, par * 1024 + d * 128: par * 1024 + (d + 1) * 128],
                                yfn_b[:, par * 512:(par + 1) * 512], start=(I == 0), stop=(I == 31)),
                                PE if d == 7 else None)
                        pe_wo[par] = pm
                    pe_last = pe_wo[1]
                    p.wait("scalar", PE, pe_last)

                    def _eout(e, dh=dh, td=td):
                        for d in range(8):
                            last = e.activation(
                                acc_bf[:, (dh * 8 + d) * TT + td * 512:(dh * 8 + d) * TT + (td + 1) * 512],
                                ps_out[:, d * 512:(d + 1) * 512], AF.Identity)
                        return last
                    ac_out_evac = p.op("scalar", _eout, AC)
        if dbg_out("w_out_bf", (4096, H), BF16) is not None:
            p.wait("sync", GP, gp_wout)
            p.dma("sync", dbg["w_out_bf"].ap(), w_out_bf.ap(), ST)
        if dbg_out("rstd_bc", (128, TT), F32) is not None:
            p.wait("sync", AC, AC.n)
            p.dma("sync", dbg["rstd_bc"].ap(), rstd_bc[:], ST)
        if dbg_out("acc_bf", (128, DC * TT), BF16) is not None:
            p.wait("sync", AC, ac_out_evac)
            p.dma("sync", dbg["acc_bf"].ap(), acc_bf[:], ST)
        if dbg_out("acc_bf2", (128, DC * TT), BF16) is not None:
            p.wait("sync", AC, ac_out_evac)
            p.wait("sync", ST, ST.n)
            p.dma("sync", dbg["acc_bf2"].ap(), acc_bf[:], ST)
        barrier()
        a_p6.close()

        a_p7 = ExitStack()
        h_all = sbuf_in(a_p7, "h_all", [128, IC * TT], BF16)
        wgu_buf = sbuf_in(a_p7, "wgu_buf", [128, 2 * 2 * 2048], BF16)
        g_sb = sbuf_in(a_p7, "g_sb", [128, 2 * TT], BF16)
        with nc.psum_tensor("ps_g0", [128, TT], F32) as ps_g0, \
             nc.psum_tensor("ps_u0", [128, TT], F32) as ps_u0, \
             nc.psum_tensor("ps_g1", [128, TT], F32) as ps_g1, \
             nc.psum_tensor("ps_u1", [128, TT], F32) as ps_u1:
            ps_gu = [(ps_g0, ps_u0), (ps_g1, ps_u1)]
            pe_gu = [0, 0]
            ve_h = [0, 0]
            for i in range(IC):
                par = i % 2
                if pe_gu[par]:
                    p.wait("sync", PE, pe_gu[par])
                wldgu = 0
                for wi, wsrc in enumerate((wg_dram, wu_dram)):
                    for hh in range(2):
                        wldgu = p.dma(
                            "sync",
                            wgu_buf[:, par * 4096 + wi * 2048 + hh * 1024: par * 4096 + wi * 2048 + (hh + 1) * 1024]
                            .rearrange("q (k c) -> q k c", k=8),
                            wsrc.ap()[hh * 1024:(hh + 1) * 1024, i * 128:(i + 1) * 128]
                            .rearrange("(k q) c -> q k c", k=8), LD)
                p.wait("tensor", LD, wldgu)
                if i == 0:
                    p.wait("tensor", CC, cc_marks["u"])
                if ve_h[par]:
                    p.wait("tensor", VE, ve_h[par])
                psg, psu = ps_gu[par]
                for kk in range(HB):
                    for (a, b) in ((3, 515), (515, 1027)):
                        p.op("tensor", lambda e, par=par, kk=kk, a=a, b=b, psg=psg: e.matmul(
                            psg[:, a - 3: b - 3],
                            wgu_buf[:, par * 4096 + kk * 128: par * 4096 + (kk + 1) * 128],
                            xn[:, kk * T + a: kk * T + b], start=(kk == 0), stop=(kk == HB - 1)))
                        pm = p.op("tensor", lambda e, par=par, kk=kk, a=a, b=b, psu=psu: e.matmul(
                            psu[:, a - 3: b - 3],
                            wgu_buf[:, par * 4096 + 2048 + kk * 128: par * 4096 + 2048 + (kk + 1) * 128],
                            xn[:, kk * T + a: kk * T + b], start=(kk == 0), stop=(kk == HB - 1)),
                            PE if (kk == HB - 1 and b == 1027) else None)
                pe_gu[par] = pm
                p.wait("scalar", PE, pm)
                if ve_h[par]:
                    p.wait("scalar", VE, ve_h[par])
                am_g = p.op("scalar", lambda e, psg=psg, par=par: e.activation(
                    g_sb[:, par * TT:(par + 1) * TT], psg[:, :], AF.Silu), AC)
                p.wait("vector", AC, am_g)
                ve_h[par] = p.op("vector", lambda e, i=i, psu=psu, par=par: e.tensor_mul(
                    h_all[:, i * TT:(i + 1) * TT], g_sb[:, par * TT:(par + 1) * TT], psu[:, :]), VE)
        barrier()

        wd_buf = sbuf_in(a_p7, "wd_buf", [128, 2 * 1024], BF16)
        x_res = sbuf_in(a_p7, "x_res", [128, 2 * 512], F32)
        o_stage = sbuf_in(a_p7, "o_stage", [128, 2 * 512], F32)
        o_stage2 = sbuf_in(a_p7, "o_stage2", [128, 2 * 512], F32)
        with nc.psum_tensor("ps_d", [128, 8 * 512], F32) as ps_d:
            ve_fin = 0
            st_out = [0, 0]
            for th in range(2):
                for oh in range(2):
                    pe_d = [0, 0]
                    for i in range(IC):
                        par = i % 2
                        if pe_d[par]:
                            p.wait("sync", PE, pe_d[par])
                        wldd = p.dma("sync", wd_buf[:, par * 1024:(par + 1) * 1024],
                                     wd_dram.ap()[i * 128:(i + 1) * 128, oh * 1024:(oh + 1) * 1024], LD)
                        p.wait("tensor", LD, wldd)
                        if th == 0 and oh == 0 and i == 0:
                            p.wait("tensor", CC, cc_marks["d"])
                        if i == 0 and ve_fin:
                            p.wait("tensor", VE, ve_fin)
                        for d in range(8):
                            pm = p.op("tensor", lambda e, d=d, par=par, i=i, th=th: e.matmul(
                                ps_d[:, d * 512:(d + 1) * 512],
                                wd_buf[:, par * 1024 + d * 128: par * 1024 + (d + 1) * 128],
                                h_all[:, i * TT + th * 512: i * TT + (th + 1) * 512],
                                start=(i == 0), stop=(i == IC - 1)), PE if d == 7 else None)
                        pe_d[par] = pm
                    p.wait("vector", PE, pe_d[(IC - 1) % 2])
                    for d in range(8):
                        dch = oh * 8 + d
                        sp = d % 2
                        ldx2 = p.dma("sync", x_res[:, sp * 512:(sp + 1) * 512],
                                     x_in.ap()[dch * 128:(dch + 1) * 128, 3 + th * 512: 3 + (th + 1) * 512], LD)
                        p.wait("vector", LD, ldx2)
                        if st_out[sp]:
                            p.wait("vector", ST, st_out[sp])

                        def _fin(e, d=d, dch=dch, th=th, sp=sp):
                            e.tensor_add(o_stage[:, sp * 512:(sp + 1) * 512], ps_d[:, d * 512:(d + 1) * 512],
                                         acc_bf[:, dch * TT + th * 512: dch * TT + (th + 1) * 512])
                            return e.tensor_add(o_stage2[:, sp * 512:(sp + 1) * 512],
                                                o_stage[:, sp * 512:(sp + 1) * 512], x_res[:, sp * 512:(sp + 1) * 512])
                        p.op("vector", _fin, VE)
                        ve_fin = p.op("vector", lambda e, sp=sp: e.tensor_copy(
                            flsh_f[:, 0:32], o_stage2[:, sp * 512:(sp + 1) * 512:16]), VE)
                        p.wait("sync", VE, ve_fin)
                        st_out[sp] = p.dma("sync", out_t.ap()[dch * 128:(dch + 1) * 128, th * 512:(th + 1) * 512],
                                           o_stage2[:, sp * 512:(sp + 1) * 512], ST)
            p.wait("sync", ST, ST.n)
        a_p7.close()
        a_out.close()

        snaps.append({e: len(p.ev[e]) for e in Prog.ENGS})
        if PHASE_LIMIT is not None and PHASE_LIMIT <= len(snaps):
            cut = snaps[PHASE_LIMIT - 1]
            for e in Prog.ENGS:
                p.ev[e] = p.ev[e][:cut[e]]
        block = ctx.enter_context(nc.Block())
        for eng in Prog.ENGS:
            evs = p.ev[eng]
            if eng == "tensor":
                def _run_tensor(e, evs=evs):
                    flow = PEFlow(e, PEQ.h)
                    for f in evs:
                        f(flow)
                getattr(block, eng)(_run_tensor)
            else:
                getattr(block, eng)(lambda e, evs=evs: [f(e) for f in evs])

    return nc, dbg


def host_inputs(inputs):
    x = np.asarray(inputs["L0_attn_hidden_states"])
    w_in = np.asarray(inputs["L0_attn_W_in"])
    w_out = np.asarray(inputs["L0_attn_W_out"])
    conv_w = np.asarray(inputs["L0_attn_conv_w"])
    conv_b = np.asarray(inputs["L0_attn_conv_b"])
    dt_bias = np.asarray(inputs["L0_attn_dt_bias"])
    a_log = np.asarray(inputs["L0_attn_A_log"])
    dvec = np.asarray(inputs["L0_attn_D"])
    norm_w = np.asarray(inputs["L0_attn_norm_w"])
    lut = np.asarray(inputs["L0_ffn_lut"])
    gidx = np.asarray(inputs["L0_ffn_gate_idx"])
    uidx = np.asarray(inputs["L0_ffn_up_idx"])
    didx = np.asarray(inputs["L0_ffn_down_idx"])

    w_inT = np.zeros((H, PROJ), np.float32)
    w_inT[:, :8512] = w_in.T
    w_inT = np.ascontiguousarray(w_inT)
    w_outT = np.ascontiguousarray(w_out.T)
    convw_r = np.ascontiguousarray(conv_w.reshape(34, 128, 4).transpose(1, 0, 2).reshape(128, 136))
    convb_r = np.ascontiguousarray(conv_b.reshape(34, 128).T)
    nw_r = np.ascontiguousarray(norm_w.reshape(32, 128).T)
    gidxT = np.ascontiguousarray(gidx.T)
    uidxT = np.ascontiguousarray(uidx.T)
    didxT = np.ascontiguousarray(didx.T)

    in_maps = []
    for core in range(NCORES):
        b, half = core // 2, core % 2
        start = half * TT
        if half == 0:
            halo = np.zeros((3, H), np.float32)
        else:
            halo = x[b, start - 3: start]
        xc = np.ascontiguousarray(np.concatenate([halo, x[b, start:start + TT]], axis=0).T)
        selv = np.zeros((1, 8), np.float32)
        if half == 1:
            selv[0, core - 1] = 1.0
        in_maps.append({
            "x_core": xc,
            "w_inT": w_inT,
            "w_outT": w_outT,
            "convw": convw_r,
            "convb": convb_r,
            "dt_bias": np.ascontiguousarray(dt_bias.reshape(NH, 1)),
            "A_log": np.ascontiguousarray(a_log.reshape(NH, 1)),
            "D": np.ascontiguousarray(dvec.reshape(NH, 1)),
            "norm_w": nw_r,
            "lut": lut,
            "gidxT": np.ascontiguousarray(gidxT[core * 256:(core + 1) * 256]),
            "uidxT": np.ascontiguousarray(uidxT[core * 256:(core + 1) * 256]),
            "didxT": np.ascontiguousarray(didxT[core * 688:(core + 1) * 688]),
            "sel": selv,
        })
    return in_maps


def kernel(**inputs):
    in_maps = host_inputs(inputs)
    nc, dbg = build_program()
    res = run_bass_kernel_spmd(nc, in_maps, core_ids=list(range(NCORES)))
    out = np.empty((4, 2048, H), np.float32)
    for core in range(NCORES):
        b, half = core // 2, core % 2
        out[b, half * TT:(half + 1) * TT, :] = np.asarray(res.results[core]["out_core"]).T
    kernel._last_results = res.results
    return out


# version 13
# speedup vs baseline: 1.0009x; 1.0009x over previous
import numpy as np
import ml_dtypes
import concourse.bass as bass
import concourse.mybir as mybir
from concourse.bass_utils import run_bass_kernel_spmd
from contextlib import ExitStack

F32 = mybir.dt.float32
BF16 = mybir.dt.bfloat16
I32 = mybir.dt.int32
AF = mybir.ActivationFunctionType
ALU = mybir.AluOpType
BF = ml_dtypes.bfloat16

NCORES = 8
H = 2048
HB = 16
T = 1027
TT = 1024
OC = 67
PROJ = 8576
NH = 64
PD = 64
IC = 43
INTER = 5504
DC = 16

DEBUG = []
PHASE_LIMIT = None
DEV_SYNC = False
SKIP_CC = False
SKIP_BUILD = False


class Sem:
    def __init__(self, handle, dma=False):
        self.h = handle
        self.n = 0
        self.dma = dma

    def bump(self, k):
        self.n += k
        return self.n


class _MMShim:

    def __init__(self, flow):
        self.flow = flow

    def then_inc(self, sem, k):
        self.flow.e.wait_ge(self.flow.s, self.flow.n)
        return self.flow.e.nop().then_inc(sem, k)


class PEFlow:

    LAG = 1

    def __init__(self, e, sem_handle):
        self.e = e
        self.s = sem_handle
        self.n = 0

    def _flow(self, ins):
        ins.then_inc(self.s, 1)
        self.n += 1
        if self.n > self.LAG:
            self.e.wait_ge(self.s, self.n - self.LAG)
        return _MMShim(self)

    def matmul(self, *a, **k):
        return self._flow(self.e.matmul(*a, **k))

    def transpose(self, *a, **k):
        return self._flow(self.e.transpose(*a, **k))

    def __getattr__(self, name):
        return getattr(self.e, name)


class Prog:
    ENGS = ("sync", "tensor", "vector", "scalar", "gpsimd")

    def __init__(self, nc):
        self.nc = nc
        self.ev = {e: [] for e in self.ENGS}
        self.used = set()

    def do(self, eng, fn):
        self.used.add(eng)
        self.ev[eng].append(fn)

    def wait(self, eng, sem, val):
        if val and val > 0:
            if sem.dma:
                val = sem.n
            self.do(eng, lambda e, s=sem.h, v=val: e.wait_ge(s, v))

    def dma(self, eng, out, in_, sem, **kw):
        tgt = sem.bump(16)
        self.do(eng, lambda e, o=out, i=in_, s=sem.h, kw=kw: e.dma_start(out=o, in_=i, **kw).then_inc(s, 16))
        return tgt

    def op(self, eng, fn, sem=None):
        if sem is None:
            self.do(eng, fn)
            return None
        tgt = sem.bump(1)
        self.do(eng, lambda e, f=fn, s=sem.h: f(e).then_inc(s, 1))
        return tgt


def build_program():
    nc = bass.Bass("TRN2", target_bir_lowering=False, debug=False, detect_race_conditions=False)
    p = Prog(nc)
    dram = {}

    def din(name, shape, dt):
        dram[name] = nc.dram_tensor(name, list(shape), dt, kind="ExternalInput")
        return dram[name]

    def dint(name, shape, dt, **kw):
        dram[name] = nc.dram_tensor(name, list(shape), dt, kind="Internal", **kw)
        return dram[name]

    x_in = din("x_core", (H, T), F32)
    w_inT = din("w_inT", (H, PROJ), F32)
    w_outT = din("w_outT", (4096, H), F32)
    convw = din("convw", (128, 34 * 4), F32)
    convb = din("convb", (128, 34), F32)
    dtb = din("dt_bias", (NH, 1), F32)
    alog = din("A_log", (NH, 1), F32)
    dvec = din("D", (NH, 1), F32)
    nw = din("norm_w", (128, 32), F32)
    lut = din("lut", (16,), F32)
    gidx = din("gidxT", (256, INTER), I32)
    uidx = din("uidxT", (256, INTER), I32)
    didx = din("didxT", (688, H), I32)
    sel = din("sel", (1, 8), F32)

    out_t = nc.dram_tensor("out_core", [H, TT], F32, kind="ExternalOutput")

    w_in_bf = dint("w_in_bf", (H, PROJ), BF16)
    w_out_bf = dint("w_out_bf", (4096, H), BF16)
    z_dram = dint("z_dram", (32, 128, TT), BF16)
    x4_dram = dint("x4_dram", (32, 128, TT), BF16)
    yf_dram = dint("yf_dram", (32, 128, TT), BF16)
    acs_dram = dint("acs_dram", (NH, TT), F32)
    expa_dram = dint("expa_dram", (NH, TT), BF16)
    aend_dram = dint("aend_dram", (1, 4 * NH), F32)
    cc_in_g = dint("cc_in_g", (256, INTER), BF16)
    cc_in_u = dint("cc_in_u", (256, INTER), BF16)
    cc_in_d = dint("cc_in_d", (688, H), BF16)
    wg_dram = dint("wg_dram", (H, INTER), BF16)
    wu_dram = dint("wu_dram", (H, INTER), BF16)
    wd_dram = dint("wd_dram", (INTER, H), BF16)
    cc_in_s = dint("cc_in_s", (128, NH * PD), BF16)
    cc_out_s = dint("cc_out_s", (128 * NCORES, NH * PD), BF16)

    ident_bf = nc.inline_tensor(np.eye(128, dtype=BF), name="ident_bf")
    ident_f = nc.inline_tensor(np.eye(128, dtype=np.float32), name="ident_f")
    ones_sq_bf = nc.inline_tensor(np.ones((128, 128), BF), name="ones_sq_bf")
    _m0 = (np.arange(256)[None, :] >= np.arange(128)[:, None]).astype(BF)
    mask0 = nc.inline_tensor(_m0, name="mask0")
    eps_c = nc.inline_tensor(np.full((128, 1), 1e-5, np.float32), name="eps_c")

    dbg = {}
    if "wg_dram" in DEBUG:
        dbg["wd_dram"] = nc.dram_tensor("dbg_wd_dram", [INTER, H], BF16, kind="ExternalOutput")

    def dbg_out(name, shape, dt):
        if name in DEBUG:
            dbg[name] = nc.dram_tensor("dbg_" + name, list(shape), dt, kind="ExternalOutput")
            return dbg[name]
        return None

    ctx = ExitStack()
    with ctx:
        def sem(name, dma=False):
            return Sem(ctx.enter_context(nc.semaphore(name)), dma=dma)

        LD = sem("LD", dma=True)
        LDS = sem("LDS", dma=True)
        STS = sem("STS", dma=True)
        PEQ = sem("PEQ")
        PE = sem("PE")
        VE = sem("VE")
        AC = sem("AC")
        GP = sem("GP", dma=True)
        CC = sem("CC", dma=True)
        ST = sem("ST", dma=True)
        STB = sem("STB", dma=True)

        def sbuf_in(stack, name, shape, dt, align=32):
            return stack.enter_context(nc.sbuf_tensor(name, list(shape), dt, align_bytes=align))

        def sbuf(name, shape, dt):
            return sbuf_in(ctx, name, shape, dt)

        snaps = []

        def barrier():
            if DEV_SYNC:
                p.wait("gpsimd", CC, CC.n)
                p.wait("gpsimd", GP, GP.n)
            mpe = p.op("tensor", lambda e: e.nop(), PE)
            mve = p.op("vector", lambda e: e.nop(), VE)
            mac = p.op("scalar", lambda e: e.nop(), AC)
            for eng in ("sync", "tensor", "vector", "scalar", "gpsimd"):
                p.wait(eng, PE, mpe)
                p.wait(eng, VE, mve)
                p.wait(eng, AC, mac)
                p.wait(eng, LD, LD.n)
                p.wait(eng, ST, ST.n)
                p.wait(eng, LDS, LDS.n)
                p.wait(eng, STS, STS.n)
                p.wait(eng, STB, STB.n)
                if DEV_SYNC:
                    p.wait(eng, CC, CC.n)
            snaps.append({e: len(p.ev[e]) for e in Prog.ENGS})

        xn = sbuf("xn", [128, HB * T], BF16)
        b_f = sbuf("b_f", [128, TT], BF16)
        c_f = sbuf("c_f", [128, TT], BF16)
        dt_t = sbuf("dt_t", [128, 8 * NH], F32)
        acs_t = sbuf("acs_t", [128, 8 * NH], F32)
        decay_t = sbuf("decay_t", [128, 8 * NH], F32)
        wbc = sbuf("wbc", [128, 4 * NH], F32)
        dbc_f = sbuf("dbc_f", [128, 32], F32)
        selbc = sbuf("selbc", [128, 8], F32)
        nw_sb = sbuf("nw_sb", [128, 32], F32)
        lut_sb = sbuf("lut_sb", [128, 16], F32)
        convw_sb = sbuf("convw_sb", [128, 34 * 4], F32)
        convb_sb = sbuf("convb_sb", [128, 34], F32)
        dtb_sb = sbuf("dtb_sb", [NH, 1], F32)
        alog_sb = sbuf("alog_sb", [NH, 1], F32)
        nexpal_sb = sbuf("nexpal_sb", [NH, 1], F32)
        d_sb = sbuf("d_sb", [NH, 1], F32)
        idb_sb = sbuf("idb_sb", [128, 128], BF16)
        idf_sb = sbuf("idf_sb", [128, 128], F32)
        ones_sb = sbuf("ones_sb", [128, 128], BF16)
        mask0_sb = sbuf("mask0_sb", [128, 256], BF16)
        eps_sb = sbuf("eps_sb", [128, 1], F32)
        ss_sb = sbuf("ss_sb", [128, TT], F32)
        flsh_b = sbuf("flsh_b", [128, 256], BF16)
        flsh_f = sbuf("flsh_f", [128, 64], F32)

        for dst, src in [
            (idb_sb, ident_bf), (idf_sb, ident_f), (ones_sb, ones_sq_bf), (mask0_sb, mask0),
        ]:
            p.dma("sync", dst[:], src.ap(), LD)
        p.dma("sync", convw_sb[:], convw.ap(), LD)
        p.dma("sync", convb_sb[:], convb.ap(), LD)
        p.dma("sync", dtb_sb[:], dtb.ap(), LD)
        p.dma("sync", alog_sb[:], alog.ap(), LD)
        p.dma("sync", d_sb[:], dvec.ap(), LD)
        p.dma("sync", nw_sb[:], nw.ap(), LD)
        p.dma("sync", lut_sb[:], lut.ap()[None, :].broadcast_to([128, 16]), LD)
        p.dma("sync", selbc[:], sel.ap()[0:1, :].broadcast_to([128, 8]), LD)
        for ph in range(2):
            p.dma("sync", dbc_f[ph * 64:(ph + 1) * 64, :],
                  dvec.ap()[ph::2, 0][None, :].broadcast_to([64, 32]), LD, allow_slow_non_contiguous=True)
        ld_small = p.dma("sync", eps_sb[:], eps_c.ap(), LD)


        barrier()
        a_ln = ExitStack()
        x_f = sbuf_in(a_ln, "x_f", [128, HB * T], F32)
        x_bf = sbuf_in(a_ln, "x_bf", [128, 2 * T], BF16)
        x2b = sbuf_in(a_ln, "x2b", [128, 2 * T], BF16)
        rstd_ln = sbuf_in(a_ln, "rstd_ln", [128, T], F32)
        mr_ln = sbuf_in(a_ln, "mr_ln", [128, T], F32)
        vtmp = sbuf_in(a_ln, "vtmp", [128, T], F32)
        xtmp = sbuf_in(a_ln, "xtmp", [128, T], F32)
        ldx = []
        for kk in range(HB):
            ldx.append(p.dma("sync", x_f[:, kk * T:(kk + 1) * T], x_in.ap()[kk * 128:(kk + 1) * 128, :], LD))
        p.wait("gpsimd", LD, ldx[-1])
        gp_win_marks = []
        for j in range(8):
            c0, c1 = j * 9 * 128, min((j + 1) * 9 * 128, PROJ)
            p.dma("gpsimd", w_in_bf.ap()[:, c0:c1], w_inT.ap()[:, c0:c1], GP)
            gp_win_marks.append(GP.n)
        slices = [(0, 512), (512, 1024), (1024, 1027)]
        ve_xn = None
        with nc.psum_tensor("ps_sum", [128, 1536], F32) as ps_sum, \
             nc.psum_tensor("ps_sq", [128, 1536], F32) as ps_sq:
            pe_kk = {}
            ac_kk = {}
            ve_kk = {}
            for kk in range(HB):
                par = kk % 2
                p.wait("vector", LD, ldx[kk])
                if kk >= 2:
                    p.wait("vector", PE, pe_kk[kk - 2])
                ve_kk[kk] = p.op("vector", lambda e, kk=kk, par=par: e.tensor_copy(
                    x_bf[:, par * T:(par + 1) * T], x_f[:, kk * T:(kk + 1) * T]), VE)
                p.wait("scalar", VE, ve_kk[kk])
                ac_kk[kk] = p.op("scalar", lambda e, kk=kk, par=par: e.activation(
                    x2b[:, par * T:(par + 1) * T], x_f[:, kk * T:(kk + 1) * T], AF.Square), AC)
                p.wait("tensor", AC, ac_kk[kk])
                if kk == 0:
                    p.wait("tensor", LD, ld_small)
                for si, (a, b) in enumerate(slices):
                    p.op("tensor", lambda e, kk=kk, par=par, a=a, b=b, si=si: e.matmul(
                        ps_sum[:, si * 512: si * 512 + (b - a)], ones_sb[:],
                        x_bf[:, par * T + a: par * T + b], start=(kk == 0), stop=(kk == HB - 1)))
                    pe_kk[kk] = p.op("tensor", lambda e, kk=kk, par=par, a=a, b=b, si=si: e.matmul(
                        ps_sq[:, si * 512: si * 512 + (b - a)], ones_sb[:],
                        x2b[:, par * T + a: par * T + b], start=(kk == 0), stop=(kk == HB - 1)),
                        PE)
            p.wait("vector", PE, pe_kk[HB - 1])

            def _stats(e):
                for si, (a, b) in enumerate(slices):
                    e.tensor_scalar(mr_ln[:, a:b], ps_sum[:, si * 512: si * 512 + (b - a)], 1.0 / H, None, op0=ALU.mult)
                    last = e.tensor_scalar(vtmp[:, a:b], ps_sq[:, si * 512: si * 512 + (b - a)], 1.0 / H, None, op0=ALU.mult)
                return last
            ve_m = p.op("vector", _stats, VE)
            p.wait("scalar", VE, ve_m)
            am2 = p.op("scalar", lambda e: e.activation(xtmp[:, :], mr_ln[:, :], AF.Square), AC)
            p.wait("vector", AC, am2)
            ve_var = p.op("vector", lambda e: e.tensor_sub(rstd_ln[:, :], vtmp[:, :], xtmp[:, :]), VE)
            p.wait("scalar", VE, ve_var)
            p.op("scalar", lambda e: e.activation(vtmp[:, :], rstd_ln[:, :], AF.Ln, bias=eps_sb[:, 0:1]))
            a_r = p.op("scalar", lambda e: e.activation(rstd_ln[:, :], vtmp[:, :], AF.Exp, scale=-0.5), AC)
            p.wait("vector", AC, a_r)
            ve_mr = p.op("vector", lambda e: e.tensor_mul(xtmp[:, :], mr_ln[:, :], rstd_ln[:, :]), VE)
            for kk in range(HB):
                def _xn(e, kk=kk):
                    e.tensor_mul(vtmp[:, :], x_f[:, kk * T:(kk + 1) * T], rstd_ln[:, :])
                    return e.tensor_sub(xn[:, kk * T:(kk + 1) * T], vtmp[:, :], xtmp[:, :])
                ve_xn = p.op("vector", _xn, VE)
        if dbg_out("xn", (128, HB * T), BF16) is not None:
            p.wait("sync", VE, ve_xn)
            p.dma("sync", dbg["xn"].ap(), xn[:], ST)
        if dbg_out("ln_rstd", (128, T), F32) is not None:
            p.wait("sync", VE, ve_xn)
            p.dma("sync", dbg["ln_rstd"].ap(), rstd_ln[:], ST)
        if dbg_out("ln_mr", (128, T), F32) is not None:
            p.wait("sync", VE, ve_xn)
            p.dma("sync", dbg["ln_mr"].ap(), mr_ln[:], ST)
        barrier()
        a_ln.close()

        a_p23 = ExitStack()
        xbc_pre = sbuf_in(a_p23, "xbc_pre", [128, 34 * T], BF16)
        dt_raw = sbuf_in(a_p23, "dt_raw", [NH, TT], F32)
        a_p2 = ExitStack()
        z_stage = sbuf_in(a_p2, "z_stage", [128, 2 * TT], BF16)
        wog = sbuf_in(a_p2, "wog", [128, 2 * HB * 128], BF16)
        idx_buf = sbuf_in(a_p2, "idx_buf", [128, INTER], I32)
        idxbf = sbuf_in(a_p2, "idxbf", [128, INTER], BF16)
        bacc = sbuf_in(a_p2, "bacc", [128, INTER], BF16)
        btmp = sbuf_in(a_p2, "btmp", [128, INTER], BF16)

        build_list = [
            (gidx, 0, 128, INTER, cc_in_g, 0), (gidx, 128, 128, INTER, cc_in_g, 128),
            (uidx, 0, 128, INTER, cc_in_u, 0), (uidx, 128, 128, INTER, cc_in_u, 128),
            (didx, 0, 128, H, cc_in_d, 0), (didx, 128, 128, H, cc_in_d, 128),
            (didx, 256, 128, H, cc_in_d, 256), (didx, 384, 128, H, cc_in_d, 384),
            (didx, 512, 128, H, cc_in_d, 512), (didx, 640, 48, H, cc_in_d, 640),
        ]
        build_st = []
        _bstate = {"prev_ve": None}

        def emit_build_tile(bi):
            src_, r0, nr, ncol, ccdst, dr0 = build_list[bi]
            if _bstate["prev_ve"] is not None:
                p.wait("sync", VE, _bstate["prev_ve"])
            ldt = p.dma("sync", idx_buf[0:nr, 0:ncol], src_.ap()[r0:r0 + nr, :], LD)
            p.wait("vector", LD, ldt)

            def _build(e, ncol=ncol):
                e.tensor_copy(idxbf[:, 0:ncol], idx_buf[:, 0:ncol])
                e.tensor_scalar(bacc[:, 0:ncol], idxbf[:, 0:ncol], 0.0, lut_sb[:, 0:1], op0=ALU.is_equal, op1=ALU.mult)
                for v in range(1, 16):
                    e.tensor_scalar(btmp[:, 0:ncol], idxbf[:, 0:ncol], float(v), lut_sb[:, v:v + 1], op0=ALU.is_equal, op1=ALU.mult)
                    last = e.tensor_add(bacc[:, 0:ncol], bacc[:, 0:ncol], btmp[:, 0:ncol])
                return last
            p.op("vector", _build, VE)
            prev_ve = p.op("vector", lambda e, ncol=ncol: e.tensor_copy(
                flsh_b[:, 0:ncol // 32], bacc[:, 0:ncol:32]), VE)
            _bstate["prev_ve"] = prev_ve
            p.wait("sync", VE, prev_ve)
            build_st.append(p.dma("sync", ccdst.ap()[dr0:dr0 + nr, :], bacc[0:nr, 0:ncol], STB))

        with nc.psum_tensor("ps_p0", [128, T], F32) as ps_p0, \
             nc.psum_tensor("ps_p1", [128, T], F32) as ps_p1:
            psets = (ps_p0, ps_p1)
            evac_m = [0, 0]
            pe_m = [0, 0]
            zst_m = [0, 0]
            wlds = [0] * OC
            gp_waited = [0]

            def emit_wload(oc):
                s = oc % 2
                tgt = gp_win_marks[min(oc // 9, 7)]
                if tgt > gp_waited[0]:
                    p.wait("scalar", GP, tgt)
                    gp_waited[0] = tgt
                if pe_m[s]:
                    p.wait("scalar", PE, pe_m[s])
                src = w_in_bf.ap()[:, oc * 128:(oc + 1) * 128].rearrange("(k q) c -> q k c", k=HB)
                for hh in range(2):
                    wlds[oc] = p.dma("scalar", wog[:, s * 2048 + hh * 1024: s * 2048 + (hh + 1) * 1024]
                                     .rearrange("q (k c) -> q k c", k=8), src[:, hh * 8:(hh + 1) * 8, :], LDS)
            emit_wload(0)
            emit_wload(1)
            for oc in range(OC):
                if not SKIP_BUILD and oc % 6 == 0 and oc // 6 < len(build_list):
                    emit_build_tile(oc // 6)
                s = oc % 2
                ps = psets[s]
                p.wait("tensor", LDS, wlds[oc])
                if oc == 0:
                    p.wait("tensor", VE, ve_xn)
                if evac_m[s]:
                    p.wait("tensor", AC, evac_m[s])
                osl = [(0, 512, 0), (512, 1024, 512), (1024, 1027, 1024)] if 32 <= oc < 66 \
                    else [(3, 515, 0), (515, 1027, 512)]
                for kk in range(HB):
                    for (a, b, oa) in osl:
                        pe_m[s] = p.op("tensor", lambda e, s=s, kk=kk, a=a, b=b, oa=oa, ps=ps: e.matmul(
                            ps[:, oa:oa + (b - a)],
                            wog[:, s * 2048 + kk * 128: s * 2048 + (kk + 1) * 128],
                            xn[:, kk * T + a: kk * T + b],
                            start=(kk == 0), stop=(kk == HB - 1)),
                            PE if (kk == HB - 1 and b == 1027) else None)
                if oc + 2 < OC:
                    emit_wload(oc + 2)
                p.wait("scalar", PE, pe_m[s])
                if oc < 32:
                    if zst_m[s]:
                        p.wait("scalar", ST, zst_m[s])
                    p.op("scalar", lambda e, ps=ps, s=s: e.activation(
                        z_stage[:, s * TT:(s + 1) * TT], ps[:, 0:TT], AF.Silu))
                    evac_m[s] = p.op("scalar", lambda e, s=s: e.activation(
                        flsh_b[:, 0:64], z_stage[:, s * TT:(s + 1) * TT:16], AF.Identity), AC)
                    p.wait("sync", AC, evac_m[s])
                    zst_m[s] = p.dma("sync", z_dram.ap()[oc, :, :], z_stage[:, s * TT:(s + 1) * TT], ST)
                elif oc < 66:
                    bj = oc - 32
                    evac_m[s] = p.op("scalar", lambda e, ps=ps, bj=bj: e.activation(
                        xbc_pre[:, bj * T: bj * T + T], ps[:, 0:T], AF.Identity), AC)
                else:
                    evac_m[s] = p.op("scalar", lambda e, ps=ps: e.activation(
                        dt_raw[0:64, :], ps[0:64, 0:TT], AF.Identity), AC)
        cc_marks = {"g": 0, "u": 0, "d": 0}
        for key, st_i, cin, cout in (("g", 1, cc_in_g, wg_dram), ("u", 3, cc_in_u, wu_dram), ("d", 9, cc_in_d, wd_dram)) \
                if not SKIP_BUILD else ():
            p.wait("gpsimd", STB, build_st[st_i])
            cc_marks[key] = CC.bump(1)
            if SKIP_CC:
                p.do("gpsimd", lambda e: e.nop().then_inc(CC.h, 1))
            else:
                p.do("gpsimd", lambda e, cin=cin, cout=cout: e.collective_compute(
                    "AllGather", ALU.bypass, replica_groups=[list(range(NCORES))],
                    ins=[cin.ap().opt()], outs=[cout.ap().opt()]).then_inc(CC.h, 1))
        for i in range(2):
            p.dma("gpsimd", w_out_bf.ap()[i * 2048:(i + 1) * 2048, :], w_outT.ap()[i * 2048:(i + 1) * 2048, :], GP)
        gp_wout = GP.n
        if dbg_out("wg_dram", (H, INTER), BF16) is not None:
            p.wait("sync", CC, cc_marks["d"])
            p.dma("sync", dbg["wg_dram"].ap(), wg_dram.ap(), ST)
            p.dma("sync", dbg["wd_dram"].ap(), wd_dram.ap(), ST)
        if dbg_out("z_dram", (32, 128, TT), BF16) is not None:
            p.wait("sync", STS, STS.n)
            p.dma("sync", dbg["z_dram"].ap(), z_dram.ap(), ST)
        if dbg_out("xbc_pre", (128, 34 * T), BF16) is not None:
            p.wait("sync", AC, AC.n)
            p.dma("sync", dbg["xbc_pre"].ap(), xbc_pre[:], ST)
        if dbg_out("dt_raw", (NH, TT), F32) is not None:
            p.wait("sync", AC, AC.n)
            p.dma("sync", dbg["dt_raw"].ap(), dt_raw[:], ST)
        barrier()
        a_p2.close()

        a_p3 = ExitStack()
        x4_stage = sbuf_in(a_p3, "x4_stage", [128, 2 * TT], BF16)
        cacc = sbuf_in(a_p3, "cacc", [128, 2 * TT], F32)
        trow = sbuf_in(a_p3, "trow", [NH, 3 * TT], F32)
        dt_f = sbuf_in(a_p3, "dt_f", [NH, TT], F32)
        acs_f = sbuf_in(a_p3, "acs_f", [NH, TT], F32)
        decay_f = sbuf_in(a_p3, "decay_f", [NH, TT], F32)
        aend_f = sbuf_in(a_p3, "aend_f", [NH, 4], F32)
        aend_e = sbuf_in(a_p3, "aend_e", [NH, 4], F32)
        expa_sb = sbuf_in(a_p3, "expa_sb", [NH, TT], BF16)
        x4st_m = [0, 0]
        ac_conv = None
        for bj in range(34):
            s = bj % 2
            if ac_conv:
                p.wait("vector", AC, ac_conv)

            def _conv(e, bj=bj):
                base = bj * T
                e.tensor_scalar(cacc[:, 0:TT], xbc_pre[:, base: base + TT],
                                convw_sb[:, bj * 4: bj * 4 + 1], None, op0=ALU.mult)
                e.scalar_tensor_tensor(cacc[:, TT:2 * TT], xbc_pre[:, base + 1: base + 1 + TT],
                                       convw_sb[:, bj * 4 + 1: bj * 4 + 2],
                                       cacc[:, 0:TT], op0=ALU.mult, op1=ALU.add)
                e.scalar_tensor_tensor(cacc[:, 0:TT], xbc_pre[:, base + 2: base + 2 + TT],
                                       convw_sb[:, bj * 4 + 2: bj * 4 + 3],
                                       cacc[:, TT:2 * TT], op0=ALU.mult, op1=ALU.add)
                return e.scalar_tensor_tensor(cacc[:, TT:2 * TT], xbc_pre[:, base + 3: base + 3 + TT],
                                              convw_sb[:, bj * 4 + 3: bj * 4 + 4],
                                              cacc[:, 0:TT], op0=ALU.mult, op1=ALU.add)
            ve_conv = p.op("vector", _conv, VE)
            p.wait("scalar", VE, ve_conv)
            if bj < 32:
                if x4st_m[s]:
                    p.wait("scalar", ST, x4st_m[s])
                p.op("scalar", lambda e, bj=bj, s=s: e.activation(
                    x4_stage[:, s * TT:(s + 1) * TT], cacc[:, TT:2 * TT], AF.Silu,
                    bias=convb_sb[:, bj:bj + 1]))
                ac_conv = p.op("scalar", lambda e, s=s: e.activation(
                    flsh_b[:, 0:64], x4_stage[:, s * TT:(s + 1) * TT:16], AF.Identity), AC)
                p.wait("sync", AC, ac_conv)
                x4st_m[s] = p.dma("sync", x4_dram.ap()[bj, :, :], x4_stage[:, s * TT:(s + 1) * TT], ST)
            elif bj == 32:
                ac_conv = p.op("scalar", lambda e: e.activation(
                    b_f[:, :], cacc[:, TT:2 * TT], AF.Silu, bias=convb_sb[:, 32:33]), AC)
            else:
                ac_conv = p.op("scalar", lambda e: e.activation(
                    c_f[:, :], cacc[:, TT:2 * TT], AF.Silu, bias=convb_sb[:, 33:34]), AC)

        ac_a = p.op("scalar", lambda e: e.activation(nexpal_sb[:, :], alog_sb[:, :], AF.Exp), AC)
        p.op("scalar", lambda e: e.activation(trow[:, 0:TT], dt_raw[:, :], AF.Abs, bias=dtb_sb[:, 0:1]))
        p.op("scalar", lambda e: e.activation(trow[:, TT:2 * TT], trow[:, 0:TT], AF.Exp, scale=-1.0))
        p.op("scalar", lambda e: e.activation(trow[:, 0:TT], trow[:, TT:2 * TT], AF.Ln, bias=1.0))
        ac_sp = p.op("scalar", lambda e: e.activation(trow[:, TT:2 * TT], dt_raw[:, :], AF.Relu, bias=dtb_sb[:, 0:1]), AC)
        p.wait("vector", AC, ac_sp)

        def _dt1(e):
            e.tensor_add(trow[:, 2 * TT:3 * TT], trow[:, 0:TT], trow[:, TT:2 * TT])
            e.tensor_scalar_min(dt_f[:, :], trow[:, 2 * TT:3 * TT], 1e10)
            e.tensor_scalar(trow[:, 0:TT], dt_f[:, :], nexpal_sb[:, 0:1], -1.0, op0=ALU.mult, op1=ALU.mult)
            for c in range(4):
                last = e.tensor_tensor_scan(acs_f[:, c * 256:(c + 1) * 256], trow[:, c * 256:(c + 1) * 256],
                                            trow[:, c * 256:(c + 1) * 256], 0.0, op0=ALU.add, op1=ALU.bypass)
            return last
        ve_scan = p.op("vector", _dt1, VE)
        p.wait("vector", VE, ve_scan)

        def _dt2(e):
            for c in range(4):
                e.tensor_copy(aend_f[:, c:c + 1], acs_f[:, c * 256 + 255: c * 256 + 256])
            for c in range(4):
                last = e.tensor_scalar(trow[:, TT + c * 256: TT + (c + 1) * 256], acs_f[:, c * 256:(c + 1) * 256],
                                       aend_f[:, c:c + 1], None, op0=ALU.subtract)
            return last
        ve_dt = p.op("vector", _dt2, VE)
        p.wait("scalar", VE, ve_dt)
        ac_dec = p.op("scalar", lambda e: e.activation(decay_f[:, :], trow[:, TT:2 * TT], AF.Exp, scale=-1.0), AC)
        ac_expa = p.op("scalar", lambda e: e.activation(expa_sb[:, :], acs_f[:, :], AF.Exp), AC)
        ve_fl = p.op("vector", lambda e: e.tensor_copy(flsh_f[0:64, 0:64], acs_f[:, 0:TT - 3:16]), VE)
        p.wait("sync", VE, ve_fl)
        p.dma("sync", acs_dram.ap(), acs_f[:, :], ST)
        ac_fl = p.op("scalar", lambda e: e.activation(flsh_b[0:64, 0:64], expa_sb[:, 0:TT - 3:16], AF.Identity), AC)
        p.wait("sync", AC, ac_fl)
        p.dma("sync", expa_dram.ap(), expa_sb[:, :], ST)
        p.op("scalar", lambda e: e.activation(aend_e[:, :], aend_f[:, :], AF.Exp))
        ac_aee = p.op("scalar", lambda e: e.activation(flsh_f[0:64, 0:4], aend_e[:, :], AF.Identity), AC)
        p.wait("sync", AC, ac_aee)
        st_aend = p.dma("sync", aend_dram.ap().rearrange("o (c h) -> (o h) c", h=NH), aend_e[:, :], ST, allow_slow_non_contiguous=True)
        p.wait("sync", ST, st_aend)
        ld_wbc = p.dma("sync", wbc[:], aend_dram.ap()[0:1, :].broadcast_to([128, 4 * NH]), LD)

        with nc.psum_tensor("ps_tr", [128, 512], F32) as ps_tr:
            p.wait("tensor", AC, ac_dec)
            for src_t, dst_t in ((dt_f, dt_t), (acs_f, acs_t), (decay_f, decay_t)):
                for g in range(2):
                    for q in range(4):
                        sub = g * 4 + q
                        pm = p.op("tensor", lambda e, src_t=src_t, sub=sub, q=q: e.transpose(
                            ps_tr[:, q * 128: q * 128 + 64], src_t[0:64, sub * 128:(sub + 1) * 128], idf_sb[0:64, 0:64]),
                            PE if q == 3 else None)
                    p.wait("scalar", PE, pm)

                    def _etr(e, dst_t=dst_t, g=g):
                        for q in range(4):
                            last = e.activation(dst_t[:, (g * 4 + q) * 64:(g * 4 + q + 1) * 64],
                                                ps_tr[:, q * 128: q * 128 + 64], AF.Identity)
                        return last
                    am = p.op("scalar", _etr, AC)
                    p.wait("tensor", AC, am)

        for name, t_, shp, dt_ in (("dt_f", dt_f, (NH, TT), F32), ("acs_f", acs_f, (NH, TT), F32),
                                   ("decay_f", decay_f, (NH, TT), F32), ("dt_t", dt_t, (128, 8 * NH), F32),
                                   ("acs_t", acs_t, (128, 8 * NH), F32), ("decay_t", decay_t, (128, 8 * NH), F32),
                                   ("wbc", wbc, (128, 4 * NH), F32)):
            if dbg_out(name, shp, dt_) is not None:
                p.wait("sync", VE, VE.n)
                p.wait("sync", AC, AC.n)
                p.wait("sync", LD, LD.n)
                p.dma("sync", dbg[name].ap(), t_[:], ST)
        if dbg_out("bc_f", (2, 128, TT), BF16) is not None:
            p.wait("sync", AC, AC.n)
            p.dma("sync", dbg["bc_f"].ap()[0], b_f[:], ST)
            p.dma("sync", dbg["bc_f"].ap()[1], c_f[:], ST)
        barrier()
        a_p3.close()
        a_p23.close()

        a_p4o = ExitStack()
        x4c = sbuf_in(a_p4o, "x4c", [128, 32 * 256], BF16)
        xs_t0 = sbuf_in(a_p4o, "xs_t0", [128, 4096], BF16)
        xs_t1 = sbuf_in(a_p4o, "xs_t1", [128, 4096], BF16)
        xs_t = (xs_t0, xs_t1)
        b_t0 = sbuf_in(a_p4o, "b_t0", [128, 128], BF16)
        b_t1 = sbuf_in(a_p4o, "b_t1", [128, 128], BF16)
        b_t = (b_t0, b_t1)
        s0_all = sbuf_in(a_p4o, "s0_all", [128, 4 * NH * PD], BF16)

        a_p4 = ExitStack()
        xs2 = sbuf_in(a_p4, "xs2", [128, 4 * PD], BF16)
        states_all = sbuf_in(a_p4, "states_all", [128, 4 * NH * PD], BF16)
        s_loc = sbuf_in(a_p4, "s_loc", [128, 2 * NH * PD], BF16)
        gbuf = sbuf_in(a_p4, "gbuf", [128, 2 * NH * PD], BF16)

        def emit_chunk_transposes(c, tag, ps_ext=None, prev_pe=0, prev_ve=0):
            p.wait("sync", PE, prev_pe)
            p.wait("sync", VE, prev_ve)
            ldm = 0
            for g in range(4):
                ldm = p.dma("sync", x4c[:, g * 8 * 256:(g + 1) * 8 * 256].rearrange("q (j c) -> q j c", j=8),
                            x4_dram.ap()[g * 8:(g + 1) * 8, :, c * 256:(c + 1) * 256].rearrange("j q c -> q j c"), LD)
            from contextlib import nullcontext
            cm = nullcontext(ps_ext) if ps_ext is not None else nc.psum_tensor("ps_xT_%s" % tag, [128, 1024], BF16)
            with cm as ps_xT:
                p.wait("tensor", LD, ldm)
                for g in range(8):
                    for q in range(4):
                        j = g * 4 + q
                        for lt in range(2):
                            pm = p.op("tensor", lambda e, j=j, lt=lt, q=q: e.transpose(
                                ps_xT[:, (q * 2 + lt) * 128:(q * 2 + lt + 1) * 128],
                                x4c[:, j * 256 + lt * 128: j * 256 + (lt + 1) * 128], idb_sb[:, :]),
                                PE if (q == 3 and lt == 1) else None)
                    p.wait("vector", PE, pm)

                    def _evac(e, g=g, c=c):
                        for q in range(4):
                            j = g * 4 + q
                            for lt in range(2):
                                sub = 2 * c + lt
                                last = e.tensor_tensor(
                                    xs_t[lt][:, j * 128:(j + 1) * 128],
                                    ps_xT[:, (q * 2 + lt) * 128:(q * 2 + lt + 1) * 128],
                                    dt_t[:, sub * 64 + 2 * j: sub * 64 + 2 * j + 2][:, :, None].broadcast_to([128, 2, 64]),
                                    op=ALU.mult)
                        return last
                    ve_l = p.op("vector", _evac, VE)
                    p.wait("tensor", VE, ve_l)
                for lt in range(2):
                    pm = p.op("tensor", lambda e, lt=lt, c=c: e.transpose(
                        ps_xT[:, lt * 128:(lt + 1) * 128], b_f[:, c * 256 + lt * 128: c * 256 + (lt + 1) * 128],
                        idb_sb[:, :]), PE if lt == 1 else None)
                p.wait("scalar", PE, pm)

                def _ebt(e):
                    for lt in range(2):
                        last = e.activation(b_t[lt][:, :], ps_xT[:, lt * 128:(lt + 1) * 128], AF.Identity)
                    return last
                am = p.op("scalar", _ebt, AC)
                p.wait("tensor", AC, am)
                pm_tr = p.op("tensor", lambda e: e.nop(), PE)
            return ve_l, am, pm_tr

        with nc.psum_tensor("ps_st0", [128, 256], F32) as ps_st0, \
             nc.psum_tensor("ps_st1", [128, 256], F32) as ps_st1, \
             nc.psum_tensor("ps_st2", [128, 256], F32) as ps_st2, \
             nc.psum_tensor("ps_st3", [128, 256], F32) as ps_st3:
            st_banks = [(ps_st0, 0), (ps_st1, 0), (ps_st2, 0), (ps_st3, 0)]
            prev_pe_tr = 0
            for c in range(4):
                ve_l, am, prev_pe_tr = emit_chunk_transposes(c, "p1c%d" % c, prev_pe=prev_pe_tr)
                p.wait("vector", AC, am)
                last_evac = [0, 0, 0, 0]
                pe_h = {}
                for h in range(NH):
                    slot = h % 4
                    ps, half = st_banks[slot]
                    xpar = h % 2
                    if h >= 2:
                        p.wait("vector", PE, pe_h[h - 2])

                    def _xs2(e, c=c, h=h, xpar=xpar):
                        for lt in range(2):
                            sub = 2 * c + lt
                            last = e.tensor_scalar(xs2[:, (xpar * 2 + lt) * PD:(xpar * 2 + lt + 1) * PD],
                                                   xs_t[lt][:, h * PD:(h + 1) * PD],
                                                   decay_t[:, sub * NH + h: sub * NH + h + 1], None, op0=ALU.mult)
                        return last
                    vm = p.op("vector", _xs2, VE)
                    p.wait("tensor", VE, vm)
                    if last_evac[slot]:
                        p.wait("tensor", AC, last_evac[slot])
                    for lt in range(2):
                        pe_h[h] = p.op("tensor", lambda e, lt=lt, ps=ps, half=half, xpar=xpar: e.matmul(
                            ps[:, half * 256: half * 256 + PD], b_t[lt][:, :],
                            xs2[:, (xpar * 2 + lt) * PD:(xpar * 2 + lt + 1) * PD],
                            start=(lt == 0), stop=(lt == 1)), PE if lt == 1 else None)
                    p.wait("scalar", PE, pe_h[h])
                    last_evac[slot] = p.op("scalar", lambda e, c=c, h=h, ps=ps, half=half: e.activation(
                        states_all[:, (c * NH + h) * PD:(c * NH + h + 1) * PD], ps[:, half * 256: half * 256 + PD],
                        AF.Identity), AC)
                p.wait("vector", AC, last_evac[(NH - 1) % 4])

        p.wait("vector", AC, AC.n)

        def _rec_local(e):
            e.memset(s_loc[:, 0:NH * PD], 0.0)
            for c in range(4):
                src_h = (c % 2) * NH * PD
                dst_h = ((c + 1) % 2) * NH * PD
                for h in range(NH):
                    last = e.scalar_tensor_tensor(
                        s_loc[:, dst_h + h * PD: dst_h + (h + 1) * PD],
                        s_loc[:, src_h + h * PD: src_h + (h + 1) * PD],
                        wbc[:, c * NH + h: c * NH + h + 1],
                        states_all[:, (c * NH + h) * PD:(c * NH + h + 1) * PD],
                        op0=ALU.mult, op1=ALU.add)
            return last
        ve_rec = p.op("vector", _rec_local, VE)
        ve_rec = p.op("vector", lambda e: e.tensor_copy(flsh_b[:, 0:256], s_loc[:, 0:NH * PD:16]), VE)
        p.wait("sync", VE, ve_rec)
        st_spub = p.dma("sync", cc_in_s.ap(), s_loc[:, 0:NH * PD], ST)
        p.wait("gpsimd", ST, st_spub)
        cc_s = CC.bump(1)
        if SKIP_CC:
            p.do("gpsimd", lambda e: e.nop().then_inc(CC.h, 1))
        else:
            p.do("gpsimd", lambda e: e.collective_compute(
                "AllGather", ALU.bypass, replica_groups=[list(range(NCORES))],
                ins=[cc_in_s.ap().opt()], outs=[cc_out_s.ap().opt()]).then_inc(CC.h, 1))
        p.wait("sync", CC, cc_s)
        for r in range(NCORES):
            gld = p.dma("sync", gbuf[:, (r % 2) * NH * PD:((r % 2) + 1) * NH * PD],
                        cc_out_s.ap()[r * 128:(r + 1) * 128, :], LD)
            p.wait("vector", LD, gld)
            if r == 0:
                vm = p.op("vector", lambda e: e.tensor_scalar(
                    s0_all[:, 0:NH * PD], gbuf[:, 0:NH * PD], selbc[:, 0:1], None, op0=ALU.mult), VE)
            else:
                def _acc(e, r=r):
                    e.tensor_scalar(s_loc[:, NH * PD:2 * NH * PD], gbuf[:, (r % 2) * NH * PD:((r % 2) + 1) * NH * PD],
                                    selbc[:, r:r + 1], None, op0=ALU.mult)
                    return e.tensor_add(s0_all[:, 0:NH * PD], s0_all[:, 0:NH * PD], s_loc[:, NH * PD:2 * NH * PD])
                vm = p.op("vector", _acc, VE)
            p.wait("sync", VE, vm)

        def _rec_corr(e):
            for c in range(3):
                for h in range(NH):
                    last = e.scalar_tensor_tensor(
                        s0_all[:, ((c + 1) * NH + h) * PD:((c + 1) * NH + h + 1) * PD],
                        s0_all[:, (c * NH + h) * PD:(c * NH + h + 1) * PD],
                        wbc[:, c * NH + h: c * NH + h + 1],
                        states_all[:, (c * NH + h) * PD:(c * NH + h + 1) * PD],
                        op0=ALU.mult, op1=ALU.add)
            return last
        ve_s0 = p.op("vector", _rec_corr, VE)
        if dbg_out("s0_all", (128, 4 * NH * PD), BF16) is not None:
            p.wait("sync", VE, ve_s0)
            p.dma("sync", dbg["s0_all"].ap(), s0_all[:], ST)
        if dbg_out("states_all", (128, 4 * NH * PD), BF16) is not None:
            p.wait("sync", VE, ve_s0)
            p.dma("sync", dbg["states_all"].ap(), states_all[:], ST)
        barrier()
        a_p4.close()

        a_p5 = ExitStack()
        y_t0 = sbuf_in(a_p5, "y_t0", [128, 4096], BF16)
        y_t1 = sbuf_in(a_p5, "y_t1", [128, 4096], BF16)
        y_t = (y_t0, y_t1)
        cbm = sbuf_in(a_p5, "cbm", [128, 384], BF16)
        m_t = sbuf_in(a_p5, "m_t", [128, 384], BF16)
        dcl = sbuf_in(a_p5, "dcl", [128, 384], F32)
        acsbc = sbuf_in(a_p5, "acsbc", [128, 4096], F32)
        eabc = sbuf_in(a_p5, "eabc", [128, 4096], BF16)
        ce_sb = sbuf_in(a_p5, "ce_sb", [128, 2 * 4096], BF16)
        e_t = sbuf_in(a_p5, "e_t", [128, 384], BF16)
        z_c = sbuf_in(a_p5, "z_c", [128, 32 * 256], BF16)
        yfst = sbuf_in(a_p5, "yfst", [128, 32 * 256], BF16)
        yft = sbuf_in(a_p5, "yft", [128, 256], BF16)

        with nc.psum_tensor("ps_xT2", [128, 1024], BF16) as ps_xT2, \
             nc.psum_tensor("ps_cb", [128, 512], F32) as ps_cb, \
             nc.psum_tensor("ps_yA", [128, 384], F32) as ps_yA, \
             nc.psum_tensor("ps_yB", [128, 384], F32) as ps_yB:
            st_yf = 0
            am_sq_prev = 0
            prev_pe_tr = 0
            prev_ve_yf = 0
            for c in range(4):
                ve_l, am_bt, prev_pe_tr = emit_chunk_transposes(c, "p2c%d" % c, ps_ext=ps_xT2,
                                                               prev_pe=prev_pe_tr, prev_ve=prev_ve_yf)
                if am_sq_prev:
                    p.wait("gpsimd", AC, am_sq_prev)
                ldz = 0
                for g in range(4):
                    ldz = p.dma("gpsimd", z_c[:, g * 8 * 256:(g + 1) * 8 * 256].rearrange("q (j c) -> q j c", j=8),
                                z_dram.ap()[g * 8:(g + 1) * 8, :, c * 256:(c + 1) * 256].rearrange("j q c -> q j c"), GP)
                p.wait("tensor", AC, am_bt)
                p.op("tensor", lambda e, c=c: e.matmul(
                    ps_cb[:, 0:256], b_f[:, c * 256: c * 256 + 128], c_f[:, c * 256:(c + 1) * 256],
                    start=True, stop=True))
                pm_cb = p.op("tensor", lambda e, c=c: e.matmul(
                    ps_cb[:, 256:384], b_f[:, c * 256 + 128: c * 256 + 256], c_f[:, c * 256 + 128:(c + 1) * 256],
                    start=True, stop=True), PE)
                p.wait("vector", PE, pm_cb)

                def _cbm(e):
                    e.tensor_mul(cbm[:, 0:256], ps_cb[:, 0:256], mask0_sb[:, :])
                    return e.tensor_mul(cbm[:, 256:384], ps_cb[:, 256:384], mask0_sb[:, 0:128])
                ve_cbm = p.op("vector", _cbm, VE)
                p.wait("tensor", VE, ve_cbm)

                y_evac = [0, 0]
                pm_y_group = {}
                vm_d_last = 0
                ve_ce_last = 0
                for g in range(4):
                    par = g % 2
                    if g >= 2:
                        p.wait("gpsimd", PE, pm_y_group[g - 2])
                    if vm_d_last:
                        p.wait("gpsimd", VE, max(vm_d_last, ve_ce_last))
                    lde = 0
                    for hf in range(2):
                        lde = p.dma("gpsimd",
                                    eabc[:, hf * 2048:(hf + 1) * 2048].rearrange("q (h l) -> q h l", h=8),
                                    expa_dram.ap()[g * 16 + hf * 8: g * 16 + (hf + 1) * 8,
                                                   c * 256:(c + 1) * 256][None].broadcast_to([128, 8, 256]), GP)
                        lde = p.dma("gpsimd",
                                    acsbc[:, hf * 2048:(hf + 1) * 2048].rearrange("q (h l) -> q h l", h=8),
                                    acs_dram.ap()[g * 16 + hf * 8: g * 16 + (hf + 1) * 8,
                                                  c * 256:(c + 1) * 256][None].broadcast_to([128, 8, 256]), GP)
                    p.wait("vector", GP, lde)
                    if g >= 2:
                        p.wait("vector", PE, pm_y_group[g - 2])

                    def _ce(e, c=c, par=par):
                        for hl in range(16):
                            last = e.tensor_mul(ce_sb[:, par * 4096 + hl * 256: par * 4096 + (hl + 1) * 256],
                                                c_f[:, c * 256:(c + 1) * 256],
                                                eabc[:, hl * 256:(hl + 1) * 256])
                        return last
                    ve_ce = p.op("vector", _ce, VE)
                    ve_ce_last = ve_ce
                    pm_y = 0
                    for hl in range(16):
                        h = g * 16 + hl
                        slot = h % 2
                        psY = (ps_yA, ps_yB)[slot]

                        def _dclamp(e, c=c, h=h, hl=hl):
                            s0c = 2 * c
                            e.tensor_scalar(dcl[:, 0:256], acsbc[:, hl * 256:(hl + 1) * 256],
                                            acs_t[:, s0c * NH + h: s0c * NH + h + 1],
                                            0.0, op0=ALU.subtract, op1=ALU.min)
                            return e.tensor_scalar(dcl[:, 256:384], acsbc[:, hl * 256 + 128:(hl + 1) * 256],
                                                   acs_t[:, (s0c + 1) * NH + h: (s0c + 1) * NH + h + 1],
                                                   0.0, op0=ALU.subtract, op1=ALU.min)
                        vm_d = p.op("vector", _dclamp, VE)
                        vm_d_last = vm_d
                        p.wait("scalar", VE, vm_d)
                        am_e = p.op("scalar", lambda e: e.activation(e_t[:, 0:384], dcl[:, 0:384], AF.Exp), AC)
                        p.wait("vector", AC, am_e)
                        vm_m = p.op("vector", lambda e: e.tensor_mul(m_t[:, 0:384], e_t[:, 0:384], cbm[:, 0:384]), VE)
                        p.wait("tensor", VE, vm_m)
                        if y_evac[slot]:
                            p.wait("tensor", AC, y_evac[slot])
                        if hl == 0:
                            p.wait("tensor", VE, ve_ce)

                        def _ymm(e, c=c, h=h, hl=hl, par=par, psY=psY):
                            e.matmul(psY[:, 0:64], m_t[:, 0:128], xs_t[0][:, h * PD:(h + 1) * PD], start=True, stop=False)
                            e.matmul(psY[:, 0:64], ce_sb[:, par * 4096 + hl * 256: par * 4096 + hl * 256 + 128],
                                     s0_all[:, (c * NH + h) * PD:(c * NH + h + 1) * PD], start=False, stop=True)
                            e.matmul(psY[:, 64:128], m_t[:, 128:256], xs_t[0][:, h * PD:(h + 1) * PD], start=True, stop=False)
                            e.matmul(psY[:, 64:128], m_t[:, 256:384], xs_t[1][:, h * PD:(h + 1) * PD], start=False, stop=False)
                            return e.matmul(psY[:, 64:128],
                                            ce_sb[:, par * 4096 + hl * 256 + 128: par * 4096 + (hl + 1) * 256],
                                            s0_all[:, (c * NH + h) * PD:(c * NH + h + 1) * PD], start=False, stop=True)
                        pm_y = p.op("tensor", _ymm, PE)
                        p.wait("scalar", PE, pm_y)

                        def _yev(e, h=h, psY=psY):
                            e.activation(y_t[0][:, h * PD:(h + 1) * PD], psY[:, 0:64], AF.Identity)
                            return e.activation(y_t[1][:, h * PD:(h + 1) * PD], psY[:, 64:128], AF.Identity)
                        y_evac[slot] = p.op("scalar", _yev, AC)
                    pm_y_group[g] = pm_y

                p.wait("tensor", AC, y_evac[0])
                p.wait("tensor", AC, y_evac[1])
                p.wait("tensor", GP, ldz)
                if st_yf:
                    p.wait("tensor", ST, st_yf)
                ve_yf = None
                for g in range(8):
                    for q in range(4):
                        j = g * 4 + q
                        for lt in range(2):
                            pm = p.op("tensor", lambda e, j=j, lt=lt, q=q: e.transpose(
                                ps_xT2[:, (q * 2 + lt) * 128:(q * 2 + lt + 1) * 128],
                                y_t[lt][:, j * 128:(j + 1) * 128], idb_sb[:, :]),
                                PE if (q == 3 and lt == 1) else None)
                    p.wait("vector", PE, pm)

                    def _yfev(e, g=g, c=c):
                        for q in range(4):
                            j = g * 4 + q
                            for lt in range(2):
                                e.scalar_tensor_tensor(
                                    yft[:, lt * 128:(lt + 1) * 128],
                                    x4c[:, j * 256 + lt * 128: j * 256 + (lt + 1) * 128],
                                    dbc_f[:, j:j + 1],
                                    ps_xT2[:, (q * 2 + lt) * 128:(q * 2 + lt + 1) * 128],
                                    op0=ALU.mult, op1=ALU.add)
                            last = e.tensor_mul(yfst[:, j * 256:(j + 1) * 256], yft[:, 0:256], z_c[:, j * 256:(j + 1) * 256])
                        return last
                    ve_yf = p.op("vector", _yfev, VE)
                    p.wait("tensor", VE, ve_yf)
                p.wait("scalar", VE, ve_yf)
                am_sq = None
                for q in range(8):
                    am_sq = p.op("scalar", lambda e, q=q: e.activation(
                        z_c[:, q * 1024:(q + 1) * 1024], yfst[:, q * 1024:(q + 1) * 1024], AF.Square), AC)
                am_sq_prev = am_sq
                p.wait("tensor", AC, am_sq)
                for j in range(32):
                    pm_ss = p.op("tensor", lambda e, j=j: e.matmul(
                        ps_yA[:, 128:384], ones_sb[:], z_c[:, j * 256:(j + 1) * 256],
                        start=(j == 0), stop=(j == 31)), PE if j == 31 else None)
                p.wait("vector", PE, pm_ss)
                vm_ss = p.op("vector", lambda e, c=c: e.tensor_copy(
                    ss_sb[:, c * 256:(c + 1) * 256], ps_yA[:, 128:384]), VE)
                p.wait("tensor", VE, vm_ss)
                prev_ve_yf = ve_yf
                ve_yf = p.op("vector", lambda e: e.tensor_copy(flsh_b[:, 0:256], yfst[:, 0:32 * 256:32]), VE)
                p.wait("sync", VE, ve_yf)
                for g in range(4):
                    st_yf = p.dma("sync", yf_dram.ap()[g * 8:(g + 1) * 8, :, c * 256:(c + 1) * 256]
                                  .rearrange("j q c -> q j c"),
                                  yfst[:, g * 8 * 256:(g + 1) * 8 * 256].rearrange("q (j c) -> q j c", j=8), ST)
                if dbg_out("y_t_c%d" % c, (2, 128, 4096), BF16) is not None:
                    p.wait("sync", AC, AC.n)
                    p.dma("sync", dbg["y_t_c%d" % c].ap()[0], y_t[0][:], ST)
                    p.dma("sync", dbg["y_t_c%d" % c].ap()[1], y_t[1][:], ST)
            ve_ss_done = vm_ss
        if dbg_out("yf", (32, 128, TT), BF16) is not None:
            p.wait("sync", ST, st_yf)
            p.dma("sync", dbg["yf"].ap(), yf_dram.ap(), ST)
        barrier()
        a_p5.close()
        a_p4o.close()

        a_out = ExitStack()
        acc_bf = sbuf_in(a_out, "acc_bf", [128, DC * TT], BF16)
        a_p6 = ExitStack()
        rstd_bc = sbuf_in(a_p6, "rstd_bc", [128, TT], F32)
        yf_td = sbuf_in(a_p6, "yf_td", [128, 32 * 512], BF16)
        yfn_b = sbuf_in(a_p6, "yfn_b", [128, 2 * 512], BF16, align=2048)
        wo_buf = sbuf_in(a_p6, "wo_buf", [128, 2 * 1024], BF16, align=2048)

        ve_v = p.op("vector", lambda e: e.tensor_scalar(rstd_bc[:, :], ss_sb[:, :], 1.0 / 4096.0, None, op0=ALU.mult), VE)
        p.wait("scalar", VE, ve_v)
        p.op("scalar", lambda e: e.activation(ss_sb[:, :], rstd_bc[:, :], AF.Ln, bias=eps_sb[:, 0:1]))
        am_r = p.op("scalar", lambda e: e.activation(rstd_bc[:, :], ss_sb[:, :], AF.Exp, scale=-0.5), AC)

        with nc.psum_tensor("ps_out", [128, 8 * 512], F32) as ps_out:
            ac_out_evac = 0
            pe_wo = [0, 0]
            pe_last = 0
            p.wait("sync", GP, gp_wout)
            for td in range(2):
                if pe_last:
                    p.wait("sync", PE, pe_last)
                ldy = 0
                for g in range(8):
                    ldy = p.dma("sync", yf_td[:, g * 4 * 512:(g + 1) * 4 * 512].rearrange("q (j c) -> q j c", j=4),
                                yf_dram.ap()[g * 4:(g + 1) * 4, :, td * 512:(td + 1) * 512]
                                .rearrange("j q c -> q j c"), LD)
                for dh in range(2):
                    for I in range(32):
                        par = I % 2
                        if pe_wo[par]:
                            p.wait("sync", PE, pe_wo[par])
                            p.wait("vector", PE, pe_wo[par])
                        wld2 = p.dma("sync", wo_buf[:, par * 1024:(par + 1) * 1024],
                                     w_out_bf.ap()[I * 128:(I + 1) * 128, dh * 1024:(dh + 1) * 1024], LD)
                        p.wait("vector", LD, ldy if I == 0 else 0)
                        if td == 0 and dh == 0 and I == 0:
                            p.wait("vector", AC, am_r)

                        vm_n = p.op("vector", lambda e, I=I, td=td, par=par: e.scalar_tensor_tensor(
                            yfn_b[:, par * 512:(par + 1) * 512], yf_td[:, I * 512:(I + 1) * 512], nw_sb[:, I:I + 1],
                            rstd_bc[:, td * 512:(td + 1) * 512], op0=ALU.mult, op1=ALU.mult), VE)
                        p.wait("tensor", VE, vm_n)
                        p.wait("tensor", LD, wld2)
                        if I == 0 and ac_out_evac:
                            p.wait("tensor", AC, ac_out_evac)
                        for d in range(8):
                            pm = p.op("tensor", lambda e, d=d, par=par, I=I: e.matmul(
                                ps_out[:, d * 512:(d + 1) * 512],
                                wo_buf[:, par * 1024 + d * 128: par * 1024 + (d + 1) * 128],
                                yfn_b[:, par * 512:(par + 1) * 512], start=(I == 0), stop=(I == 31)),
                                PE if d == 7 else None)
                        pe_wo[par] = pm
                    pe_last = pe_wo[1]
                    p.wait("scalar", PE, pe_last)

                    def _eout(e, dh=dh, td=td):
                        for d in range(8):
                            last = e.activation(
                                acc_bf[:, (dh * 8 + d) * TT + td * 512:(dh * 8 + d) * TT + (td + 1) * 512],
                                ps_out[:, d * 512:(d + 1) * 512], AF.Identity)
                        return last
                    ac_out_evac = p.op("scalar", _eout, AC)
        if dbg_out("w_out_bf", (4096, H), BF16) is not None:
            p.wait("sync", GP, gp_wout)
            p.dma("sync", dbg["w_out_bf"].ap(), w_out_bf.ap(), ST)
        if dbg_out("rstd_bc", (128, TT), F32) is not None:
            p.wait("sync", AC, AC.n)
            p.dma("sync", dbg["rstd_bc"].ap(), rstd_bc[:], ST)
        if dbg_out("acc_bf", (128, DC * TT), BF16) is not None:
            p.wait("sync", AC, ac_out_evac)
            p.dma("sync", dbg["acc_bf"].ap(), acc_bf[:], ST)
        if dbg_out("acc_bf2", (128, DC * TT), BF16) is not None:
            p.wait("sync", AC, ac_out_evac)
            p.wait("sync", ST, ST.n)
            p.dma("sync", dbg["acc_bf2"].ap(), acc_bf[:], ST)
        barrier()
        a_p6.close()

        a_p7 = ExitStack()
        h_all = sbuf_in(a_p7, "h_all", [128, IC * TT], BF16)
        wgu_buf = sbuf_in(a_p7, "wgu_buf", [128, 2 * 2 * 2048], BF16)
        g_sb = sbuf_in(a_p7, "g_sb", [128, 2 * TT], BF16)
        with nc.psum_tensor("ps_g0", [128, TT], F32) as ps_g0, \
             nc.psum_tensor("ps_u0", [128, TT], F32) as ps_u0, \
             nc.psum_tensor("ps_g1", [128, TT], F32) as ps_g1, \
             nc.psum_tensor("ps_u1", [128, TT], F32) as ps_u1:
            ps_gu = [(ps_g0, ps_u0), (ps_g1, ps_u1)]
            pe_gu = [0, 0]
            ve_h = [0, 0]
            for i in range(IC):
                par = i % 2
                if pe_gu[par]:
                    p.wait("sync", PE, pe_gu[par])
                wldgu = 0
                for wi, wsrc in enumerate((wg_dram, wu_dram)):
                    for hh in range(2):
                        wldgu = p.dma(
                            "sync",
                            wgu_buf[:, par * 4096 + wi * 2048 + hh * 1024: par * 4096 + wi * 2048 + (hh + 1) * 1024]
                            .rearrange("q (k c) -> q k c", k=8),
                            wsrc.ap()[hh * 1024:(hh + 1) * 1024, i * 128:(i + 1) * 128]
                            .rearrange("(k q) c -> q k c", k=8), LD)
                p.wait("tensor", LD, wldgu)
                if i == 0:
                    p.wait("tensor", CC, cc_marks["u"])
                if ve_h[par]:
                    p.wait("tensor", VE, ve_h[par])
                psg, psu = ps_gu[par]
                for kk in range(HB):
                    for (a, b) in ((3, 515), (515, 1027)):
                        p.op("tensor", lambda e, par=par, kk=kk, a=a, b=b, psg=psg: e.matmul(
                            psg[:, a - 3: b - 3],
                            wgu_buf[:, par * 4096 + kk * 128: par * 4096 + (kk + 1) * 128],
                            xn[:, kk * T + a: kk * T + b], start=(kk == 0), stop=(kk == HB - 1)))
                        pm = p.op("tensor", lambda e, par=par, kk=kk, a=a, b=b, psu=psu: e.matmul(
                            psu[:, a - 3: b - 3],
                            wgu_buf[:, par * 4096 + 2048 + kk * 128: par * 4096 + 2048 + (kk + 1) * 128],
                            xn[:, kk * T + a: kk * T + b], start=(kk == 0), stop=(kk == HB - 1)),
                            PE if (kk == HB - 1 and b == 1027) else None)
                pe_gu[par] = pm
                p.wait("scalar", PE, pm)
                if ve_h[par]:
                    p.wait("scalar", VE, ve_h[par])
                am_g = p.op("scalar", lambda e, psg=psg, par=par: e.activation(
                    g_sb[:, par * TT:(par + 1) * TT], psg[:, :], AF.Silu), AC)
                p.wait("vector", AC, am_g)
                ve_h[par] = p.op("vector", lambda e, i=i, psu=psu, par=par: e.tensor_mul(
                    h_all[:, i * TT:(i + 1) * TT], g_sb[:, par * TT:(par + 1) * TT], psu[:, :]), VE)
        barrier()

        wd_buf = sbuf_in(a_p7, "wd_buf", [128, 2 * 1024], BF16)
        x_res = sbuf_in(a_p7, "x_res", [128, 2 * 512], F32)
        o_stage = sbuf_in(a_p7, "o_stage", [128, 2 * 512], F32)
        o_stage2 = sbuf_in(a_p7, "o_stage2", [128, 2 * 512], F32)
        with nc.psum_tensor("ps_d", [128, 8 * 512], F32) as ps_d:
            ve_fin = 0
            st_out = [0, 0]
            for th in range(2):
                for oh in range(2):
                    pe_d = [0, 0]
                    for i in range(IC):
                        par = i % 2
                        if pe_d[par]:
                            p.wait("sync", PE, pe_d[par])
                        wldd = p.dma("sync", wd_buf[:, par * 1024:(par + 1) * 1024],
                                     wd_dram.ap()[i * 128:(i + 1) * 128, oh * 1024:(oh + 1) * 1024], LD)
                        p.wait("tensor", LD, wldd)
                        if th == 0 and oh == 0 and i == 0:
                            p.wait("tensor", CC, cc_marks["d"])
                        if i == 0 and ve_fin:
                            p.wait("tensor", VE, ve_fin)
                        for d in range(8):
                            pm = p.op("tensor", lambda e, d=d, par=par, i=i, th=th: e.matmul(
                                ps_d[:, d * 512:(d + 1) * 512],
                                wd_buf[:, par * 1024 + d * 128: par * 1024 + (d + 1) * 128],
                                h_all[:, i * TT + th * 512: i * TT + (th + 1) * 512],
                                start=(i == 0), stop=(i == IC - 1)), PE if d == 7 else None)
                        pe_d[par] = pm
                    p.wait("vector", PE, pe_d[(IC - 1) % 2])
                    for d in range(8):
                        dch = oh * 8 + d
                        sp = d % 2
                        ldx2 = p.dma("sync", x_res[:, sp * 512:(sp + 1) * 512],
                                     x_in.ap()[dch * 128:(dch + 1) * 128, 3 + th * 512: 3 + (th + 1) * 512], LD)
                        p.wait("vector", LD, ldx2)
                        if st_out[sp]:
                            p.wait("vector", ST, st_out[sp])

                        def _fin(e, d=d, dch=dch, th=th, sp=sp):
                            e.tensor_add(o_stage[:, sp * 512:(sp + 1) * 512], ps_d[:, d * 512:(d + 1) * 512],
                                         acc_bf[:, dch * TT + th * 512: dch * TT + (th + 1) * 512])
                            return e.tensor_add(o_stage2[:, sp * 512:(sp + 1) * 512],
                                                o_stage[:, sp * 512:(sp + 1) * 512], x_res[:, sp * 512:(sp + 1) * 512])
                        p.op("vector", _fin, VE)
                        ve_fin = p.op("vector", lambda e, sp=sp: e.tensor_copy(
                            flsh_f[:, 0:32], o_stage2[:, sp * 512:(sp + 1) * 512:16]), VE)
                        p.wait("sync", VE, ve_fin)
                        st_out[sp] = p.dma("sync", out_t.ap()[dch * 128:(dch + 1) * 128, th * 512:(th + 1) * 512],
                                           o_stage2[:, sp * 512:(sp + 1) * 512], ST)
            p.wait("sync", ST, ST.n)
        a_p7.close()
        a_out.close()

        snaps.append({e: len(p.ev[e]) for e in Prog.ENGS})
        if PHASE_LIMIT is not None and PHASE_LIMIT <= len(snaps):
            cut = snaps[PHASE_LIMIT - 1]
            for e in Prog.ENGS:
                p.ev[e] = p.ev[e][:cut[e]]
        block = ctx.enter_context(nc.Block())
        for eng in Prog.ENGS:
            evs = p.ev[eng]
            if eng == "tensor":
                def _run_tensor(e, evs=evs):
                    flow = PEFlow(e, PEQ.h)
                    for f in evs:
                        f(flow)
                getattr(block, eng)(_run_tensor)
            else:
                getattr(block, eng)(lambda e, evs=evs: [f(e) for f in evs])

    return nc, dbg


def host_inputs(inputs):
    x = np.asarray(inputs["L0_attn_hidden_states"])
    w_in = np.asarray(inputs["L0_attn_W_in"])
    w_out = np.asarray(inputs["L0_attn_W_out"])
    conv_w = np.asarray(inputs["L0_attn_conv_w"])
    conv_b = np.asarray(inputs["L0_attn_conv_b"])
    dt_bias = np.asarray(inputs["L0_attn_dt_bias"])
    a_log = np.asarray(inputs["L0_attn_A_log"])
    dvec = np.asarray(inputs["L0_attn_D"])
    norm_w = np.asarray(inputs["L0_attn_norm_w"])
    lut = np.asarray(inputs["L0_ffn_lut"])
    gidx = np.asarray(inputs["L0_ffn_gate_idx"])
    uidx = np.asarray(inputs["L0_ffn_up_idx"])
    didx = np.asarray(inputs["L0_ffn_down_idx"])

    w_inT = np.zeros((H, PROJ), np.float32)
    w_inT[:, :8512] = w_in.T
    w_inT = np.ascontiguousarray(w_inT)
    w_outT = np.ascontiguousarray(w_out.T)
    convw_r = np.ascontiguousarray(conv_w.reshape(34, 128, 4).transpose(1, 0, 2).reshape(128, 136))
    convb_r = np.ascontiguousarray(conv_b.reshape(34, 128).T)
    nw_r = np.ascontiguousarray(norm_w.reshape(32, 128).T)
    gidxT = np.ascontiguousarray(gidx.T)
    uidxT = np.ascontiguousarray(uidx.T)
    didxT = np.ascontiguousarray(didx.T)

    in_maps = []
    for core in range(NCORES):
        b, half = core // 2, core % 2
        start = half * TT
        if half == 0:
            halo = np.zeros((3, H), np.float32)
        else:
            halo = x[b, start - 3: start]
        xc = np.ascontiguousarray(np.concatenate([halo, x[b, start:start + TT]], axis=0).T)
        selv = np.zeros((1, 8), np.float32)
        if half == 1:
            selv[0, core - 1] = 1.0
        in_maps.append({
            "x_core": xc,
            "w_inT": w_inT,
            "w_outT": w_outT,
            "convw": convw_r,
            "convb": convb_r,
            "dt_bias": np.ascontiguousarray(dt_bias.reshape(NH, 1)),
            "A_log": np.ascontiguousarray(a_log.reshape(NH, 1)),
            "D": np.ascontiguousarray(dvec.reshape(NH, 1)),
            "norm_w": nw_r,
            "lut": lut,
            "gidxT": np.ascontiguousarray(gidxT[core * 256:(core + 1) * 256]),
            "uidxT": np.ascontiguousarray(uidxT[core * 256:(core + 1) * 256]),
            "didxT": np.ascontiguousarray(didxT[core * 688:(core + 1) * 688]),
            "sel": selv,
        })
    return in_maps


def kernel(**inputs):
    in_maps = host_inputs(inputs)
    nc, dbg = build_program()
    res = run_bass_kernel_spmd(nc, in_maps, core_ids=list(range(NCORES)))
    out = np.empty((4, 2048, H), np.float32)
    for core in range(NCORES):
        b, half = core // 2, core % 2
        out[b, half * TT:(half + 1) * TT, :] = np.asarray(res.results[core]["out_core"]).T
    kernel._last_results = res.results
    return out


# version 15 (final)
# speedup vs baseline: 1.0233x; 1.0224x over previous
import numpy as np
import ml_dtypes
import concourse.bass as bass
import concourse.mybir as mybir
from concourse.bass_utils import run_bass_kernel_spmd
from contextlib import ExitStack

F32 = mybir.dt.float32
BF16 = mybir.dt.bfloat16
I32 = mybir.dt.int32
AF = mybir.ActivationFunctionType
ALU = mybir.AluOpType
BF = ml_dtypes.bfloat16

NCORES = 8
H = 2048
HB = 16
T = 1027
TT = 1024
OC = 67
PROJ = 8576
NH = 64
PD = 64
IC = 43
INTER = 5504
DC = 16

DEBUG = []
PHASE_LIMIT = None
DEV_SYNC = False
SKIP_CC = False
SKIP_BUILD = False


class Sem:
    def __init__(self, handle, dma=False):
        self.h = handle
        self.n = 0
        self.dma = dma

    def bump(self, k):
        self.n += k
        return self.n


class _MMShim:

    def __init__(self, flow):
        self.flow = flow

    def then_inc(self, sem, k):
        self.flow.e.wait_ge(self.flow.s, self.flow.n)
        return self.flow.e.nop().then_inc(sem, k)


class PEFlow:

    LAG = 1

    def __init__(self, e, sem_handle):
        self.e = e
        self.s = sem_handle
        self.n = 0

    def _flow(self, ins):
        ins.then_inc(self.s, 1)
        self.n += 1
        if self.n > self.LAG:
            self.e.wait_ge(self.s, self.n - self.LAG)
        return _MMShim(self)

    def matmul(self, *a, **k):
        return self._flow(self.e.matmul(*a, **k))

    def transpose(self, *a, **k):
        return self._flow(self.e.transpose(*a, **k))

    def __getattr__(self, name):
        return getattr(self.e, name)


class Prog:
    ENGS = ("sync", "tensor", "vector", "scalar", "gpsimd")

    def __init__(self, nc):
        self.nc = nc
        self.ev = {e: [] for e in self.ENGS}
        self.used = set()

    def do(self, eng, fn):
        self.used.add(eng)
        self.ev[eng].append(fn)

    def wait(self, eng, sem, val):
        if val and val > 0:
            if sem.dma:
                val = sem.n
            self.do(eng, lambda e, s=sem.h, v=val: e.wait_ge(s, v))

    def dma(self, eng, out, in_, sem, **kw):
        tgt = sem.bump(16)
        self.do(eng, lambda e, o=out, i=in_, s=sem.h, kw=kw: e.dma_start(out=o, in_=i, **kw).then_inc(s, 16))
        return tgt

    def op(self, eng, fn, sem=None):
        if sem is None:
            self.do(eng, fn)
            return None
        tgt = sem.bump(1)
        self.do(eng, lambda e, f=fn, s=sem.h: f(e).then_inc(s, 1))
        return tgt


def build_program():
    nc = bass.Bass("TRN2", target_bir_lowering=False, debug=False, detect_race_conditions=False)
    p = Prog(nc)
    dram = {}

    def din(name, shape, dt):
        dram[name] = nc.dram_tensor(name, list(shape), dt, kind="ExternalInput")
        return dram[name]

    def dint(name, shape, dt, **kw):
        dram[name] = nc.dram_tensor(name, list(shape), dt, kind="Internal", **kw)
        return dram[name]

    x_in = din("x_core", (H, T), F32)
    w_inT = din("w_inT", (H, PROJ), F32)
    w_outT = din("w_outT", (4096, H), F32)
    convw = din("convw", (128, 34 * 4), F32)
    convb = din("convb", (128, 34), F32)
    dtb = din("dt_bias", (NH, 1), F32)
    alog = din("A_log", (NH, 1), F32)
    dvec = din("D", (NH, 1), F32)
    nw = din("norm_w", (128, 32), F32)
    lut = din("lut", (16,), F32)
    gidx = din("gidxT", (256, INTER), I32)
    uidx = din("uidxT", (256, INTER), I32)
    didx = din("didxT", (688, H), I32)
    sel = din("sel", (1, 8), F32)

    out_t = nc.dram_tensor("out_core", [H, TT], F32, kind="ExternalOutput")

    w_in_bf = dint("w_in_bf", (H, PROJ), BF16)
    w_out_bf = dint("w_out_bf", (4096, H), BF16)
    z_dram = dint("z_dram", (32, 128, TT), BF16)
    x4_dram = dint("x4_dram", (32, 128, TT), BF16)
    yf_dram = dint("yf_dram", (32, 128, TT), BF16)
    acs_dram = dint("acs_dram", (NH, TT), F32)
    expa_dram = dint("expa_dram", (NH, TT), BF16)
    aend_dram = dint("aend_dram", (1, 4 * NH), F32)
    cc_in_g = dint("cc_in_g", (256, INTER), BF16)
    cc_in_u = dint("cc_in_u", (256, INTER), BF16)
    cc_in_d = dint("cc_in_d", (688, H), BF16)
    wg_dram = dint("wg_dram", (H, INTER), BF16)
    wu_dram = dint("wu_dram", (H, INTER), BF16)
    wd_dram = dint("wd_dram", (INTER, H), BF16)
    cc_in_s = dint("cc_in_s", (128, NH * PD), BF16)
    cc_out_s = dint("cc_out_s", (128 * NCORES, NH * PD), BF16)

    ident_bf = nc.inline_tensor(np.eye(128, dtype=BF), name="ident_bf")
    ident_f = nc.inline_tensor(np.eye(128, dtype=np.float32), name="ident_f")
    ones_sq_bf = nc.inline_tensor(np.ones((128, 128), BF), name="ones_sq_bf")
    _m0 = (np.arange(256)[None, :] >= np.arange(128)[:, None]).astype(BF)
    mask0 = nc.inline_tensor(_m0, name="mask0")
    eps_c = nc.inline_tensor(np.full((128, 1), 1e-5, np.float32), name="eps_c")

    dbg = {}
    if "wg_dram" in DEBUG:
        dbg["wd_dram"] = nc.dram_tensor("dbg_wd_dram", [INTER, H], BF16, kind="ExternalOutput")

    def dbg_out(name, shape, dt):
        if name in DEBUG:
            dbg[name] = nc.dram_tensor("dbg_" + name, list(shape), dt, kind="ExternalOutput")
            return dbg[name]
        return None

    ctx = ExitStack()
    with ctx:
        def sem(name, dma=False):
            return Sem(ctx.enter_context(nc.semaphore(name)), dma=dma)

        LD = sem("LD", dma=True)
        LDS = sem("LDS", dma=True)
        STS = sem("STS", dma=True)
        PEQ = sem("PEQ")
        PE = sem("PE")
        VE = sem("VE")
        AC = sem("AC")
        GP = sem("GP", dma=True)
        CC = sem("CC", dma=True)
        ST = sem("ST", dma=True)
        STB = sem("STB", dma=True)

        def sbuf_in(stack, name, shape, dt, align=32):
            return stack.enter_context(nc.sbuf_tensor(name, list(shape), dt, align_bytes=align))

        def sbuf(name, shape, dt):
            return sbuf_in(ctx, name, shape, dt)

        snaps = []

        def barrier():
            if DEV_SYNC:
                p.wait("gpsimd", CC, CC.n)
                p.wait("gpsimd", GP, GP.n)
            mpe = p.op("tensor", lambda e: e.nop(), PE)
            mve = p.op("vector", lambda e: e.nop(), VE)
            mac = p.op("scalar", lambda e: e.nop(), AC)
            for eng in ("sync", "tensor", "vector", "scalar", "gpsimd"):
                p.wait(eng, PE, mpe)
                p.wait(eng, VE, mve)
                p.wait(eng, AC, mac)
                p.wait(eng, LD, LD.n)
                p.wait(eng, ST, ST.n)
                p.wait(eng, LDS, LDS.n)
                p.wait(eng, STS, STS.n)
                p.wait(eng, STB, STB.n)
                if DEV_SYNC:
                    p.wait(eng, CC, CC.n)
            snaps.append({e: len(p.ev[e]) for e in Prog.ENGS})

        xn = sbuf("xn", [128, HB * T], BF16)
        b_f = sbuf("b_f", [128, TT], BF16)
        c_f = sbuf("c_f", [128, TT], BF16)
        dt_t = sbuf("dt_t", [128, 8 * NH], F32)
        acs_t = sbuf("acs_t", [128, 8 * NH], F32)
        decay_t = sbuf("decay_t", [128, 8 * NH], F32)
        wbc = sbuf("wbc", [128, 4 * NH], F32)
        dbc_f = sbuf("dbc_f", [128, 32], F32)
        selbc = sbuf("selbc", [128, 8], F32)
        nw_sb = sbuf("nw_sb", [128, 32], F32)
        lut_sb = sbuf("lut_sb", [128, 16], F32)
        convw_sb = sbuf("convw_sb", [128, 34 * 4], F32)
        convb_sb = sbuf("convb_sb", [128, 34], F32)
        dtb_sb = sbuf("dtb_sb", [NH, 1], F32)
        alog_sb = sbuf("alog_sb", [NH, 1], F32)
        nexpal_sb = sbuf("nexpal_sb", [NH, 1], F32)
        d_sb = sbuf("d_sb", [NH, 1], F32)
        idb_sb = sbuf("idb_sb", [128, 128], BF16)
        idf_sb = sbuf("idf_sb", [128, 128], F32)
        ones_sb = sbuf("ones_sb", [128, 128], BF16)
        mask0_sb = sbuf("mask0_sb", [128, 256], BF16)
        eps_sb = sbuf("eps_sb", [128, 1], F32)
        ss_sb = sbuf("ss_sb", [128, TT], F32)
        flsh_b = sbuf("flsh_b", [128, 256], BF16)
        flsh_f = sbuf("flsh_f", [128, 64], F32)

        for dst, src in [
            (idb_sb, ident_bf), (idf_sb, ident_f), (ones_sb, ones_sq_bf), (mask0_sb, mask0),
        ]:
            p.dma("sync", dst[:], src.ap(), LD)
        p.dma("sync", convw_sb[:], convw.ap(), LD)
        p.dma("sync", convb_sb[:], convb.ap(), LD)
        p.dma("sync", dtb_sb[:], dtb.ap(), LD)
        p.dma("sync", alog_sb[:], alog.ap(), LD)
        p.dma("sync", d_sb[:], dvec.ap(), LD)
        p.dma("sync", nw_sb[:], nw.ap(), LD)
        p.dma("sync", lut_sb[:], lut.ap()[None, :].broadcast_to([128, 16]), LD)
        p.dma("sync", selbc[:], sel.ap()[0:1, :].broadcast_to([128, 8]), LD)
        for ph in range(2):
            p.dma("sync", dbc_f[ph * 64:(ph + 1) * 64, :],
                  dvec.ap()[ph::2, 0][None, :].broadcast_to([64, 32]), LD, allow_slow_non_contiguous=True)
        ld_small = p.dma("sync", eps_sb[:], eps_c.ap(), LD)


        barrier()
        a_ln = ExitStack()
        x_f = sbuf_in(a_ln, "x_f", [128, HB * T], F32)
        x_bf = sbuf_in(a_ln, "x_bf", [128, 2 * T], BF16)
        x2b = sbuf_in(a_ln, "x2b", [128, 2 * T], BF16)
        rstd_ln = sbuf_in(a_ln, "rstd_ln", [128, T], F32)
        mr_ln = sbuf_in(a_ln, "mr_ln", [128, T], F32)
        vtmp = sbuf_in(a_ln, "vtmp", [128, T], F32)
        xtmp = sbuf_in(a_ln, "xtmp", [128, T], F32)
        ldx = []
        for kk in range(HB):
            ldx.append(p.dma("sync", x_f[:, kk * T:(kk + 1) * T], x_in.ap()[kk * 128:(kk + 1) * 128, :], LD))
        p.wait("gpsimd", LD, ldx[-1])
        gp_win_marks = []
        for j in range(8):
            c0, c1 = j * 9 * 128, min((j + 1) * 9 * 128, PROJ)
            p.dma("gpsimd", w_in_bf.ap()[:, c0:c1], w_inT.ap()[:, c0:c1], GP)
            gp_win_marks.append(GP.n)
        slices = [(0, 512), (512, 1024), (1024, 1027)]
        ve_xn = None
        with nc.psum_tensor("ps_sum", [128, 1536], F32) as ps_sum, \
             nc.psum_tensor("ps_sq", [128, 1536], F32) as ps_sq:
            pe_kk = {}
            ac_kk = {}
            ve_kk = {}
            for kk in range(HB):
                par = kk % 2
                p.wait("vector", LD, ldx[kk])
                if kk >= 2:
                    p.wait("vector", PE, pe_kk[kk - 2])
                ve_kk[kk] = p.op("vector", lambda e, kk=kk, par=par: e.tensor_copy(
                    x_bf[:, par * T:(par + 1) * T], x_f[:, kk * T:(kk + 1) * T]), VE)
                p.wait("scalar", VE, ve_kk[kk])
                ac_kk[kk] = p.op("scalar", lambda e, kk=kk, par=par: e.activation(
                    x2b[:, par * T:(par + 1) * T], x_f[:, kk * T:(kk + 1) * T], AF.Square), AC)
                p.wait("tensor", AC, ac_kk[kk])
                if kk == 0:
                    p.wait("tensor", LD, ld_small)
                for si, (a, b) in enumerate(slices):
                    p.op("tensor", lambda e, kk=kk, par=par, a=a, b=b, si=si: e.matmul(
                        ps_sum[:, si * 512: si * 512 + (b - a)], ones_sb[:],
                        x_bf[:, par * T + a: par * T + b], start=(kk == 0), stop=(kk == HB - 1)))
                    pe_kk[kk] = p.op("tensor", lambda e, kk=kk, par=par, a=a, b=b, si=si: e.matmul(
                        ps_sq[:, si * 512: si * 512 + (b - a)], ones_sb[:],
                        x2b[:, par * T + a: par * T + b], start=(kk == 0), stop=(kk == HB - 1)),
                        PE)
            p.wait("vector", PE, pe_kk[HB - 1])

            def _stats(e):
                for si, (a, b) in enumerate(slices):
                    e.tensor_scalar(mr_ln[:, a:b], ps_sum[:, si * 512: si * 512 + (b - a)], 1.0 / H, None, op0=ALU.mult)
                    last = e.tensor_scalar(vtmp[:, a:b], ps_sq[:, si * 512: si * 512 + (b - a)], 1.0 / H, None, op0=ALU.mult)
                return last
            ve_m = p.op("vector", _stats, VE)
            p.wait("scalar", VE, ve_m)
            am2 = p.op("scalar", lambda e: e.activation(xtmp[:, :], mr_ln[:, :], AF.Square), AC)
            p.wait("vector", AC, am2)
            ve_var = p.op("vector", lambda e: e.tensor_sub(rstd_ln[:, :], vtmp[:, :], xtmp[:, :]), VE)
            p.wait("scalar", VE, ve_var)
            p.op("scalar", lambda e: e.activation(vtmp[:, :], rstd_ln[:, :], AF.Ln, bias=eps_sb[:, 0:1]))
            a_r = p.op("scalar", lambda e: e.activation(rstd_ln[:, :], vtmp[:, :], AF.Exp, scale=-0.5), AC)
            p.wait("vector", AC, a_r)
            ve_mr = p.op("vector", lambda e: e.tensor_mul(xtmp[:, :], mr_ln[:, :], rstd_ln[:, :]), VE)
            for kk in range(HB):
                def _xn(e, kk=kk):
                    e.tensor_mul(vtmp[:, :], x_f[:, kk * T:(kk + 1) * T], rstd_ln[:, :])
                    return e.tensor_sub(xn[:, kk * T:(kk + 1) * T], vtmp[:, :], xtmp[:, :])
                ve_xn = p.op("vector", _xn, VE)
        if dbg_out("xn", (128, HB * T), BF16) is not None:
            p.wait("sync", VE, ve_xn)
            p.dma("sync", dbg["xn"].ap(), xn[:], ST)
        if dbg_out("ln_rstd", (128, T), F32) is not None:
            p.wait("sync", VE, ve_xn)
            p.dma("sync", dbg["ln_rstd"].ap(), rstd_ln[:], ST)
        if dbg_out("ln_mr", (128, T), F32) is not None:
            p.wait("sync", VE, ve_xn)
            p.dma("sync", dbg["ln_mr"].ap(), mr_ln[:], ST)
        barrier()
        a_ln.close()

        a_p23 = ExitStack()
        xbc_pre = sbuf_in(a_p23, "xbc_pre", [128, 34 * T], BF16)
        dt_raw = sbuf_in(a_p23, "dt_raw", [NH, TT], F32)
        a_p2 = ExitStack()
        z_stage = sbuf_in(a_p2, "z_stage", [128, 2 * TT], BF16)
        wog = sbuf_in(a_p2, "wog", [128, 2 * HB * 128], BF16)
        idx_buf = sbuf_in(a_p2, "idx_buf", [128, INTER], I32)
        idxbf = sbuf_in(a_p2, "idxbf", [128, INTER], BF16)
        bacc = sbuf_in(a_p2, "bacc", [128, INTER], BF16)
        btmp = sbuf_in(a_p2, "btmp", [128, INTER], BF16)

        build_list = [
            (gidx, 0, 128, INTER, cc_in_g, 0), (gidx, 128, 128, INTER, cc_in_g, 128),
            (uidx, 0, 128, INTER, cc_in_u, 0), (uidx, 128, 128, INTER, cc_in_u, 128),
            (didx, 0, 128, H, cc_in_d, 0), (didx, 128, 128, H, cc_in_d, 128),
            (didx, 256, 128, H, cc_in_d, 256), (didx, 384, 128, H, cc_in_d, 384),
            (didx, 512, 128, H, cc_in_d, 512), (didx, 640, 48, H, cc_in_d, 640),
        ]
        build_st = []
        _bstate = {"prev_ve": None}

        def emit_build_tile(bi):
            src_, r0, nr, ncol, ccdst, dr0 = build_list[bi]
            if _bstate["prev_ve"] is not None:
                p.wait("sync", VE, _bstate["prev_ve"])
            ldt = p.dma("sync", idx_buf[0:nr, 0:ncol], src_.ap()[r0:r0 + nr, :], LD)
            p.wait("vector", LD, ldt)

            def _build(e, ncol=ncol):
                e.tensor_copy(idxbf[:, 0:ncol], idx_buf[:, 0:ncol])
                e.tensor_scalar(bacc[:, 0:ncol], idxbf[:, 0:ncol], 0.0, lut_sb[:, 0:1], op0=ALU.is_equal, op1=ALU.mult)
                for v in range(1, 16):
                    e.tensor_scalar(btmp[:, 0:ncol], idxbf[:, 0:ncol], float(v), lut_sb[:, v:v + 1], op0=ALU.is_equal, op1=ALU.mult)
                    last = e.tensor_add(bacc[:, 0:ncol], bacc[:, 0:ncol], btmp[:, 0:ncol])
                return last
            p.op("vector", _build, VE)
            prev_ve = p.op("vector", lambda e, ncol=ncol: e.tensor_copy(
                flsh_b[:, 0:ncol // 32], bacc[:, 0:ncol:32]), VE)
            _bstate["prev_ve"] = prev_ve
            p.wait("sync", VE, prev_ve)
            build_st.append(p.dma("sync", ccdst.ap()[dr0:dr0 + nr, :], bacc[0:nr, 0:ncol], STB))

        with nc.psum_tensor("ps_p0", [128, T], F32) as ps_p0, \
             nc.psum_tensor("ps_p1", [128, T], F32) as ps_p1:
            psets = (ps_p0, ps_p1)
            evac_m = [0, 0]
            pe_m = [0, 0]
            zst_m = [0, 0]
            wlds = [0] * OC
            gp_waited = [0]

            def emit_wload(oc):
                s = oc % 2
                tgt = gp_win_marks[min(oc // 9, 7)]
                if tgt > gp_waited[0]:
                    p.wait("scalar", GP, tgt)
                    gp_waited[0] = tgt
                if pe_m[s]:
                    p.wait("scalar", PE, pe_m[s])
                src = w_in_bf.ap()[:, oc * 128:(oc + 1) * 128].rearrange("(k q) c -> q k c", k=HB)
                for hh in range(2):
                    wlds[oc] = p.dma("scalar", wog[:, s * 2048 + hh * 1024: s * 2048 + (hh + 1) * 1024]
                                     .rearrange("q (k c) -> q k c", k=8), src[:, hh * 8:(hh + 1) * 8, :], LDS)
            emit_wload(0)
            emit_wload(1)
            for oc in range(OC):
                if not SKIP_BUILD and oc % 6 == 0 and oc // 6 < len(build_list):
                    emit_build_tile(oc // 6)
                s = oc % 2
                ps = psets[s]
                p.wait("tensor", LDS, wlds[oc])
                if oc == 0:
                    p.wait("tensor", VE, ve_xn)
                if evac_m[s]:
                    p.wait("tensor", AC, evac_m[s])
                osl = [(0, 512, 0), (512, 1024, 512), (1024, 1027, 1024)] if 32 <= oc < 66 \
                    else [(3, 515, 0), (515, 1027, 512)]
                for kk in range(HB):
                    for (a, b, oa) in osl:
                        pe_m[s] = p.op("tensor", lambda e, s=s, kk=kk, a=a, b=b, oa=oa, ps=ps: e.matmul(
                            ps[:, oa:oa + (b - a)],
                            wog[:, s * 2048 + kk * 128: s * 2048 + (kk + 1) * 128],
                            xn[:, kk * T + a: kk * T + b],
                            start=(kk == 0), stop=(kk == HB - 1)),
                            PE if (kk == HB - 1 and b == 1027) else None)
                if oc + 2 < OC:
                    emit_wload(oc + 2)
                p.wait("scalar", PE, pe_m[s])
                if oc < 32:
                    if zst_m[s]:
                        p.wait("scalar", ST, zst_m[s])
                    p.op("scalar", lambda e, ps=ps, s=s: e.activation(
                        z_stage[:, s * TT:(s + 1) * TT], ps[:, 0:TT], AF.Silu))
                    evac_m[s] = p.op("scalar", lambda e, s=s: e.activation(
                        flsh_b[:, 0:64], z_stage[:, s * TT:(s + 1) * TT:16], AF.Identity), AC)
                    p.wait("sync", AC, evac_m[s])
                    zst_m[s] = p.dma("sync", z_dram.ap()[oc, :, :], z_stage[:, s * TT:(s + 1) * TT], ST)
                elif oc < 66:
                    bj = oc - 32
                    evac_m[s] = p.op("scalar", lambda e, ps=ps, bj=bj: e.activation(
                        xbc_pre[:, bj * T: bj * T + T], ps[:, 0:T], AF.Identity), AC)
                else:
                    evac_m[s] = p.op("scalar", lambda e, ps=ps: e.activation(
                        dt_raw[0:64, :], ps[0:64, 0:TT], AF.Identity), AC)
        cc_marks = {"g": 0, "u": 0, "d": 0}
        for key, st_i, cin, cout in (("g", 1, cc_in_g, wg_dram), ("u", 3, cc_in_u, wu_dram), ("d", 9, cc_in_d, wd_dram)) \
                if not SKIP_BUILD else ():
            p.wait("gpsimd", STB, build_st[st_i])
            cc_marks[key] = CC.bump(1)
            if SKIP_CC:
                p.do("gpsimd", lambda e: e.nop().then_inc(CC.h, 1))
            else:
                p.do("gpsimd", lambda e, cin=cin, cout=cout: e.collective_compute(
                    "AllGather", ALU.bypass, replica_groups=[list(range(NCORES))],
                    ins=[cin.ap().opt()], outs=[cout.ap().opt()]).then_inc(CC.h, 1))
        for i in range(2):
            p.dma("gpsimd", w_out_bf.ap()[i * 2048:(i + 1) * 2048, :], w_outT.ap()[i * 2048:(i + 1) * 2048, :], GP)
        gp_wout = GP.n
        if dbg_out("wg_dram", (H, INTER), BF16) is not None:
            p.wait("sync", CC, cc_marks["d"])
            p.dma("sync", dbg["wg_dram"].ap(), wg_dram.ap(), ST)
            p.dma("sync", dbg["wd_dram"].ap(), wd_dram.ap(), ST)
        if dbg_out("z_dram", (32, 128, TT), BF16) is not None:
            p.wait("sync", STS, STS.n)
            p.dma("sync", dbg["z_dram"].ap(), z_dram.ap(), ST)
        if dbg_out("xbc_pre", (128, 34 * T), BF16) is not None:
            p.wait("sync", AC, AC.n)
            p.dma("sync", dbg["xbc_pre"].ap(), xbc_pre[:], ST)
        if dbg_out("dt_raw", (NH, TT), F32) is not None:
            p.wait("sync", AC, AC.n)
            p.dma("sync", dbg["dt_raw"].ap(), dt_raw[:], ST)
        barrier()
        a_p2.close()

        a_p3 = ExitStack()
        x4_stage = sbuf_in(a_p3, "x4_stage", [128, 2 * TT], BF16)
        cacc = sbuf_in(a_p3, "cacc", [128, 2 * TT], F32)
        trow = sbuf_in(a_p3, "trow", [NH, 3 * TT], F32)
        dt_f = sbuf_in(a_p3, "dt_f", [NH, TT], F32)
        acs_f = sbuf_in(a_p3, "acs_f", [NH, TT], F32)
        decay_f = sbuf_in(a_p3, "decay_f", [NH, TT], F32)
        aend_f = sbuf_in(a_p3, "aend_f", [NH, 4], F32)
        aend_e = sbuf_in(a_p3, "aend_e", [NH, 4], F32)
        expa_sb = sbuf_in(a_p3, "expa_sb", [NH, TT], BF16)
        x4st_m = [0, 0]
        ac_conv = None
        for bj in range(34):
            s = bj % 2
            if ac_conv:
                p.wait("vector", AC, ac_conv)

            def _conv(e, bj=bj):
                base = bj * T
                e.tensor_scalar(cacc[:, 0:TT], xbc_pre[:, base: base + TT],
                                convw_sb[:, bj * 4: bj * 4 + 1], None, op0=ALU.mult)
                e.scalar_tensor_tensor(cacc[:, TT:2 * TT], xbc_pre[:, base + 1: base + 1 + TT],
                                       convw_sb[:, bj * 4 + 1: bj * 4 + 2],
                                       cacc[:, 0:TT], op0=ALU.mult, op1=ALU.add)
                e.scalar_tensor_tensor(cacc[:, 0:TT], xbc_pre[:, base + 2: base + 2 + TT],
                                       convw_sb[:, bj * 4 + 2: bj * 4 + 3],
                                       cacc[:, TT:2 * TT], op0=ALU.mult, op1=ALU.add)
                return e.scalar_tensor_tensor(cacc[:, TT:2 * TT], xbc_pre[:, base + 3: base + 3 + TT],
                                              convw_sb[:, bj * 4 + 3: bj * 4 + 4],
                                              cacc[:, 0:TT], op0=ALU.mult, op1=ALU.add)
            ve_conv = p.op("vector", _conv, VE)
            p.wait("scalar", VE, ve_conv)
            if bj < 32:
                if x4st_m[s]:
                    p.wait("scalar", ST, x4st_m[s])
                p.op("scalar", lambda e, bj=bj, s=s: e.activation(
                    x4_stage[:, s * TT:(s + 1) * TT], cacc[:, TT:2 * TT], AF.Silu,
                    bias=convb_sb[:, bj:bj + 1]))
                ac_conv = p.op("scalar", lambda e, s=s: e.activation(
                    flsh_b[:, 0:64], x4_stage[:, s * TT:(s + 1) * TT:16], AF.Identity), AC)
                p.wait("sync", AC, ac_conv)
                x4st_m[s] = p.dma("sync", x4_dram.ap()[bj, :, :], x4_stage[:, s * TT:(s + 1) * TT], ST)
            elif bj == 32:
                ac_conv = p.op("scalar", lambda e: e.activation(
                    b_f[:, :], cacc[:, TT:2 * TT], AF.Silu, bias=convb_sb[:, 32:33]), AC)
            else:
                ac_conv = p.op("scalar", lambda e: e.activation(
                    c_f[:, :], cacc[:, TT:2 * TT], AF.Silu, bias=convb_sb[:, 33:34]), AC)

        ac_a = p.op("scalar", lambda e: e.activation(nexpal_sb[:, :], alog_sb[:, :], AF.Exp), AC)
        p.op("scalar", lambda e: e.activation(trow[:, 0:TT], dt_raw[:, :], AF.Abs, bias=dtb_sb[:, 0:1]))
        p.op("scalar", lambda e: e.activation(trow[:, TT:2 * TT], trow[:, 0:TT], AF.Exp, scale=-1.0))
        p.op("scalar", lambda e: e.activation(trow[:, 0:TT], trow[:, TT:2 * TT], AF.Ln, bias=1.0))
        ac_sp = p.op("scalar", lambda e: e.activation(trow[:, TT:2 * TT], dt_raw[:, :], AF.Relu, bias=dtb_sb[:, 0:1]), AC)
        p.wait("vector", AC, ac_sp)

        def _dt1(e):
            e.tensor_add(trow[:, 2 * TT:3 * TT], trow[:, 0:TT], trow[:, TT:2 * TT])
            e.tensor_scalar_min(dt_f[:, :], trow[:, 2 * TT:3 * TT], 1e10)
            e.tensor_scalar(trow[:, 0:TT], dt_f[:, :], nexpal_sb[:, 0:1], -1.0, op0=ALU.mult, op1=ALU.mult)
            for c in range(4):
                last = e.tensor_tensor_scan(acs_f[:, c * 256:(c + 1) * 256], trow[:, c * 256:(c + 1) * 256],
                                            trow[:, c * 256:(c + 1) * 256], 0.0, op0=ALU.add, op1=ALU.bypass)
            return last
        ve_scan = p.op("vector", _dt1, VE)
        p.wait("vector", VE, ve_scan)

        def _dt2(e):
            for c in range(4):
                e.tensor_copy(aend_f[:, c:c + 1], acs_f[:, c * 256 + 255: c * 256 + 256])
            for c in range(4):
                last = e.tensor_scalar(trow[:, TT + c * 256: TT + (c + 1) * 256], acs_f[:, c * 256:(c + 1) * 256],
                                       aend_f[:, c:c + 1], None, op0=ALU.subtract)
            return last
        ve_dt = p.op("vector", _dt2, VE)
        p.wait("scalar", VE, ve_dt)
        ac_dec = p.op("scalar", lambda e: e.activation(decay_f[:, :], trow[:, TT:2 * TT], AF.Exp, scale=-1.0), AC)
        ac_expa = p.op("scalar", lambda e: e.activation(expa_sb[:, :], acs_f[:, :], AF.Exp), AC)
        ve_fl = p.op("vector", lambda e: e.tensor_copy(flsh_f[0:64, 0:64], acs_f[:, 0:TT - 3:16]), VE)
        p.wait("sync", VE, ve_fl)
        p.dma("sync", acs_dram.ap(), acs_f[:, :], ST)
        ac_fl = p.op("scalar", lambda e: e.activation(flsh_b[0:64, 0:64], expa_sb[:, 0:TT - 3:16], AF.Identity), AC)
        p.wait("sync", AC, ac_fl)
        p.dma("sync", expa_dram.ap(), expa_sb[:, :], ST)
        p.op("scalar", lambda e: e.activation(aend_e[:, :], aend_f[:, :], AF.Exp))
        ac_aee = p.op("scalar", lambda e: e.activation(flsh_f[0:64, 0:4], aend_e[:, :], AF.Identity), AC)
        p.wait("sync", AC, ac_aee)
        st_aend = p.dma("sync", aend_dram.ap().rearrange("o (c h) -> (o h) c", h=NH), aend_e[:, :], ST, allow_slow_non_contiguous=True)
        p.wait("sync", ST, st_aend)
        ld_wbc = p.dma("sync", wbc[:], aend_dram.ap()[0:1, :].broadcast_to([128, 4 * NH]), LD)

        with nc.psum_tensor("ps_tr", [128, 512], F32) as ps_tr:
            p.wait("tensor", AC, ac_dec)
            for src_t, dst_t in ((dt_f, dt_t), (acs_f, acs_t), (decay_f, decay_t)):
                for g in range(2):
                    for q in range(4):
                        sub = g * 4 + q
                        pm = p.op("tensor", lambda e, src_t=src_t, sub=sub, q=q: e.transpose(
                            ps_tr[:, q * 128: q * 128 + 64], src_t[0:64, sub * 128:(sub + 1) * 128], idf_sb[0:64, 0:64]),
                            PE if q == 3 else None)
                    p.wait("scalar", PE, pm)

                    def _etr(e, dst_t=dst_t, g=g):
                        for q in range(4):
                            last = e.activation(dst_t[:, (g * 4 + q) * 64:(g * 4 + q + 1) * 64],
                                                ps_tr[:, q * 128: q * 128 + 64], AF.Identity)
                        return last
                    am = p.op("scalar", _etr, AC)
                    p.wait("tensor", AC, am)

        for name, t_, shp, dt_ in (("dt_f", dt_f, (NH, TT), F32), ("acs_f", acs_f, (NH, TT), F32),
                                   ("decay_f", decay_f, (NH, TT), F32), ("dt_t", dt_t, (128, 8 * NH), F32),
                                   ("acs_t", acs_t, (128, 8 * NH), F32), ("decay_t", decay_t, (128, 8 * NH), F32),
                                   ("wbc", wbc, (128, 4 * NH), F32)):
            if dbg_out(name, shp, dt_) is not None:
                p.wait("sync", VE, VE.n)
                p.wait("sync", AC, AC.n)
                p.wait("sync", LD, LD.n)
                p.dma("sync", dbg[name].ap(), t_[:], ST)
        if dbg_out("bc_f", (2, 128, TT), BF16) is not None:
            p.wait("sync", AC, AC.n)
            p.dma("sync", dbg["bc_f"].ap()[0], b_f[:], ST)
            p.dma("sync", dbg["bc_f"].ap()[1], c_f[:], ST)
        barrier()
        a_p3.close()
        a_p23.close()

        a_p4o = ExitStack()
        x4c = sbuf_in(a_p4o, "x4c", [128, 32 * 256], BF16)
        xs_t0 = sbuf_in(a_p4o, "xs_t0", [128, 4096], BF16)
        xs_t1 = sbuf_in(a_p4o, "xs_t1", [128, 4096], BF16)
        xs_t = (xs_t0, xs_t1)
        b_t0 = sbuf_in(a_p4o, "b_t0", [128, 128], BF16)
        b_t1 = sbuf_in(a_p4o, "b_t1", [128, 128], BF16)
        b_t = (b_t0, b_t1)
        s0_all = sbuf_in(a_p4o, "s0_all", [128, 4 * NH * PD], BF16)

        a_p4 = ExitStack()
        xs2 = sbuf_in(a_p4, "xs2", [128, 4 * PD], BF16)
        states_all = sbuf_in(a_p4, "states_all", [128, 4 * NH * PD], BF16)
        s_loc = sbuf_in(a_p4, "s_loc", [128, 2 * NH * PD], BF16)
        gbuf = sbuf_in(a_p4, "gbuf", [128, 2 * NH * PD], BF16)

        def emit_chunk_transposes(c, tag, ps_ext=None, prev_pe=0, prev_ve=0):
            p.wait("sync", PE, prev_pe)
            p.wait("sync", VE, prev_ve)
            ldm = 0
            for g in range(4):
                ldm = p.dma("sync", x4c[:, g * 8 * 256:(g + 1) * 8 * 256].rearrange("q (j c) -> q j c", j=8),
                            x4_dram.ap()[g * 8:(g + 1) * 8, :, c * 256:(c + 1) * 256].rearrange("j q c -> q j c"), LD)
            from contextlib import nullcontext
            cm = nullcontext(ps_ext) if ps_ext is not None else nc.psum_tensor("ps_xT_%s" % tag, [128, 1024], BF16)
            with cm as ps_xT:
                p.wait("tensor", LD, ldm)
                for g in range(8):
                    for q in range(4):
                        j = g * 4 + q
                        for lt in range(2):
                            pm = p.op("tensor", lambda e, j=j, lt=lt, q=q: e.transpose(
                                ps_xT[:, (q * 2 + lt) * 128:(q * 2 + lt + 1) * 128],
                                x4c[:, j * 256 + lt * 128: j * 256 + (lt + 1) * 128], idb_sb[:, :]),
                                PE if (q == 3 and lt == 1) else None)
                    p.wait("vector", PE, pm)

                    def _evac(e, g=g, c=c):
                        for q in range(4):
                            j = g * 4 + q
                            for lt in range(2):
                                sub = 2 * c + lt
                                last = e.tensor_tensor(
                                    xs_t[lt][:, j * 128:(j + 1) * 128],
                                    ps_xT[:, (q * 2 + lt) * 128:(q * 2 + lt + 1) * 128],
                                    dt_t[:, sub * 64 + 2 * j: sub * 64 + 2 * j + 2][:, :, None].broadcast_to([128, 2, 64]),
                                    op=ALU.mult)
                        return last
                    ve_l = p.op("vector", _evac, VE)
                    p.wait("tensor", VE, ve_l)
                for lt in range(2):
                    pm = p.op("tensor", lambda e, lt=lt, c=c: e.transpose(
                        ps_xT[:, lt * 128:(lt + 1) * 128], b_f[:, c * 256 + lt * 128: c * 256 + (lt + 1) * 128],
                        idb_sb[:, :]), PE if lt == 1 else None)
                p.wait("scalar", PE, pm)

                def _ebt(e):
                    for lt in range(2):
                        last = e.activation(b_t[lt][:, :], ps_xT[:, lt * 128:(lt + 1) * 128], AF.Identity)
                    return last
                am = p.op("scalar", _ebt, AC)
                p.wait("tensor", AC, am)
                pm_tr = p.op("tensor", lambda e: e.nop(), PE)
            return ve_l, am, pm_tr

        with nc.psum_tensor("ps_st0", [128, 256], F32) as ps_st0, \
             nc.psum_tensor("ps_st1", [128, 256], F32) as ps_st1, \
             nc.psum_tensor("ps_st2", [128, 256], F32) as ps_st2, \
             nc.psum_tensor("ps_st3", [128, 256], F32) as ps_st3:
            st_banks = [(ps_st0, 0), (ps_st1, 0), (ps_st2, 0), (ps_st3, 0)]
            prev_pe_tr = 0
            for c in range(4):
                ve_l, am, prev_pe_tr = emit_chunk_transposes(c, "p1c%d" % c, prev_pe=prev_pe_tr)
                p.wait("vector", AC, am)
                last_evac = [0, 0, 0, 0]
                pe_h = {}
                for h in range(NH):
                    slot = h % 4
                    ps, half = st_banks[slot]
                    xpar = h % 2
                    if h >= 2:
                        p.wait("vector", PE, pe_h[h - 2])

                    def _xs2(e, c=c, h=h, xpar=xpar):
                        for lt in range(2):
                            sub = 2 * c + lt
                            last = e.tensor_scalar(xs2[:, (xpar * 2 + lt) * PD:(xpar * 2 + lt + 1) * PD],
                                                   xs_t[lt][:, h * PD:(h + 1) * PD],
                                                   decay_t[:, sub * NH + h: sub * NH + h + 1], None, op0=ALU.mult)
                        return last
                    vm = p.op("vector", _xs2, VE)
                    p.wait("tensor", VE, vm)
                    if last_evac[slot]:
                        p.wait("tensor", AC, last_evac[slot])
                    for lt in range(2):
                        pe_h[h] = p.op("tensor", lambda e, lt=lt, ps=ps, half=half, xpar=xpar: e.matmul(
                            ps[:, half * 256: half * 256 + PD], b_t[lt][:, :],
                            xs2[:, (xpar * 2 + lt) * PD:(xpar * 2 + lt + 1) * PD],
                            start=(lt == 0), stop=(lt == 1)), PE if lt == 1 else None)
                    p.wait("scalar", PE, pe_h[h])
                    last_evac[slot] = p.op("scalar", lambda e, c=c, h=h, ps=ps, half=half: e.activation(
                        states_all[:, (c * NH + h) * PD:(c * NH + h + 1) * PD], ps[:, half * 256: half * 256 + PD],
                        AF.Identity), AC)
                p.wait("vector", AC, last_evac[(NH - 1) % 4])

        p.wait("vector", AC, AC.n)

        def _rec_local(e):
            e.memset(s_loc[:, 0:NH * PD], 0.0)
            for c in range(4):
                src_h = (c % 2) * NH * PD
                dst_h = ((c + 1) % 2) * NH * PD
                for h in range(NH):
                    last = e.scalar_tensor_tensor(
                        s_loc[:, dst_h + h * PD: dst_h + (h + 1) * PD],
                        s_loc[:, src_h + h * PD: src_h + (h + 1) * PD],
                        wbc[:, c * NH + h: c * NH + h + 1],
                        states_all[:, (c * NH + h) * PD:(c * NH + h + 1) * PD],
                        op0=ALU.mult, op1=ALU.add)
            return last
        ve_rec = p.op("vector", _rec_local, VE)
        ve_rec = p.op("vector", lambda e: e.tensor_copy(flsh_b[:, 0:256], s_loc[:, 0:NH * PD:16]), VE)
        p.wait("sync", VE, ve_rec)
        st_spub = p.dma("sync", cc_in_s.ap(), s_loc[:, 0:NH * PD], ST)
        p.wait("gpsimd", ST, st_spub)
        cc_s = CC.bump(1)
        if SKIP_CC:
            p.do("gpsimd", lambda e: e.nop().then_inc(CC.h, 1))
        else:
            p.do("gpsimd", lambda e: e.collective_compute(
                "AllGather", ALU.bypass, replica_groups=[list(range(NCORES))],
                ins=[cc_in_s.ap().opt()], outs=[cc_out_s.ap().opt()]).then_inc(CC.h, 1))
        p.wait("sync", CC, cc_s)
        for r in range(NCORES):
            gld = p.dma("sync", gbuf[:, (r % 2) * NH * PD:((r % 2) + 1) * NH * PD],
                        cc_out_s.ap()[r * 128:(r + 1) * 128, :], LD)
            p.wait("vector", LD, gld)
            if r == 0:
                vm = p.op("vector", lambda e: e.tensor_scalar(
                    s0_all[:, 0:NH * PD], gbuf[:, 0:NH * PD], selbc[:, 0:1], None, op0=ALU.mult), VE)
            else:
                def _acc(e, r=r):
                    e.tensor_scalar(s_loc[:, NH * PD:2 * NH * PD], gbuf[:, (r % 2) * NH * PD:((r % 2) + 1) * NH * PD],
                                    selbc[:, r:r + 1], None, op0=ALU.mult)
                    return e.tensor_add(s0_all[:, 0:NH * PD], s0_all[:, 0:NH * PD], s_loc[:, NH * PD:2 * NH * PD])
                vm = p.op("vector", _acc, VE)
            p.wait("sync", VE, vm)

        def _rec_corr(e):
            for c in range(3):
                for h in range(NH):
                    last = e.scalar_tensor_tensor(
                        s0_all[:, ((c + 1) * NH + h) * PD:((c + 1) * NH + h + 1) * PD],
                        s0_all[:, (c * NH + h) * PD:(c * NH + h + 1) * PD],
                        wbc[:, c * NH + h: c * NH + h + 1],
                        states_all[:, (c * NH + h) * PD:(c * NH + h + 1) * PD],
                        op0=ALU.mult, op1=ALU.add)
            return last
        ve_s0 = p.op("vector", _rec_corr, VE)
        if dbg_out("s0_all", (128, 4 * NH * PD), BF16) is not None:
            p.wait("sync", VE, ve_s0)
            p.dma("sync", dbg["s0_all"].ap(), s0_all[:], ST)
        if dbg_out("states_all", (128, 4 * NH * PD), BF16) is not None:
            p.wait("sync", VE, ve_s0)
            p.dma("sync", dbg["states_all"].ap(), states_all[:], ST)
        barrier()
        a_p4.close()

        a_p5 = ExitStack()
        y_t0 = sbuf_in(a_p5, "y_t0", [128, 4096], BF16)
        y_t1 = sbuf_in(a_p5, "y_t1", [128, 4096], BF16)
        y_t = (y_t0, y_t1)
        cbm = sbuf_in(a_p5, "cbm", [128, 384], BF16)
        m_t = sbuf_in(a_p5, "m_t", [128, 384], BF16)
        dcl = sbuf_in(a_p5, "dcl", [128, 384], F32)
        acsbc = sbuf_in(a_p5, "acsbc", [128, 4096], F32)
        eabc = sbuf_in(a_p5, "eabc", [128, 4096], BF16)
        ce_sb = sbuf_in(a_p5, "ce_sb", [128, 2 * 4096], BF16)
        e_t = sbuf_in(a_p5, "e_t", [128, 384], BF16)
        z_c = sbuf_in(a_p5, "z_c", [128, 32 * 256], BF16)
        yfst = sbuf_in(a_p5, "yfst", [128, 32 * 256], BF16)
        yft = sbuf_in(a_p5, "yft", [128, 256], BF16)

        with nc.psum_tensor("ps_xT2", [128, 1024], BF16) as ps_xT2, \
             nc.psum_tensor("ps_cb", [128, 512], F32) as ps_cb, \
             nc.psum_tensor("ps_yA", [128, 384], F32) as ps_yA, \
             nc.psum_tensor("ps_yB", [128, 384], F32) as ps_yB:
            st_yf = 0
            am_sq_prev = 0
            prev_pe_tr = 0
            prev_ve_yf = 0
            for c in range(4):
                ve_l, am_bt, prev_pe_tr = emit_chunk_transposes(c, "p2c%d" % c, ps_ext=ps_xT2,
                                                               prev_pe=prev_pe_tr, prev_ve=prev_ve_yf)
                if am_sq_prev:
                    p.wait("sync", AC, am_sq_prev)
                ldz = 0
                for g in range(4):
                    ldz = p.dma("sync", z_c[:, g * 8 * 256:(g + 1) * 8 * 256].rearrange("q (j c) -> q j c", j=8),
                                z_dram.ap()[g * 8:(g + 1) * 8, :, c * 256:(c + 1) * 256].rearrange("j q c -> q j c"), LD)
                p.wait("tensor", AC, am_bt)
                p.op("tensor", lambda e, c=c: e.matmul(
                    ps_cb[:, 0:256], b_f[:, c * 256: c * 256 + 128], c_f[:, c * 256:(c + 1) * 256],
                    start=True, stop=True))
                pm_cb = p.op("tensor", lambda e, c=c: e.matmul(
                    ps_cb[:, 256:384], b_f[:, c * 256 + 128: c * 256 + 256], c_f[:, c * 256 + 128:(c + 1) * 256],
                    start=True, stop=True), PE)
                p.wait("vector", PE, pm_cb)

                def _cbm(e):
                    e.tensor_mul(cbm[:, 0:256], ps_cb[:, 0:256], mask0_sb[:, :])
                    return e.tensor_mul(cbm[:, 256:384], ps_cb[:, 256:384], mask0_sb[:, 0:128])
                ve_cbm = p.op("vector", _cbm, VE)
                p.wait("tensor", VE, ve_cbm)

                y_evac = [0, 0]
                pm_y_group = {}
                vm_d_last = 0
                ve_ce_last = 0
                for g in range(4):
                    par = g % 2
                    if g >= 2:
                        p.wait("sync", PE, pm_y_group[g - 2])
                    if vm_d_last:
                        p.wait("sync", VE, max(vm_d_last, ve_ce_last))
                    lde = 0
                    for hf in range(2):
                        lde = p.dma("sync",
                                    eabc[:, hf * 2048:(hf + 1) * 2048].rearrange("q (h l) -> q h l", h=8),
                                    expa_dram.ap()[g * 16 + hf * 8: g * 16 + (hf + 1) * 8,
                                                   c * 256:(c + 1) * 256][None].broadcast_to([128, 8, 256]), LD)
                        lde = p.dma("sync",
                                    acsbc[:, hf * 2048:(hf + 1) * 2048].rearrange("q (h l) -> q h l", h=8),
                                    acs_dram.ap()[g * 16 + hf * 8: g * 16 + (hf + 1) * 8,
                                                  c * 256:(c + 1) * 256][None].broadcast_to([128, 8, 256]), LD)
                    p.wait("vector", LD, lde)
                    if g >= 2:
                        p.wait("vector", PE, pm_y_group[g - 2])

                    def _ce(e, c=c, par=par):
                        for hl in range(16):
                            last = e.tensor_mul(ce_sb[:, par * 4096 + hl * 256: par * 4096 + (hl + 1) * 256],
                                                c_f[:, c * 256:(c + 1) * 256],
                                                eabc[:, hl * 256:(hl + 1) * 256])
                        return last
                    ve_ce = p.op("vector", _ce, VE)
                    ve_ce_last = ve_ce
                    pm_y = 0
                    for hl in range(16):
                        h = g * 16 + hl
                        slot = h % 2
                        psY = (ps_yA, ps_yB)[slot]

                        def _dclamp(e, c=c, h=h, hl=hl):
                            s0c = 2 * c
                            e.tensor_scalar(dcl[:, 0:256], acsbc[:, hl * 256:(hl + 1) * 256],
                                            acs_t[:, s0c * NH + h: s0c * NH + h + 1],
                                            0.0, op0=ALU.subtract, op1=ALU.min)
                            return e.tensor_scalar(dcl[:, 256:384], acsbc[:, hl * 256 + 128:(hl + 1) * 256],
                                                   acs_t[:, (s0c + 1) * NH + h: (s0c + 1) * NH + h + 1],
                                                   0.0, op0=ALU.subtract, op1=ALU.min)
                        vm_d = p.op("vector", _dclamp, VE)
                        vm_d_last = vm_d
                        p.wait("scalar", VE, vm_d)
                        am_e = p.op("scalar", lambda e: e.activation(e_t[:, 0:384], dcl[:, 0:384], AF.Exp), AC)
                        p.wait("vector", AC, am_e)
                        vm_m = p.op("vector", lambda e: e.tensor_mul(m_t[:, 0:384], e_t[:, 0:384], cbm[:, 0:384]), VE)
                        p.wait("tensor", VE, vm_m)
                        if y_evac[slot]:
                            p.wait("tensor", AC, y_evac[slot])
                        if hl == 0:
                            p.wait("tensor", VE, ve_ce)

                        def _ymm(e, c=c, h=h, hl=hl, par=par, psY=psY):
                            e.matmul(psY[:, 0:64], m_t[:, 0:128], xs_t[0][:, h * PD:(h + 1) * PD], start=True, stop=False)
                            e.matmul(psY[:, 0:64], ce_sb[:, par * 4096 + hl * 256: par * 4096 + hl * 256 + 128],
                                     s0_all[:, (c * NH + h) * PD:(c * NH + h + 1) * PD], start=False, stop=True)
                            e.matmul(psY[:, 64:128], m_t[:, 128:256], xs_t[0][:, h * PD:(h + 1) * PD], start=True, stop=False)
                            e.matmul(psY[:, 64:128], m_t[:, 256:384], xs_t[1][:, h * PD:(h + 1) * PD], start=False, stop=False)
                            return e.matmul(psY[:, 64:128],
                                            ce_sb[:, par * 4096 + hl * 256 + 128: par * 4096 + (hl + 1) * 256],
                                            s0_all[:, (c * NH + h) * PD:(c * NH + h + 1) * PD], start=False, stop=True)
                        pm_y = p.op("tensor", _ymm, PE)
                        p.wait("scalar", PE, pm_y)

                        def _yev(e, h=h, psY=psY):
                            e.activation(y_t[0][:, h * PD:(h + 1) * PD], psY[:, 0:64], AF.Identity)
                            return e.activation(y_t[1][:, h * PD:(h + 1) * PD], psY[:, 64:128], AF.Identity)
                        y_evac[slot] = p.op("scalar", _yev, AC)
                    pm_y_group[g] = pm_y

                p.wait("tensor", AC, y_evac[0])
                p.wait("tensor", AC, y_evac[1])
                p.wait("tensor", LD, ldz)
                if st_yf:
                    p.wait("tensor", ST, st_yf)
                ve_yf = None
                for g in range(8):
                    for q in range(4):
                        j = g * 4 + q
                        for lt in range(2):
                            pm = p.op("tensor", lambda e, j=j, lt=lt, q=q: e.transpose(
                                ps_xT2[:, (q * 2 + lt) * 128:(q * 2 + lt + 1) * 128],
                                y_t[lt][:, j * 128:(j + 1) * 128], idb_sb[:, :]),
                                PE if (q == 3 and lt == 1) else None)
                    p.wait("vector", PE, pm)

                    def _yfev(e, g=g, c=c):
                        for q in range(4):
                            j = g * 4 + q
                            for lt in range(2):
                                e.scalar_tensor_tensor(
                                    yft[:, lt * 128:(lt + 1) * 128],
                                    x4c[:, j * 256 + lt * 128: j * 256 + (lt + 1) * 128],
                                    dbc_f[:, j:j + 1],
                                    ps_xT2[:, (q * 2 + lt) * 128:(q * 2 + lt + 1) * 128],
                                    op0=ALU.mult, op1=ALU.add)
                            last = e.tensor_mul(yfst[:, j * 256:(j + 1) * 256], yft[:, 0:256], z_c[:, j * 256:(j + 1) * 256])
                        return last
                    ve_yf = p.op("vector", _yfev, VE)
                    p.wait("tensor", VE, ve_yf)
                p.wait("scalar", VE, ve_yf)
                am_sq = None
                for q in range(8):
                    am_sq = p.op("scalar", lambda e, q=q: e.activation(
                        z_c[:, q * 1024:(q + 1) * 1024], yfst[:, q * 1024:(q + 1) * 1024], AF.Square), AC)
                am_sq_prev = am_sq
                p.wait("tensor", AC, am_sq)
                for j in range(32):
                    pm_ss = p.op("tensor", lambda e, j=j: e.matmul(
                        ps_yA[:, 128:384], ones_sb[:], z_c[:, j * 256:(j + 1) * 256],
                        start=(j == 0), stop=(j == 31)), PE if j == 31 else None)
                p.wait("vector", PE, pm_ss)
                vm_ss = p.op("vector", lambda e, c=c: e.tensor_copy(
                    ss_sb[:, c * 256:(c + 1) * 256], ps_yA[:, 128:384]), VE)
                p.wait("tensor", VE, vm_ss)
                prev_ve_yf = ve_yf
                ve_yf = p.op("vector", lambda e: e.tensor_copy(flsh_b[:, 0:256], yfst[:, 0:32 * 256:32]), VE)
                p.wait("sync", VE, ve_yf)
                for g in range(4):
                    st_yf = p.dma("sync", yf_dram.ap()[g * 8:(g + 1) * 8, :, c * 256:(c + 1) * 256]
                                  .rearrange("j q c -> q j c"),
                                  yfst[:, g * 8 * 256:(g + 1) * 8 * 256].rearrange("q (j c) -> q j c", j=8), ST)
                if dbg_out("y_t_c%d" % c, (2, 128, 4096), BF16) is not None:
                    p.wait("sync", AC, AC.n)
                    p.dma("sync", dbg["y_t_c%d" % c].ap()[0], y_t[0][:], ST)
                    p.dma("sync", dbg["y_t_c%d" % c].ap()[1], y_t[1][:], ST)
            ve_ss_done = vm_ss
        if dbg_out("yf", (32, 128, TT), BF16) is not None:
            p.wait("sync", ST, st_yf)
            p.dma("sync", dbg["yf"].ap(), yf_dram.ap(), ST)
        barrier()
        a_p5.close()
        a_p4o.close()

        a_out = ExitStack()
        acc_bf = sbuf_in(a_out, "acc_bf", [128, DC * TT], BF16)
        a_p6 = ExitStack()
        rstd_bc = sbuf_in(a_p6, "rstd_bc", [128, TT], F32)
        yf_td = sbuf_in(a_p6, "yf_td", [128, 32 * 512], BF16)
        yfn_b = sbuf_in(a_p6, "yfn_b", [128, 2 * 512], BF16, align=2048)
        wo_buf = sbuf_in(a_p6, "wo_buf", [128, 2 * 1024], BF16, align=2048)

        ve_v = p.op("vector", lambda e: e.tensor_scalar(rstd_bc[:, :], ss_sb[:, :], 1.0 / 4096.0, None, op0=ALU.mult), VE)
        p.wait("scalar", VE, ve_v)
        p.op("scalar", lambda e: e.activation(ss_sb[:, :], rstd_bc[:, :], AF.Ln, bias=eps_sb[:, 0:1]))
        am_r = p.op("scalar", lambda e: e.activation(rstd_bc[:, :], ss_sb[:, :], AF.Exp, scale=-0.5), AC)

        with nc.psum_tensor("ps_out", [128, 8 * 512], F32) as ps_out:
            ac_out_evac = 0
            pe_wo = [0, 0]
            pe_last = 0
            p.wait("sync", GP, gp_wout)
            for td in range(2):
                if pe_last:
                    p.wait("sync", PE, pe_last)
                ldy = 0
                for g in range(8):
                    ldy = p.dma("sync", yf_td[:, g * 4 * 512:(g + 1) * 4 * 512].rearrange("q (j c) -> q j c", j=4),
                                yf_dram.ap()[g * 4:(g + 1) * 4, :, td * 512:(td + 1) * 512]
                                .rearrange("j q c -> q j c"), LD)
                for dh in range(2):
                    for I in range(32):
                        par = I % 2
                        if pe_wo[par]:
                            p.wait("scalar", PE, pe_wo[par])
                            p.wait("vector", PE, pe_wo[par])
                        wld2 = p.dma("scalar", wo_buf[:, par * 1024:(par + 1) * 1024],
                                     w_out_bf.ap()[I * 128:(I + 1) * 128, dh * 1024:(dh + 1) * 1024], LDS)
                        p.wait("vector", LD, ldy if I == 0 else 0)
                        if td == 0 and dh == 0 and I == 0:
                            p.wait("vector", AC, am_r)

                        vm_n = p.op("vector", lambda e, I=I, td=td, par=par: e.scalar_tensor_tensor(
                            yfn_b[:, par * 512:(par + 1) * 512], yf_td[:, I * 512:(I + 1) * 512], nw_sb[:, I:I + 1],
                            rstd_bc[:, td * 512:(td + 1) * 512], op0=ALU.mult, op1=ALU.mult), VE)
                        p.wait("tensor", VE, vm_n)
                        p.wait("tensor", LDS, wld2)
                        if I == 0 and ac_out_evac:
                            p.wait("tensor", AC, ac_out_evac)
                        for d in range(8):
                            pm = p.op("tensor", lambda e, d=d, par=par, I=I: e.matmul(
                                ps_out[:, d * 512:(d + 1) * 512],
                                wo_buf[:, par * 1024 + d * 128: par * 1024 + (d + 1) * 128],
                                yfn_b[:, par * 512:(par + 1) * 512], start=(I == 0), stop=(I == 31)),
                                PE if d == 7 else None)
                        pe_wo[par] = pm
                    pe_last = pe_wo[1]
                    p.wait("scalar", PE, pe_last)

                    def _eout(e, dh=dh, td=td):
                        for d in range(8):
                            last = e.activation(
                                acc_bf[:, (dh * 8 + d) * TT + td * 512:(dh * 8 + d) * TT + (td + 1) * 512],
                                ps_out[:, d * 512:(d + 1) * 512], AF.Identity)
                        return last
                    ac_out_evac = p.op("scalar", _eout, AC)
        if dbg_out("w_out_bf", (4096, H), BF16) is not None:
            p.wait("sync", GP, gp_wout)
            p.dma("sync", dbg["w_out_bf"].ap(), w_out_bf.ap(), ST)
        if dbg_out("rstd_bc", (128, TT), F32) is not None:
            p.wait("sync", AC, AC.n)
            p.dma("sync", dbg["rstd_bc"].ap(), rstd_bc[:], ST)
        if dbg_out("acc_bf", (128, DC * TT), BF16) is not None:
            p.wait("sync", AC, ac_out_evac)
            p.dma("sync", dbg["acc_bf"].ap(), acc_bf[:], ST)
        if dbg_out("acc_bf2", (128, DC * TT), BF16) is not None:
            p.wait("sync", AC, ac_out_evac)
            p.wait("sync", ST, ST.n)
            p.dma("sync", dbg["acc_bf2"].ap(), acc_bf[:], ST)
        barrier()
        a_p6.close()

        a_p7 = ExitStack()
        h_all = sbuf_in(a_p7, "h_all", [128, IC * TT], BF16)
        wgu_buf = sbuf_in(a_p7, "wgu_buf", [128, 2 * 2 * 2048], BF16)
        g_sb = sbuf_in(a_p7, "g_sb", [128, 2 * TT], BF16)
        with nc.psum_tensor("ps_g0", [128, TT], F32) as ps_g0, \
             nc.psum_tensor("ps_u0", [128, TT], F32) as ps_u0, \
             nc.psum_tensor("ps_g1", [128, TT], F32) as ps_g1, \
             nc.psum_tensor("ps_u1", [128, TT], F32) as ps_u1:
            ps_gu = [(ps_g0, ps_u0), (ps_g1, ps_u1)]
            pe_gu = [0, 0]
            ve_h = [0, 0]
            for i in range(IC):
                par = i % 2
                if pe_gu[par]:
                    p.wait("sync", PE, pe_gu[par])
                wldgu = 0
                for wi, wsrc in enumerate((wg_dram, wu_dram)):
                    for hh in range(2):
                        wldgu = p.dma(
                            "sync",
                            wgu_buf[:, par * 4096 + wi * 2048 + hh * 1024: par * 4096 + wi * 2048 + (hh + 1) * 1024]
                            .rearrange("q (k c) -> q k c", k=8),
                            wsrc.ap()[hh * 1024:(hh + 1) * 1024, i * 128:(i + 1) * 128]
                            .rearrange("(k q) c -> q k c", k=8), LD)
                p.wait("tensor", LD, wldgu)
                if i == 0:
                    p.wait("tensor", CC, cc_marks["u"])
                if ve_h[par]:
                    p.wait("tensor", VE, ve_h[par])
                psg, psu = ps_gu[par]
                for kk in range(HB):
                    for (a, b) in ((3, 515), (515, 1027)):
                        p.op("tensor", lambda e, par=par, kk=kk, a=a, b=b, psg=psg: e.matmul(
                            psg[:, a - 3: b - 3],
                            wgu_buf[:, par * 4096 + kk * 128: par * 4096 + (kk + 1) * 128],
                            xn[:, kk * T + a: kk * T + b], start=(kk == 0), stop=(kk == HB - 1)))
                        pm = p.op("tensor", lambda e, par=par, kk=kk, a=a, b=b, psu=psu: e.matmul(
                            psu[:, a - 3: b - 3],
                            wgu_buf[:, par * 4096 + 2048 + kk * 128: par * 4096 + 2048 + (kk + 1) * 128],
                            xn[:, kk * T + a: kk * T + b], start=(kk == 0), stop=(kk == HB - 1)),
                            PE if (kk == HB - 1 and b == 1027) else None)
                pe_gu[par] = pm
                p.wait("scalar", PE, pm)
                if ve_h[par]:
                    p.wait("scalar", VE, ve_h[par])
                am_g = p.op("scalar", lambda e, psg=psg, par=par: e.activation(
                    g_sb[:, par * TT:(par + 1) * TT], psg[:, :], AF.Silu), AC)
                p.wait("vector", AC, am_g)
                ve_h[par] = p.op("vector", lambda e, i=i, psu=psu, par=par: e.tensor_mul(
                    h_all[:, i * TT:(i + 1) * TT], g_sb[:, par * TT:(par + 1) * TT], psu[:, :]), VE)
        barrier()

        wd_buf = sbuf_in(a_p7, "wd_buf", [128, 2 * 1024], BF16)
        x_res = sbuf_in(a_p7, "x_res", [128, 2 * 512], F32)
        o_stage = sbuf_in(a_p7, "o_stage", [128, 2 * 512], F32)
        o_stage2 = sbuf_in(a_p7, "o_stage2", [128, 2 * 512], F32)
        with nc.psum_tensor("ps_d", [128, 8 * 512], F32) as ps_d:
            ve_fin = 0
            st_out = [0, 0]
            pe_d = [0, 0]
            for th in range(2):
                for oh in range(2):
                    for i in range(IC):
                        par = i % 2
                        if pe_d[par]:
                            p.wait("scalar", PE, pe_d[par])
                        wldd = p.dma("scalar", wd_buf[:, par * 1024:(par + 1) * 1024],
                                     wd_dram.ap()[i * 128:(i + 1) * 128, oh * 1024:(oh + 1) * 1024], LDS)
                        p.wait("tensor", LDS, wldd)
                        if th == 0 and oh == 0 and i == 0:
                            p.wait("tensor", CC, cc_marks["d"])
                        if i == 0 and ve_fin:
                            p.wait("tensor", VE, ve_fin)
                        for d in range(8):
                            pm = p.op("tensor", lambda e, d=d, par=par, i=i, th=th: e.matmul(
                                ps_d[:, d * 512:(d + 1) * 512],
                                wd_buf[:, par * 1024 + d * 128: par * 1024 + (d + 1) * 128],
                                h_all[:, i * TT + th * 512: i * TT + (th + 1) * 512],
                                start=(i == 0), stop=(i == IC - 1)), PE if d == 7 else None)
                        pe_d[par] = pm
                    p.wait("vector", PE, pe_d[(IC - 1) % 2])
                    for d in range(8):
                        dch = oh * 8 + d
                        sp = d % 2
                        ldx2 = p.dma("sync", x_res[:, sp * 512:(sp + 1) * 512],
                                     x_in.ap()[dch * 128:(dch + 1) * 128, 3 + th * 512: 3 + (th + 1) * 512], LD)
                        p.wait("vector", LD, ldx2)
                        if st_out[sp]:
                            p.wait("vector", ST, st_out[sp])

                        def _fin(e, d=d, dch=dch, th=th, sp=sp):
                            e.tensor_add(o_stage[:, sp * 512:(sp + 1) * 512], ps_d[:, d * 512:(d + 1) * 512],
                                         acc_bf[:, dch * TT + th * 512: dch * TT + (th + 1) * 512])
                            return e.tensor_add(o_stage2[:, sp * 512:(sp + 1) * 512],
                                                o_stage[:, sp * 512:(sp + 1) * 512], x_res[:, sp * 512:(sp + 1) * 512])
                        p.op("vector", _fin, VE)
                        ve_fin = p.op("vector", lambda e, sp=sp: e.tensor_copy(
                            flsh_f[:, 0:32], o_stage2[:, sp * 512:(sp + 1) * 512:16]), VE)
                        p.wait("sync", VE, ve_fin)
                        st_out[sp] = p.dma("sync", out_t.ap()[dch * 128:(dch + 1) * 128, th * 512:(th + 1) * 512],
                                           o_stage2[:, sp * 512:(sp + 1) * 512], ST)
            p.wait("sync", ST, ST.n)
        a_p7.close()
        a_out.close()

        snaps.append({e: len(p.ev[e]) for e in Prog.ENGS})
        if PHASE_LIMIT is not None and PHASE_LIMIT <= len(snaps):
            cut = snaps[PHASE_LIMIT - 1]
            for e in Prog.ENGS:
                p.ev[e] = p.ev[e][:cut[e]]
        block = ctx.enter_context(nc.Block())
        for eng in Prog.ENGS:
            evs = p.ev[eng]
            if eng == "tensor":
                def _run_tensor(e, evs=evs):
                    flow = PEFlow(e, PEQ.h)
                    for f in evs:
                        f(flow)
                getattr(block, eng)(_run_tensor)
            else:
                getattr(block, eng)(lambda e, evs=evs: [f(e) for f in evs])

    return nc, dbg


def host_inputs(inputs):
    x = np.asarray(inputs["L0_attn_hidden_states"])
    w_in = np.asarray(inputs["L0_attn_W_in"])
    w_out = np.asarray(inputs["L0_attn_W_out"])
    conv_w = np.asarray(inputs["L0_attn_conv_w"])
    conv_b = np.asarray(inputs["L0_attn_conv_b"])
    dt_bias = np.asarray(inputs["L0_attn_dt_bias"])
    a_log = np.asarray(inputs["L0_attn_A_log"])
    dvec = np.asarray(inputs["L0_attn_D"])
    norm_w = np.asarray(inputs["L0_attn_norm_w"])
    lut = np.asarray(inputs["L0_ffn_lut"])
    gidx = np.asarray(inputs["L0_ffn_gate_idx"])
    uidx = np.asarray(inputs["L0_ffn_up_idx"])
    didx = np.asarray(inputs["L0_ffn_down_idx"])

    w_inT = np.zeros((H, PROJ), np.float32)
    w_inT[:, :8512] = w_in.T
    w_inT = np.ascontiguousarray(w_inT)
    w_outT = np.ascontiguousarray(w_out.T)
    convw_r = np.ascontiguousarray(conv_w.reshape(34, 128, 4).transpose(1, 0, 2).reshape(128, 136))
    convb_r = np.ascontiguousarray(conv_b.reshape(34, 128).T)
    nw_r = np.ascontiguousarray(norm_w.reshape(32, 128).T)
    gidxT = np.ascontiguousarray(gidx.T)
    uidxT = np.ascontiguousarray(uidx.T)
    didxT = np.ascontiguousarray(didx.T)

    in_maps = []
    for core in range(NCORES):
        b, half = core // 2, core % 2
        start = half * TT
        if half == 0:
            halo = np.zeros((3, H), np.float32)
        else:
            halo = x[b, start - 3: start]
        xc = np.ascontiguousarray(np.concatenate([halo, x[b, start:start + TT]], axis=0).T)
        selv = np.zeros((1, 8), np.float32)
        if half == 1:
            selv[0, core - 1] = 1.0
        in_maps.append({
            "x_core": xc,
            "w_inT": w_inT,
            "w_outT": w_outT,
            "convw": convw_r,
            "convb": convb_r,
            "dt_bias": np.ascontiguousarray(dt_bias.reshape(NH, 1)),
            "A_log": np.ascontiguousarray(a_log.reshape(NH, 1)),
            "D": np.ascontiguousarray(dvec.reshape(NH, 1)),
            "norm_w": nw_r,
            "lut": lut,
            "gidxT": np.ascontiguousarray(gidxT[core * 256:(core + 1) * 256]),
            "uidxT": np.ascontiguousarray(uidxT[core * 256:(core + 1) * 256]),
            "didxT": np.ascontiguousarray(didxT[core * 688:(core + 1) * 688]),
            "sel": selv,
        })
    return in_maps


def kernel(**inputs):
    in_maps = host_inputs(inputs)
    nc, dbg = build_program()
    res = run_bass_kernel_spmd(nc, in_maps, core_ids=list(range(NCORES)))
    out = np.empty((4, 2048, H), np.float32)
    for core in range(NCORES):
        b, half = core // 2, core % 2
        out[b, half * TT:(half + 1) * TT, :] = np.asarray(res.results[core]["out_core"]).T
    kernel._last_results = res.results
    return out
